# version 07
# baseline (speedup 1.0000x reference)
import os as _os
import numpy as np
import ml_dtypes
import concourse.bass as bass
import concourse.mybir as mybir
from concourse.bass_utils import run_bass_kernel_spmd

F32 = mybir.dt.float32
BF16 = mybir.dt.bfloat16
I32 = mybir.dt.int32
ALU = mybir.AluOpType
AF = mybir.ActivationFunctionType
AX = mybir.AxisListType

TOK = 2048
D = 1024
NB = 8
NTT = TOK // 128
CS = 64
NCH = TOK // CS
H = 16
HD = 64
SCALE = HD ** -0.5
INNER = 4096
NIC = INNER // 128
NEG = -65000.0

CONSTS = [0.0, 1.0, 0.75, 0.5, 1e-5, 1.0 / 1024.0, -0.5, SCALE,
          -SCALE / 64.0, 127.0, 256.0, -256.0, 1.0 / 128.0, -1.0,
          9.313225746154785e-10, 12582912.0, 6755399441055744.0]


class Sem:
    def __init__(self, es, nc, name):
        self.h = es.enter_context(nc.semaphore(name))
        self.n = 0


class Prog:

    def __init__(self):
        self.steps = []
        self.ser = {}
        self.selfwait = {"vector": True, "gpsimd": True, "scalar": True}

    def set_serial(self, eng, sem):
        self.ser[eng] = sem

    def step(self, eng, fn, waits=(), inc=None, inck=1, is_dma=False):
        ws = []
        for w in waits:
            if isinstance(w, Sem):
                ws.append((w, w.n))
            else:
                ws.append((w[0], w[1]))
        auto = None
        ser = None
        if not is_dma and eng in self.ser:
            s = self.ser[eng]
            if self.selfwait.get(eng):
                auto = (s, s.n)
            s.n += 1
            ser = s
        self.steps.append((eng, fn, ws, inc, inck, auto, ser))
        if inc is not None:
            inc.n += inck

    def dma(self, eng, fn, waits=(), inc=None):
        self.step(eng, fn, waits=waits, inc=inc, inck=16, is_dma=True)

    def emit(self, block):
        def mk(engname):
            def body(eng):
                for (e, fn, ws, inc, inck, auto, ser) in self.steps:
                    if e != engname:
                        continue
                    if auto is not None and auto[1] > 0:
                        eng.wait_ge(auto[0].h, auto[1])
                    for (s, v) in ws:
                        if v > 0:
                            eng.wait_ge(s.h, v)
                    inst = fn(eng) if fn is not None else None
                    if inst is not None:
                        if ser is not None:
                            inst.then_inc(ser.h, 1)
                        if inc is not None:
                            inst.then_inc(inc.h, inck)
            return body

        block.sync(mk("sync"))
        block.vector(mk("vector"))
        block.scalar(mk("scalar"))
        block.tensor(mk("tensor"))
        block.gpsimd(mk("gpsimd"))


def host_consts():
    cst = np.tile(np.array(CONSTS, dtype=np.float32), (128, 1))
    idn = np.eye(128, dtype=np.float32).astype(ml_dtypes.bfloat16)
    lind = np.zeros((128, 8 * 16), dtype=ml_dtypes.bfloat16)
    for j in range(8):
        lind[0:64, 16 * j + 2 * j] = 1.0
        lind[64:128, 16 * j + 2 * j + 1] = 1.0
    cvind = np.zeros((128, 16 * 32), dtype=ml_dtypes.bfloat16)
    for tt in range(16):
        cvind[0:64, 32 * tt + 2 * tt] = 1.0 / 128.0
        cvind[64:128, 32 * tt + 2 * tt + 1] = 1.0 / 128.0
    tri = np.zeros((128, 32), dtype=ml_dtypes.bfloat16)
    tri[0:32, :] = np.tril(np.ones((32, 32)), -1).T.astype(ml_dtypes.bfloat16)
    ones_col = np.ones((128, 1), dtype=ml_dtypes.bfloat16)
    ones_row32 = np.zeros((128, 32), dtype=ml_dtypes.bfloat16)
    ones_row32[0, :] = 1.0
    ones_row128 = np.zeros((128, 128), dtype=ml_dtypes.bfloat16)
    ones_row128[0, :] = 1.0
    mstk = np.zeros((128, 128), dtype=ml_dtypes.bfloat16)
    mstk[0:64, 0:64] = np.eye(64)
    mstk[0:64, 64:128] = np.eye(64)
    m64b = np.zeros((128, 64), dtype=ml_dtypes.bfloat16)
    m64b[0:64, :] = np.where(np.triu(np.ones((64, 64), dtype=bool), 1),
                             NEG, 0.0)
    cbf = np.concatenate([idn, lind, cvind, tri, ones_col, ones_row32,
                          ones_row128, mstk, m64b], axis=1)
    m64 = np.where(np.triu(np.ones((64, 64), dtype=bool), 1), NEG, 0.0)
    blk = np.concatenate([m64, m64], axis=0)
    cmask = np.tile(blk, (1, 16)).astype(np.float32)
    return cst, cbf, cmask


CB_IDN = 0
CB_LIND = 128
CB_CVIND = 256
CB_TRI = 768
CB_ONESC = 800
CB_ONESR32 = 801
CB_ONESR128 = 833
CB_MSTK = 961
CB_M64 = 1089


def build_program(dbg=()):
    nc = bass.Bass("TRN2", target_bir_lowering=False, debug=False)

    x_own = nc.dram_tensor("x_own", [TOK, D], F32, kind="ExternalInput").ap()
    x_prev = nc.dram_tensor("x_prev", [TOK, D], F32, kind="ExternalInput").ap()
    wqkv = nc.dram_tensor("wqkv", [D, 3 * D], F32, kind="ExternalInput").ap()
    wout = nc.dram_tensor("wout", [D, D], F32, kind="ExternalInput").ap()
    wgT = nc.dram_tensor("wgT", [D, INNER], F32, kind="ExternalInput").ap()
    wuT = nc.dram_tensor("wuT", [D, INNER], F32, kind="ExternalInput").ap()
    wdT = nc.dram_tensor("wdT", [INNER, D], F32, kind="ExternalInput").ap()
    cst_d = nc.dram_tensor("cst", [128, len(CONSTS)], F32, kind="ExternalInput").ap()
    cmask_d = nc.dram_tensor("cmask", [128, 1024], F32, kind="ExternalInput").ap()
    cbf_d = nc.dram_tensor("cbf", [128, 1153], BF16, kind="ExternalInput").ap()

    y = nc.dram_tensor("y", [TOK, D], F32, kind="ExternalOutput").ap()

    xhat_d = nc.dram_tensor("xhat_d", [TOK, D], BF16, kind="Internal").ap()
    a_d = nc.dram_tensor("a_d", [D, TOK], BF16, kind="Internal").ap()
    x2q_d = nc.dram_tensor("x2q_d", [TOK, D], BF16, kind="Internal").ap()
    hq_d = nc.dram_tensor("hq_d", [TOK, INNER], BF16, kind="Internal").ap()
    wgq_d = nc.dram_tensor("wgq_d", [D, INNER], BF16, kind="Internal").ap()
    wuq_d = nc.dram_tensor("wuq_d", [D, INNER], BF16, kind="Internal").ap()
    wdq_d = nc.dram_tensor("wdq_d", [INNER, D], BF16, kind="Internal").ap()
    r2_d = nc.dram_tensor("r2_d", [TOK, D], F32, kind="Internal").ap()
    crow_d = nc.dram_tensor("crow", [1, 128], F32, kind="ExternalInput").ap()

    dbg_d = {}
    for name, shape, dt in [
        ("dbg_xhat", [128, 16384], F32), ("dbg_q", [128, 16384], F32),
        ("dbg_k", [128, 16384], F32), ("dbg_v", [128, 16384], F32),
        ("dbg_c0k", [128, 8], F32), ("dbg_c0v", [1, D], F32),
        ("dbg_a", [128, 16384], F32), ("dbg_sig", [16, TOK], F32),
        ("dbg_cvx", [32, D], F32), ("dbg_stats", [128, 96], F32),
        ("dbg_xhd", [TOK, D], BF16),
    ]:
        if name in dbg:
            dbg_d[name] = nc.dram_tensor(name, shape, dt, kind="ExternalOutput").ap()

    import contextlib
    es = contextlib.ExitStack()
    galloc = []

    def _alloc(g, name):
        galloc.append((name, g))
        return g.__enter__()

    def sb(name, shape, dt):
        return _alloc(nc.sbuf_tensor(name, shape, dt), name)

    def ps(name, shape, dt=F32):
        return _alloc(nc.psum_tensor(name, shape, dt), name)

    def free(*names):
        want = set(names)
        for _ in names:
            nm, g = galloc.pop()
            assert nm in want, f"non-LIFO free: {nm} vs {want}"
            g.__exit__(None, None, None)

    def free_all():
        while galloc:
            galloc.pop()[1].__exit__(None, None, None)

    p = Prog()
    sems = {}

    def S(name):
        if name not in sems:
            sems[name] = Sem(es, nc, name)
        return sems[name]

    def barrier(nm, waits):
        bsem = S(nm)
        p.step("sync", lambda e: e.nop(), waits=waits, inc=bsem, is_dma=True,
               inck=1)
        return bsem

    vser = S("vser")
    aser = S("aser")
    pser = S("pser")
    p.set_serial("vector", vser)
    p.set_serial("scalar", aser)
    p.set_serial("gpsimd", pser)

    cst = sb("cst_sb", [128, len(CONSTS)], F32)
    for i, v in enumerate(CONSTS):
        nc.const_aps.register(F32, v, cst[:, i:i + 1])
    cbf = sb("cbf_sb", [128, 1153], BF16)

    sxcol = sb("sxcol", [128, 8], BF16)
    sxcol05 = sb("sxcol05", [128, 8], BF16)
    c0k = sb("c0k", [128, 8], F32)
    c0vrow = sb("c0vrow", [1, D], BF16)
    bcast_b = sb("bcast_b", [128, 1], F32)
    det_i = sb("det_i", [128, 1], I32)

    sumx = sb("sumx", [128, NTT], F32)
    sumsq = sb("sumsq", [128, NTT], F32)
    mcol = sb("mcol", [128, NTT], F32)
    rstd = sb("rstd", [128, NTT], F32)
    bcol = sb("bcol", [128, NTT], F32)
    sumxp = sb("sumxp", [128, NTT], F32)
    sumsqp = sb("sumsqp", [128, NTT], F32)
    mcolp = sb("mcolp", [128, NTT], F32)
    rstdp = sb("rstdp", [128, NTT], F32)
    rstdpb = sb("rstdpb", [128, NTT], BF16)
    rmb = sb("rmb", [128, NTT], BF16)
    msqt_o = sb("msqt_o", [128, NTT], F32)
    lnt_o = sb("lnt_o", [128, NTT], F32)
    msqt_p = sb("msqt_p", [128, NTT], F32)
    lnt_p = sb("lnt_p", [128, NTT], F32)
    scsb = sb("scsb", [1, 1], F32)
    sxrow = sb("sxrow", [1, D], BF16)
    crow = sb("crow_sb", [1, 128], F32)
    shiftb = sb("shiftb", [128, 1], F32)
    mgc = sb("mgc", [128, 1], F32)
    wmcol = sb("wmcol", [128, 3], F32)
    thrcol = sb("thrcol", [128, 3], F32)
    nthrcol = sb("nthrcol", [128, 3], F32)
    wsc = sb("wsc", [1, 8], F32)
    m2c = sb("m2c", [128, NTT], F32)
    sq2c = sb("sq2c", [128, NTT], F32)
    msq2c = sb("msq2c", [128, NTT], F32)
    ln2c = sb("ln2c", [128, NTT], F32)
    rstd2c = sb("rstd2c", [128, NTT], F32)
    bc2c = sb("bc2c", [128, NTT], F32)
    s1c = sb("s1c", [128, NTT], F32)
    rs1c = sb("rs1c", [128, NTT], F32)
    s2c = sb("s2c", [128, NTT], F32)
    gscc = sb("gscc", [128, NTT], F32)
    uscc = sb("uscc", [128, NTT], F32)
    dscc = sb("dscc", [128, NTT], F32)
    amc = sb("amc", [128, NTT], F32)

    Q = sb("Q", [128, NB * TOK], BF16)
    K = sb("K", [128, NB * TOK], BF16)
    V = sb("V", [128, NTT * D], BF16)
    xhat = sb("xhat", [128, NB * TOK], BF16)
    wqk_sb = sb("wqk_sb", [128, NB * 2048], BF16)
    wv_sb = sb("wv_sb", [128, NB * D], BF16)
    xin = sb("xin", [128, D], F32)
    xpin = sb("xpin", [128, D], F32)
    xpbf = sb("xpbf", [128, D], BF16)
    xhst = sb("xhst", [128, 2 * D], BF16)
    sqs = sb("sqs", [128, D], F32)

    dxo = S("dxo")
    dxp = S("dxp")
    dcst = S("dcst")
    dxh = S("dxh")
    dtrx = S("dtrx")
    dwq = S("dwq")
    t_sx = S("t_sx")
    t_c0 = S("t_c0")
    t_v = S("t_v")
    t_qk = S("t_qk")
    dbg_done = S("dbg_done")

    p.dma("sync", lambda e: e.dma_start(out=cst[:], in_=cst_d), inc=dcst)
    p.dma("sync", lambda e: e.dma_start(out=cbf[:], in_=cbf_d), inc=dcst)
    p.dma("sync", lambda e: e.dma_start(out=crow[:], in_=crow_d), inc=dcst)
    p.step("vector", lambda e: e.tensor_single_scalar(
        out=det_i[:], in_=cst[:, 2:3], scalar=0.0, op=ALU.add),
        waits=[dcst])
    p.step("vector", lambda e: e.tensor_scalar(
        out=bcast_b[:], in0=det_i[:], scalar1=-0.5, scalar2=0.5,
        op0=ALU.mult, op1=ALU.add))
    p.step("vector", lambda e: e.tensor_single_scalar(
        out=shiftb[:], in_=bcast_b[:], scalar=256.0, op=ALU.add))
    p.step("vector", lambda e: e.tensor_single_scalar(
        out=mgc[:], in_=cst[:, 1:2], scalar=9.313225746154785e-10,
        op=ALU.add), waits=[dcst])
    p.step("vector", lambda e: e.tensor_scalar(
        out=mgc[:], in0=mgc[:], scalar1=1.0, scalar2=1073741824.0,
        op0=ALU.subtract, op1=ALU.mult))
    p.step("vector", lambda e: e.tensor_scalar(
        out=mgc[:], in0=mgc[:],
        scalar1=6755399441055744.0 - 12582912.0, scalar2=12582912.0,
        op0=ALU.mult, op1=ALU.add))

    for b in range(NB):
        p.dma("gpsimd", lambda e, b=b: e.dma_start(
            out=wqk_sb[:, b * 2048:(b + 1) * 2048],
            in_=wqkv[b * 128:(b + 1) * 128, 0:2048]), inc=dwq)
        p.dma("gpsimd", lambda e, b=b: e.dma_start(
            out=wv_sb[:, b * D:(b + 1) * D],
            in_=wqkv[b * 128:(b + 1) * 128, 2048:3072]), inc=dwq)

    psx = ps("psx", [1, D])
    psc = ps("psc", [1, 1])
    dve_done_own = [0] * NTT
    dve_done_prev = [0] * NTT
    act_done_own = [0] * NTT
    act_done_prev = [0] * NTT
    pe_done_prev = [0] * NTT

    def ln_tile(i, src_d, buf, dsem, sumx_, sumsq_, m_, rstd_, msqt, lnt,
                dma_gate):
        bi = 0
        p.dma("sync", lambda e: e.dma_start(
            out=buf[:, bi:bi + D], in_=src_d[i * 128:(i + 1) * 128, :]),
            waits=dma_gate(i), inc=dsem)
        p.step("scalar", lambda e: e.activation(
            out=sqs[:], in_=buf[:, bi:bi + D], func=AF.Square,
            accum_out=sumsq_[:, i:i + 1]),
            waits=[(dsem, 16 * (i + 1)), dcst])
        p.step("vector", lambda e: e.reduce_sum(
            out=sumx_[:, i:i + 1], in_=buf[:, bi:bi + D], axis=AX.X),
            waits=[(dsem, 16 * (i + 1))])
        p.step("vector", lambda e: e.tensor_single_scalar(
            out=m_[:, i:i + 1], in_=sumx_[:, i:i + 1],
            scalar=1.0 / 1024.0, op=ALU.mult))
        p.step("vector", lambda e: e.tensor_tensor(
            out=msqt[:, i:i + 1], in0=m_[:, i:i + 1], in1=m_[:, i:i + 1],
            op=ALU.mult))
        p.step("vector", lambda e: e.scalar_tensor_tensor(
            out=sumsq_[:, i:i + 1], in0=sumsq_[:, i:i + 1],
            scalar=1.0 / 1024.0, in1=msqt[:, i:i + 1],
            op0=ALU.mult, op1=ALU.subtract), waits=[aser])
        p.step("scalar", lambda e: e.activation(
            out=lnt[:, i:i + 1], in_=sumsq_[:, i:i + 1], func=AF.Ln,
            bias=1e-5), waits=[vser])
        p.step("scalar", lambda e: e.activation(
            out=rstd_[:, i:i + 1], in_=lnt[:, i:i + 1], func=AF.Exp,
            scale=-0.5))

    for i in range(NTT):
        ln_tile(i, x_own, xin, dxo, sumx, sumsq, mcol, rstd, msqt_o, lnt_o,
                lambda j: [] if j < 1 else [(vser, dve_done_own[j - 1]),
                                            (aser, act_done_own[j - 1])])
        p.step("vector", lambda e, i=i: e.scalar_tensor_tensor(
            out=bcol[:, i:i + 1], in0=mcol[:, i:i + 1], scalar=-1.0,
            in1=rstd[:, i:i + 1], op0=ALU.mult, op1=ALU.mult),
            waits=[aser])
        dve_done_own[i] = vser.n
        p.step("scalar", lambda e, i=i, bi=(i % 2) * D: e.activation(
            out=xhst[:, bi:bi + D], in_=xin[:, 0:D], func=AF.Identity,
            scale=rstd[:, i:i + 1], bias=bcol[:, i:i + 1]),
            waits=[vser] + ([] if i < 2 else [(dxh, 16 * i)]))
        act_done_own[i] = aser.n
        p.dma("gpsimd", lambda e, i=i, bi=(i % 2) * D: e.dma_start(
            out=xhat_d[i * 128:(i + 1) * 128, :], in_=xhst[:, bi:bi + D]),
            waits=[(aser, aser.n)], inc=dxh)

        ln_tile(i, x_prev, xpin, dxp, sumxp, sumsqp, mcolp, rstdp, msqt_p,
                lnt_p,
                lambda j: [] if j < 1 else [(vser, dve_done_prev[j - 1]),
                                            (aser, act_done_prev[j - 1])])
        p.step("vector", lambda e, i=i: e.tensor_single_scalar(
            out=rstdpb[:, i:i + 1], in_=rstdp[:, i:i + 1], scalar=0.0,
            op=ALU.add), waits=[aser])
        p.step("vector", lambda e, i=i: e.scalar_tensor_tensor(
            out=rmb[:, i:i + 1], in0=mcolp[:, i:i + 1], scalar=1.0,
            in1=rstdp[:, i:i + 1], op0=ALU.mult, op1=ALU.mult))
        dve_done_prev[i] = vser.n
        p.step("scalar", lambda e, i=i: e.activation(
            out=xpbf[:, 0:D], in_=xpin[:, 0:D], func=AF.Identity),
            waits=([] if i < 1 else [(t_sx, pe_done_prev[i - 1])]))
        act_done_prev[i] = aser.n
        for hf in range(2):
            p.step("tensor", lambda e, i=i, hf=hf: e.matmul(
                out=psx[:, hf * 512:(hf + 1) * 512],
                lhsT=rstdpb[:, i:i + 1],
                rhs=xpbf[:, hf * 512:(hf + 1) * 512],
                start=(i == 0), stop=(i == NTT - 1)),
                waits=[vser, aser])
        p.step("tensor", lambda e, i=i: e.matmul(
            out=psc[:], lhsT=rmb[:, i:i + 1],
            rhs=cbf[:, CB_ONESC:CB_ONESC + 1],
            start=(i == 0), stop=(i == NTT - 1)),
            waits=[(dcst, 48)], inc=t_sx)
        pe_done_prev[i] = t_sx.n

    for b in range(NB):
        p.dma("sync", lambda e, b=b: e.dma_start(
            out=xhat[:, b * TOK:(b + 1) * TOK],
            in_=xhat_d[:, b * 128:(b + 1) * 128], transpose=True),
            waits=[(dxh, 16 * NTT)], inc=dtrx)

    p.step("vector", lambda e: e.tensor_copy(out=scsb[:], in_=psc[:]),
           waits=[(t_sx, NTT)])
    p.step("vector", lambda e: e.tensor_scalar(
        out=sxrow[:], in0=psx[:], scalar1=scsb[0:1, 0:1], scalar2=None,
        op0=ALU.subtract))
    v_after_sxrow = vser.n
    ptc = ps("ptc", [128, 16], BF16)
    for b in range(NB):
        p.step("tensor", lambda e, b=b: e.transpose(
            out=ptc[:, 2 * b:2 * b + 1],
            in_=sxrow[0:1, b * 128:(b + 1) * 128],
            identity=cbf[0:1, CB_ONESC:CB_ONESC + 1]),
            waits=[(vser, v_after_sxrow), (dcst, 48)],
            inc=(t_sx if b == NB - 1 else None))
    p.step("vector", lambda e: e.tensor_copy(out=sxcol[:], in_=ptc[:, 0:16:2]),
           waits=[(t_sx, NTT + 1)])
    p.step("vector", lambda e: e.tensor_single_scalar(
        out=sxcol05[:], in_=ptc[:, 0:16:2], scalar=1.0 / 128.0, op=ALU.mult))
    v_after_sxcol = vser.n

    pc0k = ps("pc0k", [128, 8])
    for kc in range(8):
        for b in range(NB):
            p.step("tensor", lambda e, b=b, kc=kc: e.matmul(
                out=pc0k[:, kc:kc + 1],
                lhsT=wqk_sb[:, b * 2048 + 1024 + kc * 128:
                            b * 2048 + 1024 + (kc + 1) * 128],
                rhs=sxcol[:, b:b + 1],
                start=(b == 0), stop=(b == NB - 1)),
                waits=[(vser, v_after_sxcol), (dwq, 16 * 2 * NB)],
                inc=(t_c0 if (b == NB - 1 and kc == 7) else None))
    pc0v = ps("pc0v", [1, D])
    for b in range(NB):
        for hf in range(2):
            p.step("tensor", lambda e, b=b, hf=hf: e.matmul(
                out=pc0v[:, hf * 512:(hf + 1) * 512],
                lhsT=sxcol05[:, b:b + 1],
                rhs=wv_sb[:, b * D + hf * 512: b * D + (hf + 1) * 512],
                start=(b == 0), stop=(b == NB - 1)),
                inc=(t_c0 if (b == NB - 1 and hf == 1) else None))
    p.step("vector", lambda e: e.tensor_copy(out=c0k[:], in_=pc0k[:]),
           waits=[(t_c0, 2)])
    p.step("vector", lambda e: e.tensor_copy(out=c0vrow[:], in_=pc0v[:]))
    v_carry_done = vser.n

    free("psx", "psc", "ptc", "pc0k", "pc0v")
    free("xin", "xpin", "xpbf", "xhst", "sqs")
    pvv = [ps("pvv0", [128, D]), ps("pvv1", [128, D])]
    ev_done = [0] * NTT
    for tt in range(NTT):
        pb = tt % 2
        for b in range(NB):
            for hf in range(2):
                p.step("tensor", lambda e, tt=tt, b=b, hf=hf, pb=pb: e.matmul(
                    out=pvv[pb][:, hf * 512:(hf + 1) * 512],
                    lhsT=xhat[:, b * TOK + tt * 128: b * TOK + (tt + 1) * 128],
                    rhs=wv_sb[:, b * D + hf * 512: b * D + (hf + 1) * 512],
                    start=(b == 0), stop=(b == NB - 1)),
                    waits=(([(dtrx, 16 * NB), (vser, v_carry_done)]
                            if b == 0 and hf == 0 else [])
                           + ([(vser, ev_done[tt - 2])]
                              if (tt >= 2 and b == 0 and hf == 0) else [])),
                    inc=(t_v if (b == NB - 1 and hf == 1) else None))
        p.step("vector", lambda e, tt=tt, pb=pb: e.tensor_copy(
            out=V[:, tt * D:(tt + 1) * D], in_=pvv[pb][:]),
            waits=[(t_v, tt + 1)])
        ev_done[tt] = vser.n

    eqk_done = [0] * 64
    pqk = [ps("pqk0", [128, 512]), ps("pqk1", [128, 512])]
    n_qk = 0
    for tg in range(4):
        for oc in range(16):
            pb = n_qk % 2
            for b in range(NB):
                p.step("tensor", lambda e, oc=oc, b=b, tg=tg, pb=pb: e.matmul(
                    out=pqk[pb][:],
                    lhsT=wqk_sb[:, b * 2048 + oc * 128:
                                b * 2048 + (oc + 1) * 128],
                    rhs=xhat[:, b * TOK + tg * 512: b * TOK + (tg + 1) * 512],
                    start=(b == 0), stop=(b == NB - 1)),
                    waits=([(vser if pb == 0 else aser,
                             eqk_done[n_qk - 2])] if (n_qk >= 2 and b == 0)
                           else []),
                    inc=(t_qk if b == NB - 1 else None))
            dst = Q if oc < 8 else K
            dj = oc % 8
            if pb == 0:
                p.step("vector", lambda e, dst=dst, dj=dj, tg=tg, pb=pb:
                       e.tensor_copy(
                           out=dst[:, dj * TOK + tg * 512:
                                   dj * TOK + (tg + 1) * 512],
                           in_=pqk[pb][:]), waits=[(t_qk, n_qk + 1)])
                eqk_done[n_qk] = vser.n
            else:
                p.step("scalar", lambda e, dst=dst, dj=dj, tg=tg, pb=pb:
                       e.activation(
                           out=dst[:, dj * TOK + tg * 512:
                                   dj * TOK + (tg + 1) * 512],
                           in_=pqk[pb][:], func=AF.Identity),
                       waits=[(t_qk, n_qk + 1)])
                eqk_done[n_qk] = aser.n
            n_qk += 1
    v_p1_done = vser.n
    a_p1_done = aser.n


    free("pqk0", "pqk1", "pvv0", "pvv1")
    if "dbg_xhat" in dbg_d:
        free("wv_sb")
    else:
        free("wv_sb", "wqk_sb", "xhat")

    t_cv = S("t_cv")
    t_log = S("t_log")
    t_att = S("t_att")
    dsg = S("dsg")
    dcv = S("dcv")
    p2rdy = barrier("p2rdy", [(t_qk, 64), (dtrx, 16 * NB), (dwq, 16 * 2 * NB),
                              (dxh, 16 * NTT), (dxo, 16 * NTT),
                              (dxp, 16 * NTT), (t_v, NTT), (t_sx, NTT + 1),
                              (t_c0, 2), (vser, v_p1_done),
                              (aser, a_p1_done)])

    CK = sb("CK", [128, NCH * 8], F32)
    CKcum = sb("CKcum", [128, NCH * 8], F32)
    CVsb = sb("CVsb", [32, D], BF16)
    CVXsb = sb("CVXsb", [32, D], BF16)
    sgm = sb("sgm", [16, TOK], F32)
    sgfl = sb("sgfl", [1, 2 * 1024], BF16)
    cvfl = sb("cvfl", [1, 2 * 1024], BF16)
    Sm = sb("Sm", [128, 1024], F32)
    Pn = sb("Pn", [128, 1024], BF16)
    PTsb = sb("PTsb", [128, 1024], BF16)
    mx = sb("mx", [128, 16], F32)
    rs = sb("rs", [128, 16], F32)
    rsi = sb("rsi", [128, 16], F32)
    ast = sb("ast", [128, 2 * 512], BF16)
    da = S("da")
    cmask = sb("cmask_sb", [128, 1024], F32)
    p.dma("sync", lambda e: e.dma_start(out=cmask[:], in_=cmask_d),
          waits=[(p2rdy, 1)], inc=dcst)
    CKB = sb("CKB", [128, TOK], BF16)
    Mb = sb("Mb", [128, TOK], BF16)

    for b in range(NB):
        p.step("vector", lambda e, b=b: e.reduce_sum(
            out=CK[:, b:NCH * 8:8],
            in_=K[:, b * TOK:(b + 1) * TOK].rearrange(
                "p (c x) -> p c x", c=NCH), axis=AX.X),
            waits=([(vser, v_p1_done), (aser, a_p1_done), (p2rdy, 1)]
                   if b == 0 else []))
    p.step("vector", lambda e: e.tensor_copy(out=CKcum[:, 0:8], in_=c0k[:]))
    for c in range(1, NCH):
        p.step("vector", lambda e, c=c: e.tensor_tensor(
            out=CKcum[:, c * 8:(c + 1) * 8],
            in0=CKcum[:, (c - 1) * 8:c * 8],
            in1=CK[:, (c - 1) * 8:c * 8], op=ALU.add))
    v_ckcum = vser.n

    pcv = ps("pcv", [32, D])
    for tt in range(NTT):
        for hf in range(2):
            p.step("tensor", lambda e, tt=tt, hf=hf: e.matmul(
                out=pcv[:, hf * 512:(hf + 1) * 512],
                lhsT=cbf[:, CB_CVIND + tt * 32:CB_CVIND + (tt + 1) * 32],
                rhs=V[:, tt * D + hf * 512:tt * D + (hf + 1) * 512],
                start=(tt == 0), stop=(tt == NTT - 1)),
                waits=([(t_v, NTT), (p2rdy, 1), (vser, v_p1_done),
                        (aser, a_p1_done)] if tt == 0 and hf == 0 else []),
                inc=(t_cv if (tt == NTT - 1 and hf == 1) else None))
    p.step("vector", lambda e: e.tensor_copy(out=CVsb[:], in_=pcv[:]),
           waits=[(t_cv, 1)])
    v_cvsb = vser.n
    pcvx = ps("pcvx", [32, D])
    for hf in range(2):
        p.step("tensor", lambda e, hf=hf: e.matmul(
            out=pcvx[:, hf * 512:(hf + 1) * 512],
            lhsT=cbf[0:32, CB_TRI:CB_TRI + 32],
            rhs=CVsb[:, hf * 512:(hf + 1) * 512],
            start=True, stop=False),
            waits=[(vser, v_cvsb)])
        p.step("tensor", lambda e, hf=hf: e.matmul(
            out=pcvx[:, hf * 512:(hf + 1) * 512],
            lhsT=cbf[0:1, CB_ONESR32:CB_ONESR32 + 32],
            rhs=c0vrow[0:1, hf * 512:(hf + 1) * 512],
            start=False, stop=True),
            waits=[(vser, v_carry_done)],
            inc=(t_cv if hf == 1 else None))
    p.step("vector", lambda e: e.tensor_copy(out=CVXsb[:], in_=pcvx[:]),
           waits=[(t_cv, 2)])
    v_cvx = vser.n
    free("pcv", "pcvx")

    plog = [ps("plog%d" % g, [16, 512]) for g in range(4)]
    mdone = [0] * NB
    for b in range(NB):
        p.step("vector", lambda e, b=b: e.tensor_copy(
            out=CKB[:],
            in_=CKcum[:, b:NCH * 8:8].unsqueeze(2)
                .broadcast_to([128, NCH, CS])),
            waits=([(t_log, t_log.n)] if b > 0 else []))
        p.step("vector", lambda e, b=b: e.tensor_tensor(
            out=Mb[:], in0=Q[:, b * TOK:(b + 1) * TOK], in1=CKB[:],
            op=ALU.mult))
        mdone[b] = vser.n
        for g in range(4):
            p.step("tensor", lambda e, b=b, g=g: e.matmul(
                out=plog[g][:],
                lhsT=cbf[:, CB_LIND + b * 16:CB_LIND + (b + 1) * 16],
                rhs=Mb[:, g * 512:(g + 1) * 512],
                start=(b == 0), stop=(b == NB - 1)),
                waits=[(vser, mdone[b])],
                inc=(t_log if g == 3 else None))
    for g in range(4):
        p.step("scalar", lambda e, g=g: e.activation(
            out=sgm[:, g * 512:(g + 1) * 512], in_=plog[g][:], func=AF.Exp,
            scale=-SCALE / 64.0), waits=[(t_log, NB), (p2rdy, 1)])
    p.step("vector", lambda e: e.tensor_single_scalar(
        out=sgm[:], in_=sgm[:], scalar=1.0, op=ALU.add), waits=[aser])
    p.step("vector", lambda e: e.reciprocal(out=sgm[:], in_=sgm[:]))
    v_sig = vser.n
    free("plog0", "plog1", "plog2", "plog3")
    free("Mb", "CKB")

    dwl = S("dwl")
    dqw = S("dqw")
    t_wq = S("t_wq")
    wstg = sb("wstg", [128, INNER], F32)
    abuf = sb("abuf", [128, INNER], BF16)
    wabs = sb("wabs", [128, 27], F32)
    pwq = ps("pwq", [128, 8])
    pw1 = pwq[:, 0:4]
    pwt = pwq[0:1, 4:8]
    wsrc = [(wgT, wgq_d, 0), (wuT, wuq_d, 1), (wdT, wdq_d, 2)]

    def wtile_aps(mat, t):
        srcT, dstT, mi = wsrc[mat]
        if mi < 2:
            return (srcT[t * 128:(t + 1) * 128, :],
                    dstT[t * 128:(t + 1) * 128, :])
        return (srcT[t * 512:(t + 1) * 512, :].rearrange(
                    "(a p) f -> p a f", p=128),
                dstT[t * 512:(t + 1) * 512, :].rearrange(
                    "(a p) f -> p a f", p=128))

    def wq_p1_tile(mat, t, eng="gpsimd"):
        p.dma("gpsimd", lambda e, mat=mat, t=t: e.dma_start(
            out=(wstg[:].rearrange("p (a f) -> p a f", a=4) if mat == 2
                 else wstg[:]),
            in_=wtile_aps(mat, t)[0]),
            waits=[pser, vser, aser], inc=dwl)
        p.step("scalar", lambda e, mat=mat, t=t: e.activation(
            out=wstg[:], in_=wstg[:], func=AF.Abs,
            accum_out=wabs[:, mat * 9 + t:mat * 9 + t + 1]), waits=[dwl])

    def wq_thr(mat):
        p.step("vector", lambda e, mat=mat: e.reduce_sum(
            out=wabs[:, mat * 9 + 8:mat * 9 + 9],
            in_=wabs[:, mat * 9:mat * 9 + 8], axis=AX.X),
            waits=[aser, pser])
        vv = vser.n
        p.step("tensor", lambda e, mat=mat: e.matmul(
            out=pwt[0:1, mat:mat + 1], lhsT=wabs[:, mat * 9 + 8:mat * 9 + 9],
            rhs=cst[:, 1:2],
            start=True, stop=True), waits=[(vser, vv)], inc=t_wq)
        p.step("vector", lambda e, mat=mat: e.tensor_scalar(
            out=wsc[0:1, mat:mat + 1], in0=pwt[0:1, mat:mat + 1],
            scalar1=1.0 / (D * INNER), scalar2=1e-5,
            op0=ALU.mult, op1=ALU.max), waits=[(t_wq, t_wq.n)])
        p.step("vector", lambda e, mat=mat: e.tensor_single_scalar(
            out=wsc[0:1, 3 + mat:4 + mat], in_=wsc[0:1, mat:mat + 1],
            scalar=0.5, op=ALU.mult))
        p.step("vector", lambda e, mat=mat: e.tensor_single_scalar(
            out=wsc[0:1, 6:7], in_=wsc[0:1, mat:mat + 1], scalar=-1.0,
            op=ALU.mult))
        vv2 = vser.n
        p.step("tensor", lambda e, mat=mat: e.matmul(
            out=pw1[:, mat:mat + 1], lhsT=crow[:], rhs=wsc[0:1, 6:7],
            start=True, stop=True), waits=[(vser, vv2)], inc=t_wq)
        p.step("vector", lambda e, mat=mat: e.tensor_copy(
            out=wmcol[:, mat:mat + 1], in_=pw1[:, mat:mat + 1]),
            waits=[(t_wq, t_wq.n)])
        p.step("tensor", lambda e, mat=mat: e.matmul(
            out=pw1[:, 3:4], lhsT=crow[:], rhs=wsc[0:1, 3 + mat:4 + mat],
            start=True, stop=True), waits=[vser], inc=t_wq)
        p.step("vector", lambda e, mat=mat: e.tensor_copy(
            out=thrcol[:, mat:mat + 1], in_=pw1[:, 3:4]),
            waits=[(t_wq, t_wq.n)])
        p.step("vector", lambda e, mat=mat: e.tensor_single_scalar(
            out=nthrcol[:, mat:mat + 1], in_=thrcol[:, mat:mat + 1],
            scalar=-1.0, op=ALU.mult))

    def wq_p2_tile(mat, t, eng="gpsimd"):
        p.dma("gpsimd", lambda e, mat=mat, t=t: e.dma_start(
            out=(wstg[:].rearrange("p (a f) -> p a f", a=4) if mat == 2
                 else wstg[:]),
            in_=wtile_aps(mat, t)[0]),
            waits=[pser, vser, aser], inc=dwl)
        p.step(eng, lambda e, mat=mat: e.tensor_scalar(
            out=abuf[:], in0=wstg[:], scalar1=thrcol[:, mat:mat + 1],
            scalar2=None, op0=ALU.is_gt), waits=[dwl, dqw])
        p.step(eng, lambda e, mat=mat: e.tensor_scalar(
            out=wstg[:], in0=wstg[:], scalar1=nthrcol[:, mat:mat + 1],
            scalar2=None, op0=ALU.is_lt))
        p.step(eng, lambda e: e.tensor_tensor(
            out=abuf[:], in0=wstg[:], in1=abuf[:], op=ALU.subtract))
        sem = pser if eng == "gpsimd" else vser
        vv = sem.n
        p.dma("gpsimd", lambda e, mat=mat, t=t, sem=sem, vv=vv: e.dma_start(
            out=wtile_aps(mat, t)[1],
            in_=(abuf[:].rearrange("p (a f) -> p a f", a=4) if mat == 2
                 else abuf[:])), waits=[(sem, vv)], inc=dqw)

    WQE = _os.environ.get("WQ_ENG", "gpsimd")
    wq_sched = {}
    for t in range(4):
        wq_sched.setdefault(t, []).extend(
            [lambda m=0, k=2 * t: wq_p1_tile(m, k),
             lambda m=0, k=2 * t + 1: wq_p1_tile(m, k)])
    wq_sched.setdefault(4, []).append(lambda: wq_thr(0))
    for t in range(4):
        wq_sched.setdefault(4 + t, []).extend(
            [lambda m=1, k=2 * t: wq_p1_tile(m, k),
             lambda m=1, k=2 * t + 1: wq_p1_tile(m, k)])
    wq_sched.setdefault(8, []).append(lambda: wq_thr(1))
    for t in range(5, 13):
        wq_sched.setdefault(t, []).extend(
            [lambda m=0, k=2 * (t - 5): wq_p2_tile(m, k, WQE),
             lambda m=0, k=2 * (t - 5) + 1: wq_p2_tile(m, k, WQE)]
            if t < 9 else
            [lambda m=1, k=2 * (t - 9): wq_p2_tile(m, k, WQE),
             lambda m=1, k=2 * (t - 9) + 1: wq_p2_tile(m, k, WQE)])
    for t in range(4):
        wq_sched.setdefault(12 + t, []).extend(
            [lambda m=2, k=2 * t: wq_p1_tile(m, k, "vector"),
             lambda m=2, k=2 * t + 1: wq_p1_tile(m, k, "vector")])
    wq_tail = ([lambda: wq_thr(2)]
               + [lambda m=2, k=k: wq_p2_tile(m, k, "vector")
                  for k in range(8)])

    rank1_done = [0] * NCH
    sgcv_issued = [None] * NCH

    def issue_flat_dma(c, gate_batch):
        w = [(vser, max(v_sig, v_cvx)), (p2rdy, 1)]
        if c >= 2:
            w.append((t_att, rank1_done[c - 2]))
        bufc = c % 2
        p.dma("gpsimd", lambda e, c=c, bufc=bufc: e.dma_start(
            out=sgfl[0:1, bufc * 1024:(bufc + 1) * 1024].rearrange(
                "o (h x) -> o h x", h=16),
            in_=sgm[:, c * CS:(c + 1) * CS]), waits=w, inc=dsg)
        p.dma("gpsimd", lambda e, c=c, bufc=bufc: e.dma_start(
            out=cvfl[0:1, bufc * 1024:(bufc + 1) * 1024],
            in_=CVXsb[c:c + 1, :]), waits=list(w), inc=dcv)
        sgcv_issued[c] = dsg.n

    for c in (0, 1):
        issue_flat_dma(c, None)

    _ps2 = _os.environ.get("S_SINGLE", "0") != "1"
    psS = ([ps("psSa", [128, 1024]), ps("psSb", [128, 1024])] if _ps2
           else [ps("psSa", [128, 1024])] * 2)
    psPT = ps("psPT", [128, 1024], BF16)
    psA = [ps("psA0", [128, 512]), ps("psA1", [128, 512])]
    s_free_at = [vser.n, vser.n]
    a_chunk0 = aser.n
    pt_free_at = 0
    a_free_at = [0, 0]

    for t in range(NCH // 2):
        c0 = 2 * t
        sp = (t % 2) if _ps2 else 0
        nS = 0
        for c2 in (0, 1):
            c = c0 + c2
            for j in range(NB):
                for hp in (0, 1):
                    nS += 1
                    p.step("tensor", lambda e, c=c, c2=c2, j=j, hp=hp, sp=sp:
                           e.matmul(
                        out=psS[sp][hp * 64:(hp + 1) * 64,
                                    c2 * 512 + j * 64:c2 * 512 + (j + 1) * 64],
                        lhsT=Q[hp * 64:(hp + 1) * 64,
                               j * TOK + c * CS:j * TOK + (c + 1) * CS],
                        rhs=K[hp * 64:(hp + 1) * 64,
                              j * TOK + c * CS:j * TOK + (c + 1) * CS],
                        start=True, stop=True,
                        tile_position=(hp * 64, hp * 64)),
                        waits=(([(vser, s_free_at[sp])]
                                + ([(aser, a_chunk0)] if t == 0 else []))
                               if nS == 1 else []),
                        inc=(t_att if nS == 32 else None))
        tS = t_att.n
        p.step("vector", lambda e, sp=sp: e.tensor_tensor(
            out=Sm[:], in0=psS[sp][:], in1=cmask[:], op=ALU.add),
            waits=[(t_att, tS), dcst])
        s_free_at[sp] = vser.n
        p.step("vector", lambda e: e.reduce_max(
            out=mx[:], in_=Sm[:].rearrange("p (g x) -> p g x", g=16),
            axis=AX.X))
        p.step("vector", lambda e: e.tensor_tensor(
            out=Sm[:].rearrange("p (g x) -> p g x", g=16),
            in0=Sm[:].rearrange("p (g x) -> p g x", g=16),
            in1=mx[:].unsqueeze(2).broadcast_to([128, 16, CS]),
            op=ALU.subtract))
        v_sub = vser.n
        p.step("scalar", lambda e: e.activation(
            out=Sm[:], in_=Sm[:], func=AF.Exp, scale=SCALE),
            waits=[(vser, v_sub)])
        p.step("vector", lambda e: e.reduce_sum(
            out=rs[:], in_=Sm[:].rearrange("p (g x) -> p g x", g=16),
            axis=AX.X), waits=[aser])
        p.step("vector", lambda e: e.reciprocal(out=rsi[:], in_=rs[:]))
        p.step("vector", lambda e: e.tensor_tensor(
            out=Pn[:].rearrange("p (g x) -> p g x", g=16),
            in0=Sm[:].rearrange("p (g x) -> p g x", g=16),
            in1=rsi[:].unsqueeze(2).broadcast_to([128, 16, CS]),
            op=ALU.mult))
        v_pn = vser.n
        nT = 0
        for c2 in (0, 1):
            for j in range(NB):
                nT += 1
                p.step("tensor", lambda e, c2=c2, j=j: e.transpose(
                    out=psPT[c2 * 64:(c2 + 1) * 64, j * 128:(j + 1) * 128],
                    in_=Pn[:, c2 * 512 + j * 64:c2 * 512 + (j + 1) * 64],
                    identity=cbf[:, CB_IDN:CB_IDN + 128],
                    tile_position=(0, c2 * 64)),
                    waits=([(vser, max(v_pn, pt_free_at))] if nT == 1 else []),
                    inc=(t_att if nT == 16 else None))
        tT = t_att.n
        p.step("vector", lambda e: e.tensor_copy(out=PTsb[:], in_=psPT[:]),
               waits=[(t_att, tT)])
        pt_free_at = vser.n
        for c2 in (0, 1):
            c = c0 + c2
            first = True
            for j in range(NB):
                for he in (0, 1):
                    p.step("tensor", lambda e, c=c, c2=c2, j=j, he=he: e.matmul(
                        out=psA[c2][he * 64:(he + 1) * 64,
                                    j * 64:(j + 1) * 64],
                        lhsT=V[c2 * 64:(c2 + 1) * 64,
                               (c // 2) * D + j * 128 + he * 64:
                               (c // 2) * D + j * 128 + (he + 1) * 64],
                        rhs=PTsb[c2 * 64:(c2 + 1) * 64,
                                 j * 128 + he * 64:j * 128 + (he + 1) * 64],
                        start=True, stop=False,
                        tile_position=(c2 * 64, he * 64)),
                        waits=([(vser, pt_free_at), (aser, a_free_at[c2]),
                                dsg, dcv] if first else []))
                    first = False
                    p.step("tensor", lambda e, c=c, c2=c2, j=j, he=he: e.matmul(
                        out=psA[c2][he * 64:(he + 1) * 64,
                                    j * 64:(j + 1) * 64],
                        lhsT=cvfl[0:1, (c % 2) * 1024 + j * 128 + he * 64:
                                  (c % 2) * 1024 + j * 128 + (he + 1) * 64],
                        rhs=sgfl[0:1, (c % 2) * 1024 + (2 * j + he) * 64:
                                 (c % 2) * 1024 + (2 * j + he + 1) * 64],
                        start=False, stop=True,
                        tile_position=(0, he * 64)),
                        inc=(t_att if (j == NB - 1 and he == 1) else None))
            rank1_done[c] = t_att.n
            p.step("scalar", lambda e, c=c, c2=c2: e.activation(
                out=ast[:, c2 * 512:(c2 + 1) * 512],
                in_=psA[c2][:], func=AF.Identity),
                waits=[(t_att, rank1_done[c])] + ([da] if t >= 1 else []))
            a_free_at[c2] = aser.n
            p.dma("gpsimd", lambda e, c=c, c2=c2: e.dma_start(
                out=a_d[:, c * CS:(c + 1) * CS].rearrange(
                    "(j p) q -> p j q", p=128),
                in_=ast[:, c2 * 512:(c2 + 1) * 512].rearrange(
                    "p (j q) -> p j q", j=NB)),
                waits=[aser], inc=da)
        for c in (c0 + 2, c0 + 3):
            if c < NCH:
                issue_flat_dma(c, t)
        if _os.environ.get("WQ_OFF", "0") != "1":
            for fn in wq_sched.get(t, []):
                fn()
    if _os.environ.get("WQ_OFF", "0") != "1":
        for fn in wq_tail:
            fn()
    v_wq_done = vser.n
    wq_bar = barrier("wq_bar", [(dqw, dqw.n), (dwl, dwl.n),
                                (vser, v_wq_done), (t_wq, t_wq.n)])
    v_att_done = vser.n
    a_att_done = aser.n
    att_bar = barrier("att_bar", [(vser, v_att_done), (aser, a_att_done),
                                  (da, da.n), dsg, dcv,
                                  (t_att, t_att.n)])

    if "dbg_stats" in dbg_d:
        for idx, tns in enumerate([mcol, sumsq, rstd, sumx, lnt_o, bcol]):
            p.dma("gpsimd", lambda e, idx=idx, tns=tns: e.dma_start(
                out=dbg_d["dbg_stats"][:, idx * 16:(idx + 1) * 16], in_=tns[:]),
                waits=[(aser, act_done_own[NTT - 1]),
                       (vser, dve_done_own[NTT - 1])], inc=dbg_done)
    if "dbg_xhd" in dbg_d:
        p.dma("gpsimd", lambda e: e.dma_start(
            out=dbg_d["dbg_xhd"], in_=xhat_d), waits=[(dxh, 16 * NTT)],
            inc=dbg_done)
    if "dbg_xhat" in dbg_d:
        p.dma("gpsimd", lambda e: e.dma_start(
            out=dbg_d["dbg_xhat"], in_=xhat[:]), waits=[(dtrx, 16 * NB)],
            inc=dbg_done)
    for nm, srct in [("dbg_q", Q), ("dbg_k", K), ("dbg_v", V)]:
        if nm in dbg_d:
            p.dma("gpsimd", lambda e, nm=nm, srct=srct: e.dma_start(
                out=dbg_d[nm], in_=srct[:]),
                waits=[(vser, v_p1_done), (aser, a_p1_done)], inc=dbg_done)
    if "dbg_a" in dbg_d:
        p.dma("gpsimd", lambda e: e.dma_start(
            out=dbg_d["dbg_a"].rearrange("p (j x) -> (j p) x", j=NB), in_=a_d),
            waits=[(att_bar, 1)], inc=dbg_done)
    if "dbg_sig" in dbg_d:
        p.dma("gpsimd", lambda e: e.dma_start(out=dbg_d["dbg_sig"], in_=sgm[:]),
              waits=[(vser, v_sig)], inc=dbg_done)
    if "dbg_cvx" in dbg_d:
        p.dma("gpsimd", lambda e: e.dma_start(out=dbg_d["dbg_cvx"],
                                              in_=CVXsb[:]),
              waits=[(vser, v_cvx)], inc=dbg_done)
    if "dbg_c0k" in dbg_d:
        p.dma("gpsimd", lambda e: e.dma_start(out=dbg_d["dbg_c0k"], in_=c0k[:]),
              waits=[(vser, v_carry_done)], inc=dbg_done)
    if "dbg_c0v" in dbg_d:
        p.dma("gpsimd", lambda e: e.dma_start(out=dbg_d["dbg_c0v"],
                                              in_=c0vrow[:]),
              waits=[(vser, v_carry_done)], inc=dbg_done)


    if _ps2:
        free("psA0", "psA1", "psPT", "psSb", "psSa")
    else:
        free("psA0", "psA1", "psPT", "psSa")
    free("pwq", "wabs", "abuf", "wstg")
    free("ast", "rsi", "rs", "mx", "PTsb", "Pn", "Sm", "cvfl", "sgfl",
         "sgm", "CVXsb", "CVsb", "CKcum", "CK", "cmask_sb")
    free("V", "K", "Q")
    dr2 = S("dr2")
    dxr = S("dxr")
    dal = S("dal")
    dwo = S("dwo")
    dwgl = S("dwgl")
    dwul = S("dwul")
    dx2q = S("dx2q")
    t_wo = S("t_wo")
    Wgq_sb = sb("Wgq_sb", [128, NB * INNER], BF16)
    A_sb = sb("A_sb", [128, NB * TOK], BF16)
    wout_sb = sb("wout_sb", [128, NB * D], BF16)
    xre = sb("xre", [128, 2 * D], F32)
    r2st = sb("r2st", [128, 2 * D], F32)
    xh2 = sb("xh2", [128, D], F32)
    x2qst = sb("x2qst", [128, 2 * D], BF16)
    psW = [ps("psW0", [128, D]), ps("psW1", [128, D])]
    for b in range(NB):
        p.dma("gpsimd", lambda e, b=b: e.dma_start(
            out=A_sb[:, b * TOK:(b + 1) * TOK],
            in_=a_d[b * 128:(b + 1) * 128, :]),
            waits=[(att_bar, 1), (wq_bar, 1)], inc=dal)
        p.dma("gpsimd", lambda e, b=b: e.dma_start(
            out=wout_sb[:, b * D:(b + 1) * D],
            in_=wout[b * 128:(b + 1) * 128, :]),
            waits=[(wq_bar, 1)], inc=dwo)
        p.dma("gpsimd", lambda e, b=b: e.dma_start(
            out=Wgq_sb[:, b * INNER:(b + 1) * INNER],
            in_=wgq_d[b * 128:(b + 1) * 128, :]),
            waits=[(wq_bar, 1)], inc=dwgl)
    r2_evac_v = [0] * NTT
    for tt in range(NTT):
        pb = tt % 2
        p.dma("sync", lambda e, tt=tt, pb=pb: e.dma_start(
            out=xre[:, pb * D:(pb + 1) * D],
            in_=x_own[tt * 128:(tt + 1) * 128, :]),
            waits=([(wq_bar, 1)] if tt < 1 else [(vser, r2_evac_v[tt - 1])]),
            inc=dxr)
        for b in range(NB):
            for hf in range(2):
                p.step("tensor", lambda e, tt=tt, b=b, hf=hf, pb=pb: e.matmul(
                    out=psW[pb][:, hf * 512:(hf + 1) * 512],
                    lhsT=A_sb[:, b * TOK + tt * 128:b * TOK + (tt + 1) * 128],
                    rhs=wout_sb[:, b * D + hf * 512:b * D + (hf + 1) * 512],
                    start=(b == 0), stop=(b == NB - 1)),
                    waits=(([(dal, 16 * NB), (dwo, 16 * NB), (wq_bar, 1)]
                            if tt == 0 else [])
                           + ([(vser, r2_evac_v[tt - 2])]
                              if (tt >= 2 and b == 0 and hf == 0) else [])),
                    inc=(t_wo if (b == NB - 1 and hf == 1) else None))
        p.step("vector", lambda e, tt=tt, pb=pb: e.scalar_tensor_tensor(
            out=r2st[:, pb * D:(pb + 1) * D], in0=psW[pb][:], scalar=1.0,
            in1=xre[:, pb * D:(pb + 1) * D], op0=ALU.mult, op1=ALU.add),
            waits=[(t_wo, tt + 1), (dxr, 16 * (tt + 1)), dr2])
        p.dma("gpsimd", lambda e, tt=tt, pb=pb: e.dma_start(
            out=r2_d[tt * 128:(tt + 1) * 128, :],
            in_=r2st[:, pb * D:(pb + 1) * D]),
            waits=[vser], inc=dr2)
        i = tt
        p.step("scalar", lambda e, i=i, pb=pb: e.activation(
            out=xh2[:], in_=r2st[:, pb * D:(pb + 1) * D], func=AF.Square,
            accum_out=sq2c[:, i:i + 1]), waits=[vser])
        p.step("vector", lambda e, i=i, pb=pb: e.reduce_sum(
            out=m2c[:, i:i + 1], in_=r2st[:, pb * D:(pb + 1) * D],
            axis=AX.X))
        p.step("vector", lambda e, i=i: e.tensor_single_scalar(
            out=m2c[:, i:i + 1], in_=m2c[:, i:i + 1],
            scalar=1.0 / 1024.0, op=ALU.mult))
        p.step("vector", lambda e, i=i: e.tensor_tensor(
            out=msq2c[:, i:i + 1], in0=m2c[:, i:i + 1],
            in1=m2c[:, i:i + 1], op=ALU.mult))
        p.step("vector", lambda e, i=i: e.scalar_tensor_tensor(
            out=sq2c[:, i:i + 1], in0=sq2c[:, i:i + 1],
            scalar=1.0 / 1024.0, in1=msq2c[:, i:i + 1],
            op0=ALU.mult, op1=ALU.subtract), waits=[aser])
        p.step("scalar", lambda e, i=i: e.activation(
            out=ln2c[:, i:i + 1], in_=sq2c[:, i:i + 1], func=AF.Ln,
            bias=1e-5), waits=[vser])
        p.step("scalar", lambda e, i=i: e.activation(
            out=rstd2c[:, i:i + 1], in_=ln2c[:, i:i + 1], func=AF.Exp,
            scale=-0.5))
        p.step("vector", lambda e, i=i: e.scalar_tensor_tensor(
            out=bc2c[:, i:i + 1], in0=m2c[:, i:i + 1], scalar=-1.0,
            in1=rstd2c[:, i:i + 1], op0=ALU.mult, op1=ALU.mult),
            waits=[aser])
        p.step("vector", lambda e, i=i, pb=pb: e.tensor_scalar(
            out=xh2[:], in0=r2st[:, pb * D:(pb + 1) * D],
            scalar1=rstd2c[:, i:i + 1], scalar2=bc2c[:, i:i + 1],
            op0=ALU.mult, op1=ALU.add), waits=[aser])
        p.step("vector", lambda e, i=i: e.reduce_max(
            out=amc[:, i:i + 1], in_=xh2[:], axis=AX.X,
            apply_absolute_value=True))
        p.step("vector", lambda e, i=i: e.tensor_single_scalar(
            out=amc[:, i:i + 1], in_=amc[:, i:i + 1], scalar=1e-5,
            op=ALU.max))
        p.step("vector", lambda e, i=i: e.reciprocal(
            out=s1c[:, i:i + 1], in_=amc[:, i:i + 1]))
        p.step("vector", lambda e, i=i: e.tensor_single_scalar(
            out=s1c[:, i:i + 1], in_=s1c[:, i:i + 1], scalar=127.0,
            op=ALU.mult))
        p.step("vector", lambda e, i=i: e.tensor_single_scalar(
            out=rs1c[:, i:i + 1], in_=amc[:, i:i + 1],
            scalar=1.0 / 127.0, op=ALU.mult))
        p.step("vector", lambda e, i=i: e.tensor_tensor(
            out=gscc[:, i:i + 1], in0=rs1c[:, i:i + 1],
            in1=wmcol[:, 0:1], op=ALU.mult))
        p.step("vector", lambda e, i=i: e.tensor_tensor(
            out=uscc[:, i:i + 1], in0=rs1c[:, i:i + 1],
            in1=wmcol[:, 1:2], op=ALU.mult))
        p.step("vector", lambda e, i=i: e.tensor_scalar(
            out=xh2[:], in0=xh2[:], scalar1=s1c[:, i:i + 1],
            scalar2=mgc[:], op0=ALU.mult, op1=ALU.add))
        p.step("vector", lambda e, i=i, pb=pb: e.tensor_scalar(
            out=x2qst[:, pb * D:(pb + 1) * D], in0=xh2[:],
            scalar1=mgc[:], scalar2=None, op0=ALU.subtract),
            waits=[dx2q])
        r2_evac_v[tt] = vser.n
        p.dma("gpsimd", lambda e, i=i, pb=pb: e.dma_start(
            out=x2q_d[i * 128:(i + 1) * 128, :],
            in_=x2qst[:, pb * D:(pb + 1) * D]),
            waits=[vser], inc=dx2q)
    p3_bar = barrier("p3_bar", [(dr2, 16 * NTT), (dx2q, 16 * NTT),
                                (t_wo, NTT), (vser, vser.n), (aser, aser.n)])
    free("psW0", "psW1", "x2qst", "xh2", "r2st", "xre", "wout_sb",
         "A_sb")

    dtx2 = S("dtx2")
    dhq = S("dhq")
    t_gu = S("t_gu")
    Wuq_sb = sb("Wuq_sb", [128, NB * INNER], BF16)
    for b in range(NB):
        p.dma("gpsimd", lambda e, b=b: e.dma_start(
            out=Wuq_sb[:, b * INNER:(b + 1) * INNER],
            in_=wuq_d[b * 128:(b + 1) * 128, :]),
            waits=[(p3_bar, 1)], inc=dwul)
    X2qT = sb("X2qT", [128, 2 * NB * 512], BF16)
    Hh = sb("Hh", [128, 5 * INNER], BF16)
    gbuf = sb("gbuf", [128, 3 * 512], BF16)
    hqs = sb("hqs", [128, D], F32)
    psG = [ps("psG0", [128, 512]), ps("psG1", [128, 512]),
           ps("psG2", [128, 512])]
    psU = [ps("psU0", [128, 512]), ps("psU1", [128, 512]),
           ps("psU2", [128, 512])]
    n_gu = 0
    gu_evac_done = [0] * 128
    aq2_done_v = [0] * NTT

    dtx2_done = [0] * 4

    def emit_transposes(q):
        qp = q % 2
        for b in range(NB):
            p.dma("sync", lambda e, q=q, b=b, qp=qp: e.dma_start(
                out=X2qT[:, qp * NB * 512 + b * 512:
                         qp * NB * 512 + (b + 1) * 512],
                in_=x2q_d[q * 512:(q + 1) * 512, b * 128:(b + 1) * 128],
                transpose=True),
                waits=[(p3_bar, 1)]
                      + ([(dtx2, 16 * NB * q)] if (q >= 1 and b == 0) else [])
                      + ([(t_gu, 32 * (q - 1))] if q >= 2 else []),
                inc=dtx2)
        dtx2_done[q] = dtx2.n

    emit_transposes(0)
    for i in range(NTT):
        q, t4 = i // 4, i % 4
        qp = q % 2
        if t4 == 0 and q + 1 < 4:
            emit_transposes(q + 1)
        for ic in range(8):
            pb = n_gu % (3 if _os.environ.get("GU2", "0") != "1" else 2)
            for b in range(NB):
                p.step("tensor", lambda e, qp=qp, t4=t4, ic=ic, b=b, pb=pb:
                       e.matmul(
                    out=psG[pb][:],
                    lhsT=X2qT[:, qp * NB * 512 + b * 512 + t4 * 128:
                              qp * NB * 512 + b * 512 + (t4 + 1) * 128],
                    rhs=Wgq_sb[:, b * INNER + ic * 512:
                               b * INNER + (ic + 1) * 512],
                    start=(b == 0), stop=(b == NB - 1)),
                    waits=(([(dtx2, dtx2_done[q]),
                             (dwgl, 16 * NB), (dwul, 16 * NB)]
                            if b == 0 and ic == 0 and t4 == 0 else [])
                           + ([(vser, gu_evac_done[n_gu - 3])]
                              if (n_gu >= 3 and b == 0) else [])))
                p.step("tensor", lambda e, qp=qp, t4=t4, ic=ic, b=b, pb=pb:
                       e.matmul(
                    out=psU[pb][:],
                    lhsT=X2qT[:, qp * NB * 512 + b * 512 + t4 * 128:
                              qp * NB * 512 + b * 512 + (t4 + 1) * 128],
                    rhs=Wuq_sb[:, b * INNER + ic * 512:
                               b * INNER + (ic + 1) * 512],
                    start=(b == 0), stop=(b == NB - 1)),
                    inc=(t_gu if b == NB - 1 else None))
            p.step("scalar", lambda e, i=i, ic=ic, pb=pb: e.activation(
                out=gbuf[:, pb * 512:(pb + 1) * 512], in_=psG[pb][:],
                func=AF.Silu, scale=gscc[:, i:i + 1]),
                waits=[(t_gu, t_gu.n)])
            p.step("vector", lambda e, i=i, ic=ic, pb=pb:
                   e.scalar_tensor_tensor(
                out=Hh[:, (i % 5) * INNER + ic * 512:
                       (i % 5) * INNER + (ic + 1) * 512],
                in0=psU[pb][:], scalar=uscc[:, i:i + 1],
                in1=gbuf[:, pb * 512:(pb + 1) * 512],
                op0=ALU.mult, op1=ALU.mult),
                waits=[aser] + ([dhq] if ic == 0 else []))
            gu_evac_done[n_gu] = vser.n
            n_gu += 1
        sl = (i % 5) * INNER
        p.step("vector", lambda e, sl=sl, i=i: e.reduce_max(
            out=amc[:, i:i + 1], in_=Hh[:, sl:sl + INNER], axis=AX.X,
            apply_absolute_value=True))
        p.step("vector", lambda e, i=i: e.tensor_single_scalar(
            out=amc[:, i:i + 1], in_=amc[:, i:i + 1], scalar=1e-5,
            op=ALU.max))
        p.step("vector", lambda e, i=i: e.reciprocal(
            out=s2c[:, i:i + 1], in_=amc[:, i:i + 1]))
        p.step("vector", lambda e, i=i: e.tensor_single_scalar(
            out=s2c[:, i:i + 1], in_=s2c[:, i:i + 1], scalar=127.0,
            op=ALU.mult))
        p.step("vector", lambda e, i=i: e.tensor_single_scalar(
            out=dscc[:, i:i + 1], in_=amc[:, i:i + 1],
            scalar=1.0 / 127.0, op=ALU.mult))
        p.step("vector", lambda e, i=i: e.tensor_tensor(
            out=dscc[:, i:i + 1], in0=dscc[:, i:i + 1],
            in1=wmcol[:, 2:3], op=ALU.mult))
        for hqc in range(4):
            p.step("vector", lambda e, sl=sl, i=i, hqc=hqc: e.tensor_scalar(
                out=hqs[:], in0=Hh[:, sl + hqc * D:sl + (hqc + 1) * D],
                scalar1=s2c[:, i:i + 1], scalar2=mgc[:],
                op0=ALU.mult, op1=ALU.add))
            p.step("vector", lambda e, sl=sl, i=i, hqc=hqc: e.tensor_scalar(
                out=Hh[:, sl + hqc * D:sl + (hqc + 1) * D], in0=hqs[:],
                scalar1=mgc[:], scalar2=None, op0=ALU.subtract))
        aq2_done_v[i] = vser.n
        p.dma("gpsimd", lambda e, sl=sl, i=i: e.dma_start(
            out=hq_d[i * 128:(i + 1) * 128, :],
            in_=Hh[:, sl:sl + INNER]),
            waits=[vser], inc=dhq)
    p5a_bar = barrier("p5a_bar", [(dhq, 16 * NTT), (vser, vser.n),
                                  (aser, aser.n), (t_gu, t_gu.n)])
    free("psU2", "psU1", "psU0", "psG2", "psG1", "psG0")
    free("hqs", "gbuf", "Hh", "X2qT", "Wuq_sb", "Wgq_sb")

    dwdl = S("dwdl")
    dhql = S("dhql")
    dr2b = S("dr2b")
    dyo = S("dyo")
    t_dn = S("t_dn")
    Wdq_sb = sb("Wdq_sb", [128, NIC * D], BF16)
    HqT = sb("HqT", [128, NIC * 512], BF16)
    r2b = sb("r2b", [128, 2 * D], F32)
    yst = sb("yst", [128, 2 * D], F32)
    psD = [ps("psD0", [128, D]), ps("psD1", [128, D])]
    for t in range(NB):
        p.dma("gpsimd", lambda e, t=t: e.dma_start(
            out=Wdq_sb[:, t * 4 * D:(t + 1) * 4 * D].rearrange(
                "p (a f) -> p a f", a=4),
            in_=wdq_d[t * 512:(t + 1) * 512, :].rearrange(
                "(a p) f -> p a f", p=128)),
            waits=[(p5a_bar, 1)], inc=dwdl)
    y_evac_v = [0] * NTT
    for q in range(4):
        for b32 in range(NIC):
            p.dma("sync", lambda e, q=q, b32=b32: e.dma_start(
                out=HqT[:, b32 * 512:(b32 + 1) * 512],
                in_=hq_d[q * 512:(q + 1) * 512, b32 * 128:(b32 + 1) * 128],
                transpose=True),
                waits=[(p5a_bar, 1)] + ([(t_dn, t_dn.n)] if q >= 1 else []),
                inc=dtx2)
        for t4 in range(4):
            i = q * 4 + t4
            pb = i % 2
            p.dma("sync", lambda e, i=i, pb=pb: e.dma_start(
                out=r2b[:, pb * D:(pb + 1) * D],
                in_=r2_d[i * 128:(i + 1) * 128, :]),
                waits=[(p5a_bar, 1)] + ([(vser, y_evac_v[i - 1])]
                                        if i >= 1 else []), inc=dr2b)
            for b32 in range(NIC):
                for hf in range(2):
                    p.step("tensor", lambda e, q=q, t4=t4, b32=b32, hf=hf,
                           pb=pb: e.matmul(
                        out=psD[pb][:, hf * 512:(hf + 1) * 512],
                        lhsT=HqT[:, b32 * 512 + t4 * 128:
                                 b32 * 512 + (t4 + 1) * 128],
                        rhs=Wdq_sb[:, b32 * D + hf * 512:
                                   b32 * D + (hf + 1) * 512],
                        start=(b32 == 0), stop=(b32 == NIC - 1)),
                        waits=(([(dtx2, dtx2.n), (dwdl, 16 * NB)]
                                if t4 == 0 and b32 == 0 and hf == 0 else [])
                               + ([(vser, y_evac_v[i - 2])]
                                  if (i >= 2 and b32 == 0 and hf == 0)
                                  else [])),
                        inc=(t_dn if (b32 == NIC - 1 and hf == 1) else None))
            p.step("vector", lambda e, i=i, pb=pb: e.scalar_tensor_tensor(
                out=yst[:, pb * D:(pb + 1) * D], in0=psD[pb][:],
                scalar=dscc[:, i:i + 1], in1=r2b[:, pb * D:(pb + 1) * D],
                op0=ALU.mult, op1=ALU.add),
                waits=[(t_dn, i + 1), (dr2b, 16 * (i + 1)), dyo])
            y_evac_v[i] = vser.n
            p.dma("gpsimd", lambda e, i=i, pb=pb: e.dma_start(
                out=y[i * 128:(i + 1) * 128, :],
                in_=yst[:, pb * D:(pb + 1) * D]),
                waits=[vser], inc=dyo)

    p.step("sync", None, waits=[dbg_done, dxh, dwq, dtrx, dxo, dxp,
                                dsg, dcv, (dyo, 16 * NTT), dwl, dqw, dal,
                                dwo, dxr, dr2, dwgl, dwul, dx2q,
                                dtx2, dhq, dwdl, dr2b])

    block = es.enter_context(nc.Block())
    p.emit(block)
    es.close()
    free_all()
    return nc


def make_inputs(inputs):
    x = np.ascontiguousarray(np.asarray(inputs["L0_attn_x"], dtype=np.float32)
                             .reshape(4 * 4096, D))
    wqkv_a = np.ascontiguousarray(np.asarray(inputs["L0_attn_W_qkv"], np.float32))
    wout_a = np.ascontiguousarray(np.asarray(inputs["L0_attn_W_out"], np.float32))
    wgT = np.ascontiguousarray(np.asarray(inputs["L0_ffn_w_gate"], np.float32).T)
    wuT = np.ascontiguousarray(np.asarray(inputs["L0_ffn_w_up"], np.float32).T)
    wdT = np.ascontiguousarray(np.asarray(inputs["L0_ffn_w_down"], np.float32).T)
    cst, cbf, cmask = host_consts()
    crow = np.ones((1, 128), dtype=np.float32)
    zeros = np.zeros((TOK, D), dtype=np.float32)
    in_maps = []
    for c in range(8):
        xo = x[c * TOK:(c + 1) * TOK]
        xp = zeros if c % 2 == 0 else x[(c - 1) * TOK:c * TOK]
        in_maps.append(dict(x_own=np.ascontiguousarray(xo),
                            x_prev=np.ascontiguousarray(xp),
                            wqkv=wqkv_a, wout=wout_a, wgT=wgT, wuT=wuT,
                            wdT=wdT, cst=cst, cbf=cbf, cmask=cmask,
                            crow=crow))
    return in_maps


def kernel(**inputs):
    in_maps = make_inputs(inputs)
    nc = build_program()
    res = run_bass_kernel_spmd(nc, in_maps, core_ids=list(range(8)))
    out = np.concatenate([r["y"] for r in res.results], axis=0)
    return out.reshape(4, 4096, D)


# version 08
# speedup vs baseline: 1.5706x; 1.5706x over previous
import os as _os
import numpy as np
import ml_dtypes
import concourse.bass as bass
import concourse.mybir as mybir
from concourse.bass_utils import run_bass_kernel_spmd

F32 = mybir.dt.float32
BF16 = mybir.dt.bfloat16
I32 = mybir.dt.int32
ALU = mybir.AluOpType
AF = mybir.ActivationFunctionType
AX = mybir.AxisListType

TOK = 2048
D = 1024
NB = 8
NTT = TOK // 128
CS = 64
NCH = TOK // CS
H = 16
HD = 64
SCALE = HD ** -0.5
INNER = 4096
NIC = INNER // 128
NEG = -65000.0

CONSTS = [0.0, 1.0, 0.75, 0.5, 1e-5, 1.0 / 1024.0, -0.5, SCALE,
          -SCALE / 64.0, 127.0, 256.0, -256.0, 1.0 / 128.0, -1.0,
          9.313225746154785e-10, 12582912.0, 6755399441055744.0]


class Sem:
    def __init__(self, es, nc, name):
        self.h = es.enter_context(nc.semaphore(name))
        self.n = 0


class Prog:

    def __init__(self):
        self.steps = []
        self.ser = {}
        self.selfwait = {"vector": True, "gpsimd": True, "scalar": True}

    def set_serial(self, eng, sem):
        self.ser[eng] = sem

    def step(self, eng, fn, waits=(), inc=None, inck=1, is_dma=False):
        ws = []
        for w in waits:
            if isinstance(w, Sem):
                ws.append((w, w.n))
            else:
                ws.append((w[0], w[1]))
        auto = None
        ser = None
        if not is_dma and eng in self.ser:
            s = self.ser[eng]
            if self.selfwait.get(eng):
                auto = (s, s.n)
            s.n += 1
            ser = s
        self.steps.append((eng, fn, ws, inc, inck, auto, ser))
        if inc is not None:
            inc.n += inck

    def dma(self, eng, fn, waits=(), inc=None):
        self.step(eng, fn, waits=waits, inc=inc, inck=16, is_dma=True)

    def emit(self, block):
        def mk(engname):
            def body(eng):
                for (e, fn, ws, inc, inck, auto, ser) in self.steps:
                    if e != engname:
                        continue
                    if auto is not None and auto[1] > 0:
                        eng.wait_ge(auto[0].h, auto[1])
                    for (s, v) in ws:
                        if v > 0:
                            eng.wait_ge(s.h, v)
                    inst = fn(eng) if fn is not None else None
                    if inst is not None:
                        if ser is not None:
                            inst.then_inc(ser.h, 1)
                        if inc is not None:
                            inst.then_inc(inc.h, inck)
            return body

        block.sync(mk("sync"))
        block.vector(mk("vector"))
        block.scalar(mk("scalar"))
        block.tensor(mk("tensor"))
        block.gpsimd(mk("gpsimd"))


def host_consts():
    cst = np.tile(np.array(CONSTS, dtype=np.float32), (128, 1))
    idn = np.eye(128, dtype=np.float32).astype(ml_dtypes.bfloat16)
    lind = np.zeros((128, 8 * 16), dtype=ml_dtypes.bfloat16)
    for j in range(8):
        lind[0:64, 16 * j + 2 * j] = 1.0
        lind[64:128, 16 * j + 2 * j + 1] = 1.0
    cvind = np.zeros((128, 16 * 32), dtype=ml_dtypes.bfloat16)
    for tt in range(16):
        cvind[0:64, 32 * tt + 2 * tt] = 1.0 / 128.0
        cvind[64:128, 32 * tt + 2 * tt + 1] = 1.0 / 128.0
    tri = np.zeros((128, 32), dtype=ml_dtypes.bfloat16)
    tri[0:32, :] = np.tril(np.ones((32, 32)), -1).T.astype(ml_dtypes.bfloat16)
    ones_col = np.ones((128, 1), dtype=ml_dtypes.bfloat16)
    ones_row32 = np.zeros((128, 32), dtype=ml_dtypes.bfloat16)
    ones_row32[0, :] = 1.0
    ones_row128 = np.zeros((128, 128), dtype=ml_dtypes.bfloat16)
    ones_row128[0, :] = 1.0
    mstk = np.zeros((128, 128), dtype=ml_dtypes.bfloat16)
    mstk[0:64, 0:64] = np.eye(64)
    mstk[0:64, 64:128] = np.eye(64)
    m64b = np.zeros((128, 64), dtype=ml_dtypes.bfloat16)
    m64b[0:64, :] = np.where(np.triu(np.ones((64, 64), dtype=bool), 1),
                             NEG, 0.0)
    cbf = np.concatenate([idn, lind, cvind, tri, ones_col, ones_row32,
                          ones_row128, mstk, m64b], axis=1)
    m64 = np.where(np.triu(np.ones((64, 64), dtype=bool), 1), NEG, 0.0)
    blk = np.concatenate([m64, m64], axis=0)
    cmask = np.tile(blk, (1, 16)).astype(np.float32)
    return cst, cbf, cmask


CB_IDN = 0
CB_LIND = 128
CB_CVIND = 256
CB_TRI = 768
CB_ONESC = 800
CB_ONESR32 = 801
CB_ONESR128 = 833
CB_MSTK = 961
CB_M64 = 1089


def build_program(dbg=()):
    nc = bass.Bass("TRN2", target_bir_lowering=False, debug=False)

    x_own = nc.dram_tensor("x_own", [TOK, D], F32, kind="ExternalInput").ap()
    x_prev = nc.dram_tensor("x_prev", [TOK, D], F32, kind="ExternalInput").ap()
    wqkv = nc.dram_tensor("wqkv", [D, 3 * D], F32, kind="ExternalInput").ap()
    wout = nc.dram_tensor("wout", [D, D], F32, kind="ExternalInput").ap()
    wgT = nc.dram_tensor("wgT", [D, INNER], F32, kind="ExternalInput").ap()
    wuT = nc.dram_tensor("wuT", [D, INNER], F32, kind="ExternalInput").ap()
    wdT = nc.dram_tensor("wdT", [INNER, D], F32, kind="ExternalInput").ap()
    cst_d = nc.dram_tensor("cst", [128, len(CONSTS)], F32, kind="ExternalInput").ap()
    cmask_d = nc.dram_tensor("cmask", [128, 1024], F32, kind="ExternalInput").ap()
    cbf_d = nc.dram_tensor("cbf", [128, 1153], BF16, kind="ExternalInput").ap()

    y = nc.dram_tensor("y", [TOK, D], F32, kind="ExternalOutput").ap()

    xhat_d = nc.dram_tensor("xhat_d", [TOK, D], BF16, kind="Internal").ap()
    a_d = nc.dram_tensor("a_d", [D, TOK], BF16, kind="Internal").ap()
    x2q_d = nc.dram_tensor("x2q_d", [TOK, D], BF16, kind="Internal").ap()
    hq_d = nc.dram_tensor("hq_d", [TOK, INNER], BF16, kind="Internal").ap()
    wgq_d = nc.dram_tensor("wgq_d", [D, INNER], BF16, kind="Internal").ap()
    wuq_d = nc.dram_tensor("wuq_d", [D, INNER], BF16, kind="Internal").ap()
    wdq_d = nc.dram_tensor("wdq_d", [INNER, D], BF16, kind="Internal").ap()
    r2_d = nc.dram_tensor("r2_d", [TOK, D], F32, kind="Internal").ap()
    crow_d = nc.dram_tensor("crow", [1, 128], F32, kind="ExternalInput").ap()

    dbg_d = {}
    for name, shape, dt in [
        ("dbg_xhat", [128, 16384], F32), ("dbg_q", [128, 16384], F32),
        ("dbg_k", [128, 16384], F32), ("dbg_v", [128, 16384], F32),
        ("dbg_c0k", [128, 8], F32), ("dbg_c0v", [1, D], F32),
        ("dbg_a", [128, 16384], F32), ("dbg_sig", [16, TOK], F32),
        ("dbg_cvx", [32, D], F32), ("dbg_stats", [128, 96], F32),
        ("dbg_xhd", [TOK, D], BF16),
    ]:
        if name in dbg:
            dbg_d[name] = nc.dram_tensor(name, shape, dt, kind="ExternalOutput").ap()

    import contextlib
    es = contextlib.ExitStack()
    galloc = []

    def _alloc(g, name):
        galloc.append((name, g))
        return g.__enter__()

    def sb(name, shape, dt):
        return _alloc(nc.sbuf_tensor(name, shape, dt), name)

    def ps(name, shape, dt=F32):
        return _alloc(nc.psum_tensor(name, shape, dt), name)

    def free(*names):
        want = set(names)
        for _ in names:
            nm, g = galloc.pop()
            assert nm in want, f"non-LIFO free: {nm} vs {want}"
            g.__exit__(None, None, None)

    def free_all():
        while galloc:
            galloc.pop()[1].__exit__(None, None, None)

    p = Prog()
    sems = {}

    def S(name):
        if name not in sems:
            sems[name] = Sem(es, nc, name)
        return sems[name]

    def barrier(nm, waits):
        bsem = S(nm)
        p.step("sync", lambda e: e.nop(), waits=waits, inc=bsem, is_dma=True,
               inck=1)
        return bsem

    vser = S("vser")
    aser = S("aser")
    pser = S("pser")
    p.set_serial("vector", vser)
    p.set_serial("scalar", aser)
    p.set_serial("gpsimd", pser)

    cst = sb("cst_sb", [128, len(CONSTS)], F32)
    for i, v in enumerate(CONSTS):
        nc.const_aps.register(F32, v, cst[:, i:i + 1])
    cbf = sb("cbf_sb", [128, 1153], BF16)

    sxcol = sb("sxcol", [128, 8], BF16)
    sxcol05 = sb("sxcol05", [128, 8], BF16)
    c0k = sb("c0k", [128, 8], F32)
    c0vrow = sb("c0vrow", [1, D], BF16)
    bcast_b = sb("bcast_b", [128, 1], F32)
    det_i = sb("det_i", [128, 1], I32)

    sumx = sb("sumx", [128, NTT], F32)
    sumsq = sb("sumsq", [128, NTT], F32)
    mcol = sb("mcol", [128, NTT], F32)
    rstd = sb("rstd", [128, NTT], F32)
    bcol = sb("bcol", [128, NTT], F32)
    sumxp = sb("sumxp", [128, NTT], F32)
    sumsqp = sb("sumsqp", [128, NTT], F32)
    mcolp = sb("mcolp", [128, NTT], F32)
    rstdp = sb("rstdp", [128, NTT], F32)
    rstdpb = sb("rstdpb", [128, NTT], BF16)
    rmb = sb("rmb", [128, NTT], BF16)
    msqt_o = sb("msqt_o", [128, NTT], F32)
    lnt_o = sb("lnt_o", [128, NTT], F32)
    msqt_p = sb("msqt_p", [128, NTT], F32)
    lnt_p = sb("lnt_p", [128, NTT], F32)
    scsb = sb("scsb", [1, 1], F32)
    sxrow = sb("sxrow", [1, D], BF16)
    crow = sb("crow_sb", [1, 128], F32)
    shiftb = sb("shiftb", [128, 1], F32)
    mgc = sb("mgc", [128, 1], F32)
    wmcol = sb("wmcol", [128, 3], F32)
    thrcol = sb("thrcol", [128, 3], F32)
    nthrcol = sb("nthrcol", [128, 3], F32)
    wsc = sb("wsc", [1, 8], F32)
    m2c = sb("m2c", [128, NTT], F32)
    sq2c = sb("sq2c", [128, NTT], F32)
    msq2c = sb("msq2c", [128, NTT], F32)
    ln2c = sb("ln2c", [128, NTT], F32)
    rstd2c = sb("rstd2c", [128, NTT], F32)
    bc2c = sb("bc2c", [128, NTT], F32)
    s1c = sb("s1c", [128, NTT], F32)
    rs1c = sb("rs1c", [128, NTT], F32)
    s2c = sb("s2c", [128, NTT], F32)
    gscc = sb("gscc", [128, NTT], F32)
    uscc = sb("uscc", [128, NTT], F32)
    dscc = sb("dscc", [128, NTT], F32)
    amc = sb("amc", [128, NTT], F32)

    Q = sb("Q", [128, NB * TOK], BF16)
    K = sb("K", [128, NB * TOK], BF16)
    V = sb("V", [128, NTT * D], BF16)
    xhat = sb("xhat", [128, NB * TOK], BF16)
    wqk_sb = sb("wqk_sb", [128, NB * 2048], BF16)
    wv_sb = sb("wv_sb", [128, NB * D], BF16)
    xin = sb("xin", [128, D], F32)
    xpin = sb("xpin", [128, D], F32)
    xpbf = sb("xpbf", [128, D], BF16)
    xhst = sb("xhst", [128, 2 * D], BF16)
    sqs = sb("sqs", [128, D], F32)

    dxo = S("dxo")
    dxp = S("dxp")
    dcst = S("dcst")
    dxh = S("dxh")
    dtrx = S("dtrx")
    dwq = S("dwq")
    t_sx = S("t_sx")
    t_c0 = S("t_c0")
    t_v = S("t_v")
    t_qk = S("t_qk")
    dbg_done = S("dbg_done")

    p.dma("sync", lambda e: e.dma_start(out=cst[:], in_=cst_d), inc=dcst)
    p.dma("sync", lambda e: e.dma_start(out=cbf[:], in_=cbf_d), inc=dcst)
    p.dma("sync", lambda e: e.dma_start(out=crow[:], in_=crow_d), inc=dcst)
    p.step("vector", lambda e: e.tensor_single_scalar(
        out=det_i[:], in_=cst[:, 2:3], scalar=0.0, op=ALU.add),
        waits=[dcst])
    p.step("vector", lambda e: e.tensor_scalar(
        out=bcast_b[:], in0=det_i[:], scalar1=-0.5, scalar2=0.5,
        op0=ALU.mult, op1=ALU.add))
    p.step("vector", lambda e: e.tensor_single_scalar(
        out=shiftb[:], in_=bcast_b[:], scalar=256.0, op=ALU.add))
    p.step("vector", lambda e: e.tensor_single_scalar(
        out=mgc[:], in_=cst[:, 1:2], scalar=9.313225746154785e-10,
        op=ALU.add), waits=[dcst])
    p.step("vector", lambda e: e.tensor_scalar(
        out=mgc[:], in0=mgc[:], scalar1=1.0, scalar2=1073741824.0,
        op0=ALU.subtract, op1=ALU.mult))
    p.step("vector", lambda e: e.tensor_scalar(
        out=mgc[:], in0=mgc[:],
        scalar1=6755399441055744.0 - 12582912.0, scalar2=12582912.0,
        op0=ALU.mult, op1=ALU.add))

    for b in range(NB):
        p.dma("gpsimd", lambda e, b=b: e.dma_start(
            out=wqk_sb[:, b * 2048:(b + 1) * 2048],
            in_=wqkv[b * 128:(b + 1) * 128, 0:2048]), inc=dwq)
        p.dma("gpsimd", lambda e, b=b: e.dma_start(
            out=wv_sb[:, b * D:(b + 1) * D],
            in_=wqkv[b * 128:(b + 1) * 128, 2048:3072]), inc=dwq)

    psx = ps("psx", [1, D])
    psc = ps("psc", [1, 1])
    dve_done_own = [0] * NTT
    dve_done_prev = [0] * NTT
    act_done_own = [0] * NTT
    act_done_prev = [0] * NTT
    pe_done_prev = [0] * NTT

    def ln_tile(i, src_d, buf, dsem, sumx_, sumsq_, m_, rstd_, msqt, lnt,
                dma_gate):
        bi = 0
        p.dma("sync", lambda e: e.dma_start(
            out=buf[:, bi:bi + D], in_=src_d[i * 128:(i + 1) * 128, :]),
            waits=dma_gate(i), inc=dsem)
        p.step("scalar", lambda e: e.activation(
            out=sqs[:], in_=buf[:, bi:bi + D], func=AF.Square,
            accum_out=sumsq_[:, i:i + 1]),
            waits=[(dsem, 16 * (i + 1)), dcst])
        p.step("vector", lambda e: e.reduce_sum(
            out=sumx_[:, i:i + 1], in_=buf[:, bi:bi + D], axis=AX.X),
            waits=[(dsem, 16 * (i + 1))])
        p.step("vector", lambda e: e.tensor_single_scalar(
            out=m_[:, i:i + 1], in_=sumx_[:, i:i + 1],
            scalar=1.0 / 1024.0, op=ALU.mult))
        p.step("vector", lambda e: e.tensor_tensor(
            out=msqt[:, i:i + 1], in0=m_[:, i:i + 1], in1=m_[:, i:i + 1],
            op=ALU.mult))
        p.step("vector", lambda e: e.scalar_tensor_tensor(
            out=sumsq_[:, i:i + 1], in0=sumsq_[:, i:i + 1],
            scalar=1.0 / 1024.0, in1=msqt[:, i:i + 1],
            op0=ALU.mult, op1=ALU.subtract), waits=[aser])
        p.step("scalar", lambda e: e.activation(
            out=lnt[:, i:i + 1], in_=sumsq_[:, i:i + 1], func=AF.Ln,
            bias=1e-5), waits=[vser])
        p.step("scalar", lambda e: e.activation(
            out=rstd_[:, i:i + 1], in_=lnt[:, i:i + 1], func=AF.Exp,
            scale=-0.5))

    for i in range(NTT):
        ln_tile(i, x_own, xin, dxo, sumx, sumsq, mcol, rstd, msqt_o, lnt_o,
                lambda j: [] if j < 1 else [(vser, dve_done_own[j - 1]),
                                            (aser, act_done_own[j - 1])])
        p.step("vector", lambda e, i=i: e.scalar_tensor_tensor(
            out=bcol[:, i:i + 1], in0=mcol[:, i:i + 1], scalar=-1.0,
            in1=rstd[:, i:i + 1], op0=ALU.mult, op1=ALU.mult),
            waits=[aser])
        dve_done_own[i] = vser.n
        p.step("scalar", lambda e, i=i, bi=(i % 2) * D: e.activation(
            out=xhst[:, bi:bi + D], in_=xin[:, 0:D], func=AF.Identity,
            scale=rstd[:, i:i + 1], bias=bcol[:, i:i + 1]),
            waits=[vser] + ([] if i < 2 else [(dxh, 16 * i)]))
        act_done_own[i] = aser.n
        p.dma("gpsimd", lambda e, i=i, bi=(i % 2) * D: e.dma_start(
            out=xhat_d[i * 128:(i + 1) * 128, :], in_=xhst[:, bi:bi + D]),
            waits=[(aser, aser.n)], inc=dxh)

        ln_tile(i, x_prev, xpin, dxp, sumxp, sumsqp, mcolp, rstdp, msqt_p,
                lnt_p,
                lambda j: [] if j < 1 else [(vser, dve_done_prev[j - 1]),
                                            (aser, act_done_prev[j - 1])])
        p.step("vector", lambda e, i=i: e.tensor_single_scalar(
            out=rstdpb[:, i:i + 1], in_=rstdp[:, i:i + 1], scalar=0.0,
            op=ALU.add), waits=[aser])
        p.step("vector", lambda e, i=i: e.scalar_tensor_tensor(
            out=rmb[:, i:i + 1], in0=mcolp[:, i:i + 1], scalar=1.0,
            in1=rstdp[:, i:i + 1], op0=ALU.mult, op1=ALU.mult))
        dve_done_prev[i] = vser.n
        p.step("scalar", lambda e, i=i: e.activation(
            out=xpbf[:, 0:D], in_=xpin[:, 0:D], func=AF.Identity),
            waits=([] if i < 1 else [(t_sx, pe_done_prev[i - 1])]))
        act_done_prev[i] = aser.n
        for hf in range(2):
            p.step("tensor", lambda e, i=i, hf=hf: e.matmul(
                out=psx[:, hf * 512:(hf + 1) * 512],
                lhsT=rstdpb[:, i:i + 1],
                rhs=xpbf[:, hf * 512:(hf + 1) * 512],
                start=(i == 0), stop=(i == NTT - 1)),
                waits=[vser, aser])
        p.step("tensor", lambda e, i=i: e.matmul(
            out=psc[:], lhsT=rmb[:, i:i + 1],
            rhs=cbf[:, CB_ONESC:CB_ONESC + 1],
            start=(i == 0), stop=(i == NTT - 1)),
            waits=[(dcst, 48)], inc=t_sx)
        pe_done_prev[i] = t_sx.n

    for b in range(NB):
        p.dma("sync", lambda e, b=b: e.dma_start(
            out=xhat[:, b * TOK:(b + 1) * TOK],
            in_=xhat_d[:, b * 128:(b + 1) * 128], transpose=True),
            waits=[(dxh, 16 * NTT)], inc=dtrx)

    p.step("vector", lambda e: e.tensor_copy(out=scsb[:], in_=psc[:]),
           waits=[(t_sx, NTT)])
    p.step("vector", lambda e: e.tensor_scalar(
        out=sxrow[:], in0=psx[:], scalar1=scsb[0:1, 0:1], scalar2=None,
        op0=ALU.subtract))
    v_after_sxrow = vser.n
    ptc = ps("ptc", [128, 16], BF16)
    for b in range(NB):
        p.step("tensor", lambda e, b=b: e.transpose(
            out=ptc[:, 2 * b:2 * b + 1],
            in_=sxrow[0:1, b * 128:(b + 1) * 128],
            identity=cbf[0:1, CB_ONESC:CB_ONESC + 1]),
            waits=[(vser, v_after_sxrow), (dcst, 48)],
            inc=(t_sx if b == NB - 1 else None))
    p.step("vector", lambda e: e.tensor_copy(out=sxcol[:], in_=ptc[:, 0:16:2]),
           waits=[(t_sx, NTT + 1)])
    p.step("vector", lambda e: e.tensor_single_scalar(
        out=sxcol05[:], in_=ptc[:, 0:16:2], scalar=1.0 / 128.0, op=ALU.mult))
    v_after_sxcol = vser.n

    pc0k = ps("pc0k", [128, 8])
    for kc in range(8):
        for b in range(NB):
            p.step("tensor", lambda e, b=b, kc=kc: e.matmul(
                out=pc0k[:, kc:kc + 1],
                lhsT=wqk_sb[:, b * 2048 + 1024 + kc * 128:
                            b * 2048 + 1024 + (kc + 1) * 128],
                rhs=sxcol[:, b:b + 1],
                start=(b == 0), stop=(b == NB - 1)),
                waits=[(vser, v_after_sxcol), (dwq, 16 * 2 * NB)],
                inc=(t_c0 if (b == NB - 1 and kc == 7) else None))
    pc0v = ps("pc0v", [1, D])
    for b in range(NB):
        for hf in range(2):
            p.step("tensor", lambda e, b=b, hf=hf: e.matmul(
                out=pc0v[:, hf * 512:(hf + 1) * 512],
                lhsT=sxcol05[:, b:b + 1],
                rhs=wv_sb[:, b * D + hf * 512: b * D + (hf + 1) * 512],
                start=(b == 0), stop=(b == NB - 1)),
                inc=(t_c0 if (b == NB - 1 and hf == 1) else None))
    p.step("vector", lambda e: e.tensor_copy(out=c0k[:], in_=pc0k[:]),
           waits=[(t_c0, 2)])
    p.step("vector", lambda e: e.tensor_copy(out=c0vrow[:], in_=pc0v[:]))
    v_carry_done = vser.n

    free("psx", "psc", "ptc", "pc0k", "pc0v")
    free("xin", "xpin", "xpbf", "xhst", "sqs")
    pvv = [ps("pvv0", [128, D]), ps("pvv1", [128, D])]
    ev_done = [0] * NTT
    for tt in range(NTT):
        pb = tt % 2
        for b in range(NB):
            for hf in range(2):
                p.step("tensor", lambda e, tt=tt, b=b, hf=hf, pb=pb: e.matmul(
                    out=pvv[pb][:, hf * 512:(hf + 1) * 512],
                    lhsT=xhat[:, b * TOK + tt * 128: b * TOK + (tt + 1) * 128],
                    rhs=wv_sb[:, b * D + hf * 512: b * D + (hf + 1) * 512],
                    start=(b == 0), stop=(b == NB - 1)),
                    waits=(([(dtrx, 16 * NB), (vser, v_carry_done)]
                            if b == 0 and hf == 0 else [])
                           + ([(vser, ev_done[tt - 2])]
                              if (tt >= 2 and b == 0 and hf == 0) else [])),
                    inc=(t_v if (b == NB - 1 and hf == 1) else None))
        p.step("vector", lambda e, tt=tt, pb=pb: e.tensor_copy(
            out=V[:, tt * D:(tt + 1) * D], in_=pvv[pb][:]),
            waits=[(t_v, tt + 1)])
        ev_done[tt] = vser.n

    eqk_done = [0] * 64
    pqk = [ps("pqk0", [128, 512]), ps("pqk1", [128, 512])]
    n_qk = 0
    for tg in range(4):
        for oc in range(16):
            pb = n_qk % 2
            for b in range(NB):
                p.step("tensor", lambda e, oc=oc, b=b, tg=tg, pb=pb: e.matmul(
                    out=pqk[pb][:],
                    lhsT=wqk_sb[:, b * 2048 + oc * 128:
                                b * 2048 + (oc + 1) * 128],
                    rhs=xhat[:, b * TOK + tg * 512: b * TOK + (tg + 1) * 512],
                    start=(b == 0), stop=(b == NB - 1)),
                    waits=([(vser if pb == 0 else aser,
                             eqk_done[n_qk - 2])] if (n_qk >= 2 and b == 0)
                           else []),
                    inc=(t_qk if b == NB - 1 else None))
            dst = Q if oc < 8 else K
            dj = oc % 8
            if pb == 0:
                p.step("vector", lambda e, dst=dst, dj=dj, tg=tg, pb=pb:
                       e.tensor_copy(
                           out=dst[:, dj * TOK + tg * 512:
                                   dj * TOK + (tg + 1) * 512],
                           in_=pqk[pb][:]), waits=[(t_qk, n_qk + 1)])
                eqk_done[n_qk] = vser.n
            else:
                p.step("scalar", lambda e, dst=dst, dj=dj, tg=tg, pb=pb:
                       e.activation(
                           out=dst[:, dj * TOK + tg * 512:
                                   dj * TOK + (tg + 1) * 512],
                           in_=pqk[pb][:], func=AF.Identity),
                       waits=[(t_qk, n_qk + 1)])
                eqk_done[n_qk] = aser.n
            n_qk += 1
    v_p1_done = vser.n
    a_p1_done = aser.n


    free("pqk0", "pqk1", "pvv0", "pvv1")
    if "dbg_xhat" in dbg_d:
        free("wv_sb")
    else:
        free("wv_sb", "wqk_sb", "xhat")

    t_cv = S("t_cv")
    t_log = S("t_log")
    t_att = S("t_att")
    dsg = S("dsg")
    dcv = S("dcv")
    p2rdy = barrier("p2rdy", [(t_qk, 64), (dtrx, 16 * NB), (dwq, 16 * 2 * NB),
                              (dxh, 16 * NTT), (dxo, 16 * NTT),
                              (dxp, 16 * NTT), (t_v, NTT), (t_sx, NTT + 1),
                              (t_c0, 2), (vser, v_p1_done),
                              (aser, a_p1_done)])

    CK = sb("CK", [128, NCH * 8], F32)
    CKcum = sb("CKcum", [128, NCH * 8], F32)
    CVsb = sb("CVsb", [32, D], BF16)
    CVXsb = sb("CVXsb", [32, D], BF16)
    sgm = sb("sgm", [16, TOK], F32)
    sgfl = sb("sgfl", [1, 2 * 1024], BF16)
    cvfl = sb("cvfl", [1, 2 * 1024], BF16)
    Sm = sb("Sm", [128, 1024], F32)
    Pn = sb("Pn", [128, 1024], BF16)
    PTsb = sb("PTsb", [128, 1024], BF16)
    mx = sb("mx", [128, 16], F32)
    rs = sb("rs", [128, 16], F32)
    rsi = sb("rsi", [128, 16], F32)
    ast = sb("ast", [128, 2 * 512], BF16)
    da = S("da")
    cmask = sb("cmask_sb", [128, 1024], F32)
    p.dma("sync", lambda e: e.dma_start(out=cmask[:], in_=cmask_d),
          waits=[(p2rdy, 1)], inc=dcst)
    CKB = sb("CKB", [128, TOK], BF16)
    Mb = sb("Mb", [128, TOK], BF16)

    for b in range(NB):
        p.step("vector", lambda e, b=b: e.reduce_sum(
            out=CK[:, b:NCH * 8:8],
            in_=K[:, b * TOK:(b + 1) * TOK].rearrange(
                "p (c x) -> p c x", c=NCH), axis=AX.X),
            waits=([(vser, v_p1_done), (aser, a_p1_done), (p2rdy, 1)]
                   if b == 0 else []))
    p.step("vector", lambda e: e.tensor_copy(out=CKcum[:, 0:8], in_=c0k[:]))
    for c in range(1, NCH):
        p.step("vector", lambda e, c=c: e.tensor_tensor(
            out=CKcum[:, c * 8:(c + 1) * 8],
            in0=CKcum[:, (c - 1) * 8:c * 8],
            in1=CK[:, (c - 1) * 8:c * 8], op=ALU.add))
    v_ckcum = vser.n

    pcv = ps("pcv", [32, D])
    for tt in range(NTT):
        for hf in range(2):
            p.step("tensor", lambda e, tt=tt, hf=hf: e.matmul(
                out=pcv[:, hf * 512:(hf + 1) * 512],
                lhsT=cbf[:, CB_CVIND + tt * 32:CB_CVIND + (tt + 1) * 32],
                rhs=V[:, tt * D + hf * 512:tt * D + (hf + 1) * 512],
                start=(tt == 0), stop=(tt == NTT - 1)),
                waits=([(t_v, NTT), (p2rdy, 1), (vser, v_p1_done),
                        (aser, a_p1_done)] if tt == 0 and hf == 0 else []),
                inc=(t_cv if (tt == NTT - 1 and hf == 1) else None))
    p.step("vector", lambda e: e.tensor_copy(out=CVsb[:], in_=pcv[:]),
           waits=[(t_cv, 1)])
    v_cvsb = vser.n
    pcvx = ps("pcvx", [32, D])
    for hf in range(2):
        p.step("tensor", lambda e, hf=hf: e.matmul(
            out=pcvx[:, hf * 512:(hf + 1) * 512],
            lhsT=cbf[0:32, CB_TRI:CB_TRI + 32],
            rhs=CVsb[:, hf * 512:(hf + 1) * 512],
            start=True, stop=False),
            waits=[(vser, v_cvsb)])
        p.step("tensor", lambda e, hf=hf: e.matmul(
            out=pcvx[:, hf * 512:(hf + 1) * 512],
            lhsT=cbf[0:1, CB_ONESR32:CB_ONESR32 + 32],
            rhs=c0vrow[0:1, hf * 512:(hf + 1) * 512],
            start=False, stop=True),
            waits=[(vser, v_carry_done)],
            inc=(t_cv if hf == 1 else None))
    p.step("vector", lambda e: e.tensor_copy(out=CVXsb[:], in_=pcvx[:]),
           waits=[(t_cv, 2)])
    v_cvx = vser.n
    free("pcv", "pcvx")

    plog = [ps("plog%d" % g, [16, 512]) for g in range(4)]
    mdone = [0] * NB
    for b in range(NB):
        p.step("vector", lambda e, b=b: e.tensor_copy(
            out=CKB[:],
            in_=CKcum[:, b:NCH * 8:8].unsqueeze(2)
                .broadcast_to([128, NCH, CS])),
            waits=([(t_log, t_log.n)] if b > 0 else []))
        p.step("vector", lambda e, b=b: e.tensor_tensor(
            out=Mb[:], in0=Q[:, b * TOK:(b + 1) * TOK], in1=CKB[:],
            op=ALU.mult))
        mdone[b] = vser.n
        for g in range(4):
            p.step("tensor", lambda e, b=b, g=g: e.matmul(
                out=plog[g][:],
                lhsT=cbf[:, CB_LIND + b * 16:CB_LIND + (b + 1) * 16],
                rhs=Mb[:, g * 512:(g + 1) * 512],
                start=(b == 0), stop=(b == NB - 1)),
                waits=[(vser, mdone[b])],
                inc=(t_log if g == 3 else None))
    for g in range(4):
        p.step("scalar", lambda e, g=g: e.activation(
            out=sgm[:, g * 512:(g + 1) * 512], in_=plog[g][:], func=AF.Exp,
            scale=-SCALE / 64.0), waits=[(t_log, NB), (p2rdy, 1)])
    p.step("vector", lambda e: e.tensor_single_scalar(
        out=sgm[:], in_=sgm[:], scalar=1.0, op=ALU.add), waits=[aser])
    p.step("vector", lambda e: e.reciprocal(out=sgm[:], in_=sgm[:]))
    v_sig = vser.n
    free("plog0", "plog1", "plog2", "plog3")
    free("Mb", "CKB")

    dwl = S("dwl")
    dqw = S("dqw")
    t_wq = S("t_wq")
    wstg = sb("wstg", [128, INNER], F32)
    abuf = sb("abuf", [128, INNER], BF16)
    wabs = sb("wabs", [128, 27], F32)
    pwq = ps("pwq", [128, 8])
    pw1 = pwq[:, 0:4]
    pwt = pwq[0:1, 4:8]
    wsrc = [(wgT, wgq_d, 0), (wuT, wuq_d, 1), (wdT, wdq_d, 2)]

    def wtile_aps(mat, t):
        srcT, dstT, mi = wsrc[mat]
        if mi < 2:
            return (srcT[t * 128:(t + 1) * 128, :],
                    dstT[t * 128:(t + 1) * 128, :])
        return (srcT[t * 512:(t + 1) * 512, :].rearrange(
                    "(a p) f -> p a f", p=128),
                dstT[t * 512:(t + 1) * 512, :].rearrange(
                    "(a p) f -> p a f", p=128))

    def wq_p1_tile(mat, t, eng="gpsimd"):
        p.dma("gpsimd", lambda e, mat=mat, t=t: e.dma_start(
            out=(wstg[:].rearrange("p (a f) -> p a f", a=4) if mat == 2
                 else wstg[:]),
            in_=wtile_aps(mat, t)[0]),
            waits=[pser, vser, aser], inc=dwl)
        p.step("scalar", lambda e, mat=mat, t=t: e.activation(
            out=wstg[:], in_=wstg[:], func=AF.Abs,
            accum_out=wabs[:, mat * 9 + t:mat * 9 + t + 1]), waits=[dwl])

    def wq_thr(mat):
        p.step("vector", lambda e, mat=mat: e.reduce_sum(
            out=wabs[:, mat * 9 + 8:mat * 9 + 9],
            in_=wabs[:, mat * 9:mat * 9 + 8], axis=AX.X),
            waits=[aser, pser])
        vv = vser.n
        p.step("tensor", lambda e, mat=mat: e.matmul(
            out=pwt[0:1, mat:mat + 1], lhsT=wabs[:, mat * 9 + 8:mat * 9 + 9],
            rhs=cst[:, 1:2],
            start=True, stop=True), waits=[(vser, vv)], inc=t_wq)
        p.step("vector", lambda e, mat=mat: e.tensor_scalar(
            out=wsc[0:1, mat:mat + 1], in0=pwt[0:1, mat:mat + 1],
            scalar1=1.0 / (D * INNER), scalar2=1e-5,
            op0=ALU.mult, op1=ALU.max), waits=[(t_wq, t_wq.n)])
        p.step("vector", lambda e, mat=mat: e.tensor_single_scalar(
            out=wsc[0:1, 3 + mat:4 + mat], in_=wsc[0:1, mat:mat + 1],
            scalar=0.5, op=ALU.mult))
        p.step("vector", lambda e, mat=mat: e.tensor_single_scalar(
            out=wsc[0:1, 6:7], in_=wsc[0:1, mat:mat + 1], scalar=-1.0,
            op=ALU.mult))
        vv2 = vser.n
        p.step("tensor", lambda e, mat=mat: e.matmul(
            out=pw1[:, mat:mat + 1], lhsT=crow[:], rhs=wsc[0:1, 6:7],
            start=True, stop=True), waits=[(vser, vv2)], inc=t_wq)
        p.step("vector", lambda e, mat=mat: e.tensor_copy(
            out=wmcol[:, mat:mat + 1], in_=pw1[:, mat:mat + 1]),
            waits=[(t_wq, t_wq.n)])
        p.step("tensor", lambda e, mat=mat: e.matmul(
            out=pw1[:, 3:4], lhsT=crow[:], rhs=wsc[0:1, 3 + mat:4 + mat],
            start=True, stop=True), waits=[vser], inc=t_wq)
        p.step("vector", lambda e, mat=mat: e.tensor_copy(
            out=thrcol[:, mat:mat + 1], in_=pw1[:, 3:4]),
            waits=[(t_wq, t_wq.n)])
        p.step("vector", lambda e, mat=mat: e.tensor_single_scalar(
            out=nthrcol[:, mat:mat + 1], in_=thrcol[:, mat:mat + 1],
            scalar=-1.0, op=ALU.mult))

    def wq_p2_tile(mat, t, eng="gpsimd"):
        p.dma("gpsimd", lambda e, mat=mat, t=t: e.dma_start(
            out=(wstg[:].rearrange("p (a f) -> p a f", a=4) if mat == 2
                 else wstg[:]),
            in_=wtile_aps(mat, t)[0]),
            waits=[pser, vser, aser], inc=dwl)
        p.step(eng, lambda e, mat=mat: e.tensor_scalar(
            out=abuf[:], in0=wstg[:], scalar1=thrcol[:, mat:mat + 1],
            scalar2=None, op0=ALU.is_gt), waits=[dwl, dqw])
        p.step(eng, lambda e, mat=mat: e.tensor_scalar(
            out=wstg[:], in0=wstg[:], scalar1=nthrcol[:, mat:mat + 1],
            scalar2=None, op0=ALU.is_lt))
        p.step(eng, lambda e: e.tensor_tensor(
            out=abuf[:], in0=wstg[:], in1=abuf[:], op=ALU.subtract))
        sem = pser if eng == "gpsimd" else vser
        vv = sem.n
        p.dma("gpsimd", lambda e, mat=mat, t=t, sem=sem, vv=vv: e.dma_start(
            out=wtile_aps(mat, t)[1],
            in_=(abuf[:].rearrange("p (a f) -> p a f", a=4) if mat == 2
                 else abuf[:])), waits=[(sem, vv)], inc=dqw)

    WQE = _os.environ.get("WQ_ENG", "vector")
    wq_sched = {}
    for t in range(4):
        wq_sched.setdefault(t, []).extend(
            [lambda m=0, k=2 * t: wq_p1_tile(m, k),
             lambda m=0, k=2 * t + 1: wq_p1_tile(m, k)])
    wq_sched.setdefault(4, []).append(lambda: wq_thr(0))
    for t in range(4):
        wq_sched.setdefault(4 + t, []).extend(
            [lambda m=1, k=2 * t: wq_p1_tile(m, k),
             lambda m=1, k=2 * t + 1: wq_p1_tile(m, k)])
    wq_sched.setdefault(8, []).append(lambda: wq_thr(1))
    for t in range(5, 13):
        wq_sched.setdefault(t, []).extend(
            [lambda m=0, k=2 * (t - 5): wq_p2_tile(m, k, WQE),
             lambda m=0, k=2 * (t - 5) + 1: wq_p2_tile(m, k, WQE)]
            if t < 9 else
            [lambda m=1, k=2 * (t - 9): wq_p2_tile(m, k, WQE),
             lambda m=1, k=2 * (t - 9) + 1: wq_p2_tile(m, k, WQE)])
    for t in range(4):
        wq_sched.setdefault(12 + t, []).extend(
            [lambda m=2, k=2 * t: wq_p1_tile(m, k, "vector"),
             lambda m=2, k=2 * t + 1: wq_p1_tile(m, k, "vector")])
    wq_tail = ([lambda: wq_thr(2)]
               + [lambda m=2, k=k: wq_p2_tile(m, k, "vector")
                  for k in range(8)])

    rank1_done = [0] * NCH
    sgcv_issued = [None] * NCH

    def issue_flat_dma(c, gate_batch):
        w = [(vser, max(v_sig, v_cvx)), (p2rdy, 1)]
        if c >= 2:
            w.append((t_att, rank1_done[c - 2]))
        bufc = c % 2
        p.dma("gpsimd", lambda e, c=c, bufc=bufc: e.dma_start(
            out=sgfl[0:1, bufc * 1024:(bufc + 1) * 1024].rearrange(
                "o (h x) -> o h x", h=16),
            in_=sgm[:, c * CS:(c + 1) * CS]), waits=w, inc=dsg)
        p.dma("gpsimd", lambda e, c=c, bufc=bufc: e.dma_start(
            out=cvfl[0:1, bufc * 1024:(bufc + 1) * 1024],
            in_=CVXsb[c:c + 1, :]), waits=list(w), inc=dcv)
        sgcv_issued[c] = dsg.n

    for c in (0, 1):
        issue_flat_dma(c, None)

    _ps2 = _os.environ.get("S_SINGLE", "0") != "1"
    psS = ([ps("psSa", [128, 1024]), ps("psSb", [128, 1024])] if _ps2
           else [ps("psSa", [128, 1024])] * 2)
    psPT = ps("psPT", [128, 1024], BF16)
    psA = [ps("psA0", [128, 512]), ps("psA1", [128, 512])]
    s_free_at = [vser.n, vser.n]
    a_chunk0 = aser.n
    pt_free_at = 0
    a_free_at = [0, 0]

    for t in range(NCH // 2):
        c0 = 2 * t
        sp = (t % 2) if _ps2 else 0
        nS = 0
        for c2 in (0, 1):
            c = c0 + c2
            for j in range(NB):
                for hp in (0, 1):
                    nS += 1
                    p.step("tensor", lambda e, c=c, c2=c2, j=j, hp=hp, sp=sp:
                           e.matmul(
                        out=psS[sp][hp * 64:(hp + 1) * 64,
                                    c2 * 512 + j * 64:c2 * 512 + (j + 1) * 64],
                        lhsT=Q[hp * 64:(hp + 1) * 64,
                               j * TOK + c * CS:j * TOK + (c + 1) * CS],
                        rhs=K[hp * 64:(hp + 1) * 64,
                              j * TOK + c * CS:j * TOK + (c + 1) * CS],
                        start=True, stop=True,
                        tile_position=(hp * 64, hp * 64)),
                        waits=(([(vser, s_free_at[sp])]
                                + ([(aser, a_chunk0)] if t == 0 else []))
                               if nS == 1 else []),
                        inc=(t_att if nS == 32 else None))
        tS = t_att.n
        p.step("vector", lambda e, sp=sp: e.tensor_tensor(
            out=Sm[:], in0=psS[sp][:], in1=cmask[:], op=ALU.add),
            waits=[(t_att, tS), dcst])
        s_free_at[sp] = vser.n
        p.step("vector", lambda e: e.reduce_max(
            out=mx[:], in_=Sm[:].rearrange("p (g x) -> p g x", g=16),
            axis=AX.X))
        p.step("vector", lambda e: e.tensor_tensor(
            out=Sm[:].rearrange("p (g x) -> p g x", g=16),
            in0=Sm[:].rearrange("p (g x) -> p g x", g=16),
            in1=mx[:].unsqueeze(2).broadcast_to([128, 16, CS]),
            op=ALU.subtract))
        v_sub = vser.n
        p.step("scalar", lambda e: e.activation(
            out=Sm[:], in_=Sm[:], func=AF.Exp, scale=SCALE),
            waits=[(vser, v_sub)])
        p.step("vector", lambda e: e.reduce_sum(
            out=rs[:], in_=Sm[:].rearrange("p (g x) -> p g x", g=16),
            axis=AX.X), waits=[aser])
        p.step("vector", lambda e: e.reciprocal(out=rsi[:], in_=rs[:]))
        p.step("vector", lambda e: e.tensor_tensor(
            out=Pn[:].rearrange("p (g x) -> p g x", g=16),
            in0=Sm[:].rearrange("p (g x) -> p g x", g=16),
            in1=rsi[:].unsqueeze(2).broadcast_to([128, 16, CS]),
            op=ALU.mult))
        v_pn = vser.n
        nT = 0
        for c2 in (0, 1):
            for j in range(NB):
                nT += 1
                p.step("tensor", lambda e, c2=c2, j=j: e.transpose(
                    out=psPT[c2 * 64:(c2 + 1) * 64, j * 128:(j + 1) * 128],
                    in_=Pn[:, c2 * 512 + j * 64:c2 * 512 + (j + 1) * 64],
                    identity=cbf[:, CB_IDN:CB_IDN + 128],
                    tile_position=(0, c2 * 64)),
                    waits=([(vser, max(v_pn, pt_free_at))] if nT == 1 else []),
                    inc=(t_att if nT == 16 else None))
        tT = t_att.n
        p.step("vector", lambda e: e.tensor_copy(out=PTsb[:], in_=psPT[:]),
               waits=[(t_att, tT)])
        pt_free_at = vser.n
        for c2 in (0, 1):
            c = c0 + c2
            first = True
            for j in range(NB):
                for he in (0, 1):
                    p.step("tensor", lambda e, c=c, c2=c2, j=j, he=he: e.matmul(
                        out=psA[c2][he * 64:(he + 1) * 64,
                                    j * 64:(j + 1) * 64],
                        lhsT=V[c2 * 64:(c2 + 1) * 64,
                               (c // 2) * D + j * 128 + he * 64:
                               (c // 2) * D + j * 128 + (he + 1) * 64],
                        rhs=PTsb[c2 * 64:(c2 + 1) * 64,
                                 j * 128 + he * 64:j * 128 + (he + 1) * 64],
                        start=True, stop=False,
                        tile_position=(c2 * 64, he * 64)),
                        waits=([(vser, pt_free_at), (aser, a_free_at[c2]),
                                dsg, dcv] if first else []))
                    first = False
                    p.step("tensor", lambda e, c=c, c2=c2, j=j, he=he: e.matmul(
                        out=psA[c2][he * 64:(he + 1) * 64,
                                    j * 64:(j + 1) * 64],
                        lhsT=cvfl[0:1, (c % 2) * 1024 + j * 128 + he * 64:
                                  (c % 2) * 1024 + j * 128 + (he + 1) * 64],
                        rhs=sgfl[0:1, (c % 2) * 1024 + (2 * j + he) * 64:
                                 (c % 2) * 1024 + (2 * j + he + 1) * 64],
                        start=False, stop=True,
                        tile_position=(0, he * 64)),
                        inc=(t_att if (j == NB - 1 and he == 1) else None))
            rank1_done[c] = t_att.n
            p.step("scalar", lambda e, c=c, c2=c2: e.activation(
                out=ast[:, c2 * 512:(c2 + 1) * 512],
                in_=psA[c2][:], func=AF.Identity),
                waits=[(t_att, rank1_done[c])] + ([da] if t >= 1 else []))
            a_free_at[c2] = aser.n
            p.dma("gpsimd", lambda e, c=c, c2=c2: e.dma_start(
                out=a_d[:, c * CS:(c + 1) * CS].rearrange(
                    "(j p) q -> p j q", p=128),
                in_=ast[:, c2 * 512:(c2 + 1) * 512].rearrange(
                    "p (j q) -> p j q", j=NB)),
                waits=[aser], inc=da)
        for c in (c0 + 2, c0 + 3):
            if c < NCH:
                issue_flat_dma(c, t)
        if _os.environ.get("WQ_OFF", "0") != "1":
            for fn in wq_sched.get(t, []):
                fn()
    if _os.environ.get("WQ_OFF", "0") != "1":
        for fn in wq_tail:
            fn()
    v_wq_done = vser.n
    wq_bar = barrier("wq_bar", [(dqw, dqw.n), (dwl, dwl.n),
                                (vser, v_wq_done), (t_wq, t_wq.n)])
    v_att_done = vser.n
    a_att_done = aser.n
    att_bar = barrier("att_bar", [(vser, v_att_done), (aser, a_att_done),
                                  (da, da.n), dsg, dcv,
                                  (t_att, t_att.n)])

    if "dbg_stats" in dbg_d:
        for idx, tns in enumerate([mcol, sumsq, rstd, sumx, lnt_o, bcol]):
            p.dma("gpsimd", lambda e, idx=idx, tns=tns: e.dma_start(
                out=dbg_d["dbg_stats"][:, idx * 16:(idx + 1) * 16], in_=tns[:]),
                waits=[(aser, act_done_own[NTT - 1]),
                       (vser, dve_done_own[NTT - 1])], inc=dbg_done)
    if "dbg_xhd" in dbg_d:
        p.dma("gpsimd", lambda e: e.dma_start(
            out=dbg_d["dbg_xhd"], in_=xhat_d), waits=[(dxh, 16 * NTT)],
            inc=dbg_done)
    if "dbg_xhat" in dbg_d:
        p.dma("gpsimd", lambda e: e.dma_start(
            out=dbg_d["dbg_xhat"], in_=xhat[:]), waits=[(dtrx, 16 * NB)],
            inc=dbg_done)
    for nm, srct in [("dbg_q", Q), ("dbg_k", K), ("dbg_v", V)]:
        if nm in dbg_d:
            p.dma("gpsimd", lambda e, nm=nm, srct=srct: e.dma_start(
                out=dbg_d[nm], in_=srct[:]),
                waits=[(vser, v_p1_done), (aser, a_p1_done)], inc=dbg_done)
    if "dbg_a" in dbg_d:
        p.dma("gpsimd", lambda e: e.dma_start(
            out=dbg_d["dbg_a"].rearrange("p (j x) -> (j p) x", j=NB), in_=a_d),
            waits=[(att_bar, 1)], inc=dbg_done)
    if "dbg_sig" in dbg_d:
        p.dma("gpsimd", lambda e: e.dma_start(out=dbg_d["dbg_sig"], in_=sgm[:]),
              waits=[(vser, v_sig)], inc=dbg_done)
    if "dbg_cvx" in dbg_d:
        p.dma("gpsimd", lambda e: e.dma_start(out=dbg_d["dbg_cvx"],
                                              in_=CVXsb[:]),
              waits=[(vser, v_cvx)], inc=dbg_done)
    if "dbg_c0k" in dbg_d:
        p.dma("gpsimd", lambda e: e.dma_start(out=dbg_d["dbg_c0k"], in_=c0k[:]),
              waits=[(vser, v_carry_done)], inc=dbg_done)
    if "dbg_c0v" in dbg_d:
        p.dma("gpsimd", lambda e: e.dma_start(out=dbg_d["dbg_c0v"],
                                              in_=c0vrow[:]),
              waits=[(vser, v_carry_done)], inc=dbg_done)


    if _ps2:
        free("psA0", "psA1", "psPT", "psSb", "psSa")
    else:
        free("psA0", "psA1", "psPT", "psSa")
    free("pwq", "wabs", "abuf", "wstg")
    free("ast", "rsi", "rs", "mx", "PTsb", "Pn", "Sm", "cvfl", "sgfl",
         "sgm", "CVXsb", "CVsb", "CKcum", "CK", "cmask_sb")
    free("V", "K", "Q")
    dr2 = S("dr2")
    dxr = S("dxr")
    dal = S("dal")
    dwo = S("dwo")
    dwgl = S("dwgl")
    dwul = S("dwul")
    dx2q = S("dx2q")
    t_wo = S("t_wo")
    Wgq_sb = sb("Wgq_sb", [128, NB * INNER], BF16)
    A_sb = sb("A_sb", [128, NB * TOK], BF16)
    wout_sb = sb("wout_sb", [128, NB * D], BF16)
    xre = sb("xre", [128, 2 * D], F32)
    r2st = sb("r2st", [128, 2 * D], F32)
    xh2 = sb("xh2", [128, D], F32)
    x2qst = sb("x2qst", [128, 2 * D], BF16)
    psW = [ps("psW0", [128, D]), ps("psW1", [128, D])]
    for b in range(NB):
        p.dma("gpsimd", lambda e, b=b: e.dma_start(
            out=A_sb[:, b * TOK:(b + 1) * TOK],
            in_=a_d[b * 128:(b + 1) * 128, :]),
            waits=[(att_bar, 1), (wq_bar, 1)], inc=dal)
        p.dma("gpsimd", lambda e, b=b: e.dma_start(
            out=wout_sb[:, b * D:(b + 1) * D],
            in_=wout[b * 128:(b + 1) * 128, :]),
            waits=[(wq_bar, 1)], inc=dwo)
        p.dma("gpsimd", lambda e, b=b: e.dma_start(
            out=Wgq_sb[:, b * INNER:(b + 1) * INNER],
            in_=wgq_d[b * 128:(b + 1) * 128, :]),
            waits=[(wq_bar, 1)], inc=dwgl)
    r2_evac_v = [0] * NTT
    for tt in range(NTT):
        pb = tt % 2
        p.dma("sync", lambda e, tt=tt, pb=pb: e.dma_start(
            out=xre[:, pb * D:(pb + 1) * D],
            in_=x_own[tt * 128:(tt + 1) * 128, :]),
            waits=([(wq_bar, 1)] if tt < 1 else [(vser, r2_evac_v[tt - 1])]),
            inc=dxr)
        for b in range(NB):
            for hf in range(2):
                p.step("tensor", lambda e, tt=tt, b=b, hf=hf, pb=pb: e.matmul(
                    out=psW[pb][:, hf * 512:(hf + 1) * 512],
                    lhsT=A_sb[:, b * TOK + tt * 128:b * TOK + (tt + 1) * 128],
                    rhs=wout_sb[:, b * D + hf * 512:b * D + (hf + 1) * 512],
                    start=(b == 0), stop=(b == NB - 1)),
                    waits=(([(dal, 16 * NB), (dwo, 16 * NB), (wq_bar, 1)]
                            if tt == 0 else [])
                           + ([(vser, r2_evac_v[tt - 2])]
                              if (tt >= 2 and b == 0 and hf == 0) else [])),
                    inc=(t_wo if (b == NB - 1 and hf == 1) else None))
        p.step("vector", lambda e, tt=tt, pb=pb: e.scalar_tensor_tensor(
            out=r2st[:, pb * D:(pb + 1) * D], in0=psW[pb][:], scalar=1.0,
            in1=xre[:, pb * D:(pb + 1) * D], op0=ALU.mult, op1=ALU.add),
            waits=[(t_wo, tt + 1), (dxr, 16 * (tt + 1)), dr2])
        p.dma("gpsimd", lambda e, tt=tt, pb=pb: e.dma_start(
            out=r2_d[tt * 128:(tt + 1) * 128, :],
            in_=r2st[:, pb * D:(pb + 1) * D]),
            waits=[vser], inc=dr2)
        i = tt
        p.step("scalar", lambda e, i=i, pb=pb: e.activation(
            out=xh2[:], in_=r2st[:, pb * D:(pb + 1) * D], func=AF.Square,
            accum_out=sq2c[:, i:i + 1]), waits=[vser])
        p.step("vector", lambda e, i=i, pb=pb: e.reduce_sum(
            out=m2c[:, i:i + 1], in_=r2st[:, pb * D:(pb + 1) * D],
            axis=AX.X))
        p.step("vector", lambda e, i=i: e.tensor_single_scalar(
            out=m2c[:, i:i + 1], in_=m2c[:, i:i + 1],
            scalar=1.0 / 1024.0, op=ALU.mult))
        p.step("vector", lambda e, i=i: e.tensor_tensor(
            out=msq2c[:, i:i + 1], in0=m2c[:, i:i + 1],
            in1=m2c[:, i:i + 1], op=ALU.mult))
        p.step("vector", lambda e, i=i: e.scalar_tensor_tensor(
            out=sq2c[:, i:i + 1], in0=sq2c[:, i:i + 1],
            scalar=1.0 / 1024.0, in1=msq2c[:, i:i + 1],
            op0=ALU.mult, op1=ALU.subtract), waits=[aser])
        p.step("scalar", lambda e, i=i: e.activation(
            out=ln2c[:, i:i + 1], in_=sq2c[:, i:i + 1], func=AF.Ln,
            bias=1e-5), waits=[vser])
        p.step("scalar", lambda e, i=i: e.activation(
            out=rstd2c[:, i:i + 1], in_=ln2c[:, i:i + 1], func=AF.Exp,
            scale=-0.5))
        p.step("vector", lambda e, i=i: e.scalar_tensor_tensor(
            out=bc2c[:, i:i + 1], in0=m2c[:, i:i + 1], scalar=-1.0,
            in1=rstd2c[:, i:i + 1], op0=ALU.mult, op1=ALU.mult),
            waits=[aser])
        p.step("vector", lambda e, i=i, pb=pb: e.tensor_scalar(
            out=xh2[:], in0=r2st[:, pb * D:(pb + 1) * D],
            scalar1=rstd2c[:, i:i + 1], scalar2=bc2c[:, i:i + 1],
            op0=ALU.mult, op1=ALU.add), waits=[aser])
        p.step("vector", lambda e, i=i: e.reduce_max(
            out=amc[:, i:i + 1], in_=xh2[:], axis=AX.X,
            apply_absolute_value=True))
        p.step("vector", lambda e, i=i: e.tensor_single_scalar(
            out=amc[:, i:i + 1], in_=amc[:, i:i + 1], scalar=1e-5,
            op=ALU.max))
        p.step("vector", lambda e, i=i: e.reciprocal(
            out=s1c[:, i:i + 1], in_=amc[:, i:i + 1]))
        p.step("vector", lambda e, i=i: e.tensor_single_scalar(
            out=s1c[:, i:i + 1], in_=s1c[:, i:i + 1], scalar=127.0,
            op=ALU.mult))
        p.step("vector", lambda e, i=i: e.tensor_single_scalar(
            out=rs1c[:, i:i + 1], in_=amc[:, i:i + 1],
            scalar=1.0 / 127.0, op=ALU.mult))
        p.step("vector", lambda e, i=i: e.tensor_tensor(
            out=gscc[:, i:i + 1], in0=rs1c[:, i:i + 1],
            in1=wmcol[:, 0:1], op=ALU.mult))
        p.step("vector", lambda e, i=i: e.tensor_tensor(
            out=uscc[:, i:i + 1], in0=rs1c[:, i:i + 1],
            in1=wmcol[:, 1:2], op=ALU.mult))
        p.step("vector", lambda e, i=i: e.tensor_scalar(
            out=xh2[:], in0=xh2[:], scalar1=s1c[:, i:i + 1],
            scalar2=mgc[:], op0=ALU.mult, op1=ALU.add))
        p.step("vector", lambda e, i=i, pb=pb: e.tensor_scalar(
            out=x2qst[:, pb * D:(pb + 1) * D], in0=xh2[:],
            scalar1=mgc[:], scalar2=None, op0=ALU.subtract),
            waits=[dx2q])
        r2_evac_v[tt] = vser.n
        p.dma("gpsimd", lambda e, i=i, pb=pb: e.dma_start(
            out=x2q_d[i * 128:(i + 1) * 128, :],
            in_=x2qst[:, pb * D:(pb + 1) * D]),
            waits=[vser], inc=dx2q)
    p3_bar = barrier("p3_bar", [(dr2, 16 * NTT), (dx2q, 16 * NTT),
                                (t_wo, NTT), (vser, vser.n), (aser, aser.n)])
    free("psW0", "psW1", "x2qst", "xh2", "r2st", "xre", "wout_sb",
         "A_sb")

    dtx2 = S("dtx2")
    dhq = S("dhq")
    t_gu = S("t_gu")
    Wuq_sb = sb("Wuq_sb", [128, NB * INNER], BF16)
    for b in range(NB):
        p.dma("gpsimd", lambda e, b=b: e.dma_start(
            out=Wuq_sb[:, b * INNER:(b + 1) * INNER],
            in_=wuq_d[b * 128:(b + 1) * 128, :]),
            waits=[(p3_bar, 1)], inc=dwul)
    X2qT = sb("X2qT", [128, 2 * NB * 512], BF16)
    Hh = sb("Hh", [128, 5 * INNER], BF16)
    gbuf = sb("gbuf", [128, 3 * 512], BF16)
    hqs = sb("hqs", [128, D], F32)
    psG = [ps("psG0", [128, 512]), ps("psG1", [128, 512]),
           ps("psG2", [128, 512])]
    psU = [ps("psU0", [128, 512]), ps("psU1", [128, 512]),
           ps("psU2", [128, 512])]
    n_gu = 0
    gu_evac_done = [0] * 128
    aq2_done_v = [0] * NTT

    dtx2_done = [0] * 4

    def emit_transposes(q):
        qp = q % 2
        for b in range(NB):
            p.dma("sync", lambda e, q=q, b=b, qp=qp: e.dma_start(
                out=X2qT[:, qp * NB * 512 + b * 512:
                         qp * NB * 512 + (b + 1) * 512],
                in_=x2q_d[q * 512:(q + 1) * 512, b * 128:(b + 1) * 128],
                transpose=True),
                waits=[(p3_bar, 1)]
                      + ([(dtx2, 16 * NB * q)] if (q >= 1 and b == 0) else [])
                      + ([(t_gu, 32 * (q - 1))] if q >= 2 else []),
                inc=dtx2)
        dtx2_done[q] = dtx2.n

    emit_transposes(0)
    for i in range(NTT):
        q, t4 = i // 4, i % 4
        qp = q % 2
        if t4 == 0 and q + 1 < 4:
            emit_transposes(q + 1)
        for ic in range(8):
            pb = n_gu % (3 if _os.environ.get("GU2", "0") != "1" else 2)
            for b in range(NB):
                p.step("tensor", lambda e, qp=qp, t4=t4, ic=ic, b=b, pb=pb:
                       e.matmul(
                    out=psG[pb][:],
                    lhsT=X2qT[:, qp * NB * 512 + b * 512 + t4 * 128:
                              qp * NB * 512 + b * 512 + (t4 + 1) * 128],
                    rhs=Wgq_sb[:, b * INNER + ic * 512:
                               b * INNER + (ic + 1) * 512],
                    start=(b == 0), stop=(b == NB - 1)),
                    waits=(([(dtx2, dtx2_done[q]),
                             (dwgl, 16 * NB), (dwul, 16 * NB)]
                            if b == 0 and ic == 0 and t4 == 0 else [])
                           + ([(vser, gu_evac_done[n_gu - 3])]
                              if (n_gu >= 3 and b == 0) else [])))
                p.step("tensor", lambda e, qp=qp, t4=t4, ic=ic, b=b, pb=pb:
                       e.matmul(
                    out=psU[pb][:],
                    lhsT=X2qT[:, qp * NB * 512 + b * 512 + t4 * 128:
                              qp * NB * 512 + b * 512 + (t4 + 1) * 128],
                    rhs=Wuq_sb[:, b * INNER + ic * 512:
                               b * INNER + (ic + 1) * 512],
                    start=(b == 0), stop=(b == NB - 1)),
                    inc=(t_gu if b == NB - 1 else None))
            p.step("scalar", lambda e, i=i, ic=ic, pb=pb: e.activation(
                out=gbuf[:, pb * 512:(pb + 1) * 512], in_=psG[pb][:],
                func=AF.Silu, scale=gscc[:, i:i + 1]),
                waits=[(t_gu, t_gu.n)])
            p.step("vector", lambda e, i=i, ic=ic, pb=pb:
                   e.scalar_tensor_tensor(
                out=Hh[:, (i % 5) * INNER + ic * 512:
                       (i % 5) * INNER + (ic + 1) * 512],
                in0=psU[pb][:], scalar=uscc[:, i:i + 1],
                in1=gbuf[:, pb * 512:(pb + 1) * 512],
                op0=ALU.mult, op1=ALU.mult),
                waits=[aser] + ([dhq] if ic == 0 else []))
            gu_evac_done[n_gu] = vser.n
            n_gu += 1
        sl = (i % 5) * INNER
        p.step("vector", lambda e, sl=sl, i=i: e.reduce_max(
            out=amc[:, i:i + 1], in_=Hh[:, sl:sl + INNER], axis=AX.X,
            apply_absolute_value=True))
        p.step("vector", lambda e, i=i: e.tensor_single_scalar(
            out=amc[:, i:i + 1], in_=amc[:, i:i + 1], scalar=1e-5,
            op=ALU.max))
        p.step("vector", lambda e, i=i: e.reciprocal(
            out=s2c[:, i:i + 1], in_=amc[:, i:i + 1]))
        p.step("vector", lambda e, i=i: e.tensor_single_scalar(
            out=s2c[:, i:i + 1], in_=s2c[:, i:i + 1], scalar=127.0,
            op=ALU.mult))
        p.step("vector", lambda e, i=i: e.tensor_single_scalar(
            out=dscc[:, i:i + 1], in_=amc[:, i:i + 1],
            scalar=1.0 / 127.0, op=ALU.mult))
        p.step("vector", lambda e, i=i: e.tensor_tensor(
            out=dscc[:, i:i + 1], in0=dscc[:, i:i + 1],
            in1=wmcol[:, 2:3], op=ALU.mult))
        for hqc in range(4):
            p.step("vector", lambda e, sl=sl, i=i, hqc=hqc: e.tensor_scalar(
                out=hqs[:], in0=Hh[:, sl + hqc * D:sl + (hqc + 1) * D],
                scalar1=s2c[:, i:i + 1], scalar2=mgc[:],
                op0=ALU.mult, op1=ALU.add))
            p.step("vector", lambda e, sl=sl, i=i, hqc=hqc: e.tensor_scalar(
                out=Hh[:, sl + hqc * D:sl + (hqc + 1) * D], in0=hqs[:],
                scalar1=mgc[:], scalar2=None, op0=ALU.subtract))
        aq2_done_v[i] = vser.n
        p.dma("gpsimd", lambda e, sl=sl, i=i: e.dma_start(
            out=hq_d[i * 128:(i + 1) * 128, :],
            in_=Hh[:, sl:sl + INNER]),
            waits=[vser], inc=dhq)
    p5a_bar = barrier("p5a_bar", [(dhq, 16 * NTT), (vser, vser.n),
                                  (aser, aser.n), (t_gu, t_gu.n)])
    free("psU2", "psU1", "psU0", "psG2", "psG1", "psG0")
    free("hqs", "gbuf", "Hh", "X2qT", "Wuq_sb", "Wgq_sb")

    dwdl = S("dwdl")
    dhql = S("dhql")
    dr2b = S("dr2b")
    dyo = S("dyo")
    t_dn = S("t_dn")
    Wdq_sb = sb("Wdq_sb", [128, NIC * D], BF16)
    HqT = sb("HqT", [128, NIC * 512], BF16)
    r2b = sb("r2b", [128, 2 * D], F32)
    yst = sb("yst", [128, 2 * D], F32)
    psD = [ps("psD0", [128, D]), ps("psD1", [128, D])]
    for t in range(NB):
        p.dma("gpsimd", lambda e, t=t: e.dma_start(
            out=Wdq_sb[:, t * 4 * D:(t + 1) * 4 * D].rearrange(
                "p (a f) -> p a f", a=4),
            in_=wdq_d[t * 512:(t + 1) * 512, :].rearrange(
                "(a p) f -> p a f", p=128)),
            waits=[(p5a_bar, 1)], inc=dwdl)
    y_evac_v = [0] * NTT
    for q in range(4):
        for b32 in range(NIC):
            p.dma("sync", lambda e, q=q, b32=b32: e.dma_start(
                out=HqT[:, b32 * 512:(b32 + 1) * 512],
                in_=hq_d[q * 512:(q + 1) * 512, b32 * 128:(b32 + 1) * 128],
                transpose=True),
                waits=[(p5a_bar, 1)] + ([(t_dn, t_dn.n)] if q >= 1 else []),
                inc=dtx2)
        for t4 in range(4):
            i = q * 4 + t4
            pb = i % 2
            p.dma("sync", lambda e, i=i, pb=pb: e.dma_start(
                out=r2b[:, pb * D:(pb + 1) * D],
                in_=r2_d[i * 128:(i + 1) * 128, :]),
                waits=[(p5a_bar, 1)] + ([(vser, y_evac_v[i - 1])]
                                        if i >= 1 else []), inc=dr2b)
            for b32 in range(NIC):
                for hf in range(2):
                    p.step("tensor", lambda e, q=q, t4=t4, b32=b32, hf=hf,
                           pb=pb: e.matmul(
                        out=psD[pb][:, hf * 512:(hf + 1) * 512],
                        lhsT=HqT[:, b32 * 512 + t4 * 128:
                                 b32 * 512 + (t4 + 1) * 128],
                        rhs=Wdq_sb[:, b32 * D + hf * 512:
                                   b32 * D + (hf + 1) * 512],
                        start=(b32 == 0), stop=(b32 == NIC - 1)),
                        waits=(([(dtx2, dtx2.n), (dwdl, 16 * NB)]
                                if t4 == 0 and b32 == 0 and hf == 0 else [])
                               + ([(vser, y_evac_v[i - 2])]
                                  if (i >= 2 and b32 == 0 and hf == 0)
                                  else [])),
                        inc=(t_dn if (b32 == NIC - 1 and hf == 1) else None))
            p.step("vector", lambda e, i=i, pb=pb: e.scalar_tensor_tensor(
                out=yst[:, pb * D:(pb + 1) * D], in0=psD[pb][:],
                scalar=dscc[:, i:i + 1], in1=r2b[:, pb * D:(pb + 1) * D],
                op0=ALU.mult, op1=ALU.add),
                waits=[(t_dn, i + 1), (dr2b, 16 * (i + 1)), dyo])
            y_evac_v[i] = vser.n
            p.dma("gpsimd", lambda e, i=i, pb=pb: e.dma_start(
                out=y[i * 128:(i + 1) * 128, :],
                in_=yst[:, pb * D:(pb + 1) * D]),
                waits=[vser], inc=dyo)

    p.step("sync", None, waits=[dbg_done, dxh, dwq, dtrx, dxo, dxp,
                                dsg, dcv, (dyo, 16 * NTT), dwl, dqw, dal,
                                dwo, dxr, dr2, dwgl, dwul, dx2q,
                                dtx2, dhq, dwdl, dr2b])

    block = es.enter_context(nc.Block())
    p.emit(block)
    es.close()
    free_all()
    return nc


def make_inputs(inputs):
    x = np.ascontiguousarray(np.asarray(inputs["L0_attn_x"], dtype=np.float32)
                             .reshape(4 * 4096, D))
    wqkv_a = np.ascontiguousarray(np.asarray(inputs["L0_attn_W_qkv"], np.float32))
    wout_a = np.ascontiguousarray(np.asarray(inputs["L0_attn_W_out"], np.float32))
    wgT = np.ascontiguousarray(np.asarray(inputs["L0_ffn_w_gate"], np.float32).T)
    wuT = np.ascontiguousarray(np.asarray(inputs["L0_ffn_w_up"], np.float32).T)
    wdT = np.ascontiguousarray(np.asarray(inputs["L0_ffn_w_down"], np.float32).T)
    cst, cbf, cmask = host_consts()
    crow = np.ones((1, 128), dtype=np.float32)
    zeros = np.zeros((TOK, D), dtype=np.float32)
    in_maps = []
    for c in range(8):
        xo = x[c * TOK:(c + 1) * TOK]
        xp = zeros if c % 2 == 0 else x[(c - 1) * TOK:c * TOK]
        in_maps.append(dict(x_own=np.ascontiguousarray(xo),
                            x_prev=np.ascontiguousarray(xp),
                            wqkv=wqkv_a, wout=wout_a, wgT=wgT, wuT=wuT,
                            wdT=wdT, cst=cst, cbf=cbf, cmask=cmask,
                            crow=crow))
    return in_maps


def kernel(**inputs):
    in_maps = make_inputs(inputs)
    nc = build_program()
    res = run_bass_kernel_spmd(nc, in_maps, core_ids=list(range(8)))
    out = np.concatenate([r["y"] for r in res.results], axis=0)
    return out.reshape(4, 4096, D)


# version 09
# speedup vs baseline: 1.6081x; 1.0239x over previous
import os as _os
import numpy as np
import ml_dtypes
import concourse.bass as bass
import concourse.mybir as mybir
from concourse.bass_utils import run_bass_kernel_spmd

F32 = mybir.dt.float32
BF16 = mybir.dt.bfloat16
I32 = mybir.dt.int32
ALU = mybir.AluOpType
AF = mybir.ActivationFunctionType
AX = mybir.AxisListType

TOK = 2048
D = 1024
NB = 8
NTT = TOK // 128
CS = 64
NCH = TOK // CS
H = 16
HD = 64
SCALE = HD ** -0.5
INNER = 4096
NIC = INNER // 128
NEG = -65000.0

CONSTS = [0.0, 1.0, 0.75, 0.5, 1e-5, 1.0 / 1024.0, -0.5, SCALE,
          -SCALE / 64.0, 127.0, 256.0, -256.0, 1.0 / 128.0, -1.0,
          9.313225746154785e-10, 12582912.0, 6755399441055744.0]


class Sem:
    def __init__(self, es, nc, name):
        self.h = es.enter_context(nc.semaphore(name))
        self.n = 0


class Prog:

    def __init__(self):
        self.steps = []
        self.ser = {}
        self.selfwait = {"vector": True, "gpsimd": True, "scalar": True}

    def set_serial(self, eng, sem):
        self.ser[eng] = sem

    def step(self, eng, fn, waits=(), inc=None, inck=1, is_dma=False):
        ws = []
        for w in waits:
            if isinstance(w, Sem):
                ws.append((w, w.n))
            else:
                ws.append((w[0], w[1]))
        auto = None
        ser = None
        if not is_dma and eng in self.ser:
            s = self.ser[eng]
            if self.selfwait.get(eng):
                auto = (s, s.n)
            s.n += 1
            ser = s
        self.steps.append((eng, fn, ws, inc, inck, auto, ser))
        if inc is not None:
            inc.n += inck

    def dma(self, eng, fn, waits=(), inc=None):
        self.step(eng, fn, waits=waits, inc=inc, inck=16, is_dma=True)

    def emit(self, block):
        def mk(engname):
            def body(eng):
                for (e, fn, ws, inc, inck, auto, ser) in self.steps:
                    if e != engname:
                        continue
                    if auto is not None and auto[1] > 0:
                        eng.wait_ge(auto[0].h, auto[1])
                    for (s, v) in ws:
                        if v > 0:
                            eng.wait_ge(s.h, v)
                    inst = fn(eng) if fn is not None else None
                    if inst is not None:
                        if ser is not None:
                            inst.then_inc(ser.h, 1)
                        if inc is not None:
                            inst.then_inc(inc.h, inck)
            return body

        block.sync(mk("sync"))
        block.vector(mk("vector"))
        block.scalar(mk("scalar"))
        block.tensor(mk("tensor"))
        block.gpsimd(mk("gpsimd"))


def host_consts():
    cst = np.tile(np.array(CONSTS, dtype=np.float32), (128, 1))
    idn = np.eye(128, dtype=np.float32).astype(ml_dtypes.bfloat16)
    lind = np.zeros((128, 8 * 16), dtype=ml_dtypes.bfloat16)
    for j in range(8):
        lind[0:64, 16 * j + 2 * j] = 1.0
        lind[64:128, 16 * j + 2 * j + 1] = 1.0
    cvind = np.zeros((128, 16 * 32), dtype=ml_dtypes.bfloat16)
    for tt in range(16):
        cvind[0:64, 32 * tt + 2 * tt] = 1.0 / 128.0
        cvind[64:128, 32 * tt + 2 * tt + 1] = 1.0 / 128.0
    tri = np.zeros((128, 32), dtype=ml_dtypes.bfloat16)
    tri[0:32, :] = np.tril(np.ones((32, 32)), -1).T.astype(ml_dtypes.bfloat16)
    ones_col = np.ones((128, 1), dtype=ml_dtypes.bfloat16)
    ones_row32 = np.zeros((128, 32), dtype=ml_dtypes.bfloat16)
    ones_row32[0, :] = 1.0
    ones_row128 = np.zeros((128, 128), dtype=ml_dtypes.bfloat16)
    ones_row128[0, :] = 1.0
    mstk = np.zeros((128, 128), dtype=ml_dtypes.bfloat16)
    mstk[0:64, 0:64] = np.eye(64)
    mstk[0:64, 64:128] = np.eye(64)
    m64b = np.zeros((128, 64), dtype=ml_dtypes.bfloat16)
    m64b[0:64, :] = np.where(np.triu(np.ones((64, 64), dtype=bool), 1),
                             NEG, 0.0)
    cbf = np.concatenate([idn, lind, cvind, tri, ones_col, ones_row32,
                          ones_row128, mstk, m64b], axis=1)
    m64 = np.where(np.triu(np.ones((64, 64), dtype=bool), 1), NEG, 0.0)
    blk = np.concatenate([m64, m64], axis=0)
    cmask = np.tile(blk, (1, 16)).astype(np.float32)
    return cst, cbf, cmask


CB_IDN = 0
CB_LIND = 128
CB_CVIND = 256
CB_TRI = 768
CB_ONESC = 800
CB_ONESR32 = 801
CB_ONESR128 = 833
CB_MSTK = 961
CB_M64 = 1089


def build_program(dbg=()):
    nc = bass.Bass("TRN2", target_bir_lowering=False, debug=False)

    x_own = nc.dram_tensor("x_own", [TOK, D], F32, kind="ExternalInput").ap()
    x_prev = nc.dram_tensor("x_prev", [TOK, D], F32, kind="ExternalInput").ap()
    wqkv = nc.dram_tensor("wqkv", [D, 3 * D], F32, kind="ExternalInput").ap()
    wout = nc.dram_tensor("wout", [D, D], F32, kind="ExternalInput").ap()
    wgT = nc.dram_tensor("wgT", [D, INNER], F32, kind="ExternalInput").ap()
    wuT = nc.dram_tensor("wuT", [D, INNER], F32, kind="ExternalInput").ap()
    wdT = nc.dram_tensor("wdT", [INNER, D], F32, kind="ExternalInput").ap()
    cst_d = nc.dram_tensor("cst", [128, len(CONSTS)], F32, kind="ExternalInput").ap()
    cmask_d = nc.dram_tensor("cmask", [128, 1024], F32, kind="ExternalInput").ap()
    cbf_d = nc.dram_tensor("cbf", [128, 1153], BF16, kind="ExternalInput").ap()

    y = nc.dram_tensor("y", [TOK, D], F32, kind="ExternalOutput").ap()

    xhat_d = nc.dram_tensor("xhat_d", [TOK, D], BF16, kind="Internal").ap()
    a_d = nc.dram_tensor("a_d", [D, TOK], BF16, kind="Internal").ap()
    x2q_d = nc.dram_tensor("x2q_d", [TOK, D], BF16, kind="Internal").ap()
    hq_d = nc.dram_tensor("hq_d", [TOK, INNER], BF16, kind="Internal").ap()
    wgq_d = nc.dram_tensor("wgq_d", [D, INNER], BF16, kind="Internal").ap()
    wuq_d = nc.dram_tensor("wuq_d", [D, INNER], BF16, kind="Internal").ap()
    wdq_d = nc.dram_tensor("wdq_d", [INNER, D], BF16, kind="Internal").ap()
    r2_d = nc.dram_tensor("r2_d", [TOK, D], F32, kind="Internal").ap()
    crow_d = nc.dram_tensor("crow", [1, 128], F32, kind="ExternalInput").ap()

    dbg_d = {}
    for name, shape, dt in [
        ("dbg_xhat", [128, 16384], F32), ("dbg_q", [128, 16384], F32),
        ("dbg_k", [128, 16384], F32), ("dbg_v", [128, 16384], F32),
        ("dbg_c0k", [128, 8], F32), ("dbg_c0v", [1, D], F32),
        ("dbg_a", [128, 16384], F32), ("dbg_sig", [16, TOK], F32),
        ("dbg_cvx", [32, D], F32), ("dbg_stats", [128, 96], F32),
        ("dbg_xhd", [TOK, D], BF16),
    ]:
        if name in dbg:
            dbg_d[name] = nc.dram_tensor(name, shape, dt, kind="ExternalOutput").ap()

    import contextlib
    es = contextlib.ExitStack()
    galloc = []

    def _alloc(g, name):
        galloc.append((name, g))
        return g.__enter__()

    def sb(name, shape, dt):
        return _alloc(nc.sbuf_tensor(name, shape, dt), name)

    def ps(name, shape, dt=F32):
        return _alloc(nc.psum_tensor(name, shape, dt), name)

    def free(*names):
        want = set(names)
        for _ in names:
            nm, g = galloc.pop()
            assert nm in want, f"non-LIFO free: {nm} vs {want}"
            g.__exit__(None, None, None)

    def free_all():
        while galloc:
            galloc.pop()[1].__exit__(None, None, None)

    p = Prog()
    sems = {}

    def S(name):
        if name not in sems:
            sems[name] = Sem(es, nc, name)
        return sems[name]

    def barrier(nm, waits):
        bsem = S(nm)
        p.step("sync", lambda e: e.nop(), waits=waits, inc=bsem, is_dma=True,
               inck=1)
        return bsem

    vser = S("vser")
    aser = S("aser")
    pser = S("pser")
    p.set_serial("vector", vser)
    p.set_serial("scalar", aser)
    p.set_serial("gpsimd", pser)

    cst = sb("cst_sb", [128, len(CONSTS)], F32)
    for i, v in enumerate(CONSTS):
        nc.const_aps.register(F32, v, cst[:, i:i + 1])
    cbf = sb("cbf_sb", [128, 1153], BF16)

    sxcol = sb("sxcol", [128, 8], BF16)
    sxcol05 = sb("sxcol05", [128, 8], BF16)
    c0k = sb("c0k", [128, 8], F32)
    c0vrow = sb("c0vrow", [1, D], BF16)
    bcast_b = sb("bcast_b", [128, 1], F32)
    det_i = sb("det_i", [128, 1], I32)

    sumx = sb("sumx", [128, NTT], F32)
    sumsq = sb("sumsq", [128, NTT], F32)
    mcol = sb("mcol", [128, NTT], F32)
    rstd = sb("rstd", [128, NTT], F32)
    bcol = sb("bcol", [128, NTT], F32)
    sumxp = sb("sumxp", [128, NTT], F32)
    sumsqp = sb("sumsqp", [128, NTT], F32)
    mcolp = sb("mcolp", [128, NTT], F32)
    rstdp = sb("rstdp", [128, NTT], F32)
    rstdpb = sb("rstdpb", [128, NTT], BF16)
    rmb = sb("rmb", [128, NTT], BF16)
    msqt_o = sb("msqt_o", [128, NTT], F32)
    lnt_o = sb("lnt_o", [128, NTT], F32)
    msqt_p = sb("msqt_p", [128, NTT], F32)
    lnt_p = sb("lnt_p", [128, NTT], F32)
    scsb = sb("scsb", [1, 1], F32)
    sxrow = sb("sxrow", [1, D], BF16)
    crow = sb("crow_sb", [1, 128], F32)
    shiftb = sb("shiftb", [128, 1], F32)
    mgc = sb("mgc", [128, 1], F32)
    nmgc = sb("nmgc", [128, 1], F32)
    wmcol = sb("wmcol", [128, 3], F32)
    thrcol = sb("thrcol", [128, 3], F32)
    nthrcol = sb("nthrcol", [128, 3], F32)
    wsc = sb("wsc", [1, 8], F32)
    m2c = sb("m2c", [128, NTT], F32)
    sq2c = sb("sq2c", [128, NTT], F32)
    msq2c = sb("msq2c", [128, NTT], F32)
    ln2c = sb("ln2c", [128, NTT], F32)
    rstd2c = sb("rstd2c", [128, NTT], F32)
    bc2c = sb("bc2c", [128, NTT], F32)
    s1c = sb("s1c", [128, NTT], F32)
    rs1c = sb("rs1c", [128, NTT], F32)
    s2c = sb("s2c", [128, NTT], F32)
    gscc = sb("gscc", [128, NTT], F32)
    uscc = sb("uscc", [128, NTT], F32)
    dscc = sb("dscc", [128, NTT], F32)
    amc = sb("amc", [128, NTT], F32)

    Q = sb("Q", [128, NB * TOK], BF16)
    K = sb("K", [128, NB * TOK], BF16)
    V = sb("V", [128, NTT * D], BF16)
    xhat = sb("xhat", [128, NB * TOK], BF16)
    wqk_sb = sb("wqk_sb", [128, NB * 2048], BF16)
    wv_sb = sb("wv_sb", [128, NB * D], BF16)
    xin = sb("xin", [128, D], F32)
    xpin = sb("xpin", [128, D], F32)
    xpbf = sb("xpbf", [128, D], BF16)
    xhst = sb("xhst", [128, 2 * D], BF16)
    sqs = sb("sqs", [128, D], F32)

    dxo = S("dxo")
    dxp = S("dxp")
    dcst = S("dcst")
    dxh = S("dxh")
    dtrx = S("dtrx")
    dwq = S("dwq")
    t_sx = S("t_sx")
    t_c0 = S("t_c0")
    t_v = S("t_v")
    t_qk = S("t_qk")
    dbg_done = S("dbg_done")

    p.dma("sync", lambda e: e.dma_start(out=cst[:], in_=cst_d), inc=dcst)
    p.dma("sync", lambda e: e.dma_start(out=cbf[:], in_=cbf_d), inc=dcst)
    p.dma("sync", lambda e: e.dma_start(out=crow[:], in_=crow_d), inc=dcst)
    p.step("vector", lambda e: e.tensor_single_scalar(
        out=det_i[:], in_=cst[:, 2:3], scalar=0.0, op=ALU.add),
        waits=[dcst])
    p.step("vector", lambda e: e.tensor_scalar(
        out=bcast_b[:], in0=det_i[:], scalar1=-0.5, scalar2=0.5,
        op0=ALU.mult, op1=ALU.add))
    p.step("vector", lambda e: e.tensor_single_scalar(
        out=shiftb[:], in_=bcast_b[:], scalar=256.0, op=ALU.add))
    p.step("vector", lambda e: e.tensor_single_scalar(
        out=mgc[:], in_=cst[:, 1:2], scalar=9.313225746154785e-10,
        op=ALU.add), waits=[dcst])
    p.step("vector", lambda e: e.tensor_scalar(
        out=mgc[:], in0=mgc[:], scalar1=1.0, scalar2=1073741824.0,
        op0=ALU.subtract, op1=ALU.mult))
    p.step("vector", lambda e: e.tensor_scalar(
        out=mgc[:], in0=mgc[:],
        scalar1=6755399441055744.0 - 12582912.0, scalar2=12582912.0,
        op0=ALU.mult, op1=ALU.add))
    p.step("vector", lambda e: e.tensor_single_scalar(
        out=nmgc[:], in_=mgc[:], scalar=-1.0, op=ALU.mult))

    for b in range(NB):
        p.dma("gpsimd", lambda e, b=b: e.dma_start(
            out=wqk_sb[:, b * 2048:(b + 1) * 2048],
            in_=wqkv[b * 128:(b + 1) * 128, 0:2048]), inc=dwq)
        p.dma("gpsimd", lambda e, b=b: e.dma_start(
            out=wv_sb[:, b * D:(b + 1) * D],
            in_=wqkv[b * 128:(b + 1) * 128, 2048:3072]), inc=dwq)

    psx = ps("psx", [1, D])
    psc = ps("psc", [1, 1])
    dve_done_own = [0] * NTT
    dve_done_prev = [0] * NTT
    act_done_own = [0] * NTT
    act_done_prev = [0] * NTT
    pe_done_prev = [0] * NTT

    def ln_tile(i, src_d, buf, dsem, sumx_, sumsq_, m_, rstd_, msqt, lnt,
                dma_gate):
        bi = 0
        p.dma("sync", lambda e: e.dma_start(
            out=buf[:, bi:bi + D], in_=src_d[i * 128:(i + 1) * 128, :]),
            waits=dma_gate(i), inc=dsem)
        p.step("scalar", lambda e: e.activation(
            out=sqs[:], in_=buf[:, bi:bi + D], func=AF.Square,
            accum_out=sumsq_[:, i:i + 1]),
            waits=[(dsem, 16 * (i + 1)), dcst])
        p.step("vector", lambda e: e.reduce_sum(
            out=sumx_[:, i:i + 1], in_=buf[:, bi:bi + D], axis=AX.X),
            waits=[(dsem, 16 * (i + 1))])
        p.step("vector", lambda e: e.tensor_single_scalar(
            out=m_[:, i:i + 1], in_=sumx_[:, i:i + 1],
            scalar=1.0 / 1024.0, op=ALU.mult))
        p.step("vector", lambda e: e.tensor_tensor(
            out=msqt[:, i:i + 1], in0=m_[:, i:i + 1], in1=m_[:, i:i + 1],
            op=ALU.mult))
        p.step("vector", lambda e: e.scalar_tensor_tensor(
            out=sumsq_[:, i:i + 1], in0=sumsq_[:, i:i + 1],
            scalar=1.0 / 1024.0, in1=msqt[:, i:i + 1],
            op0=ALU.mult, op1=ALU.subtract), waits=[aser])
        p.step("scalar", lambda e: e.activation(
            out=lnt[:, i:i + 1], in_=sumsq_[:, i:i + 1], func=AF.Ln,
            bias=1e-5), waits=[vser])
        p.step("scalar", lambda e: e.activation(
            out=rstd_[:, i:i + 1], in_=lnt[:, i:i + 1], func=AF.Exp,
            scale=-0.5))

    for i in range(NTT):
        ln_tile(i, x_own, xin, dxo, sumx, sumsq, mcol, rstd, msqt_o, lnt_o,
                lambda j: [] if j < 1 else [(vser, dve_done_own[j - 1]),
                                            (aser, act_done_own[j - 1])])
        p.step("vector", lambda e, i=i: e.scalar_tensor_tensor(
            out=bcol[:, i:i + 1], in0=mcol[:, i:i + 1], scalar=-1.0,
            in1=rstd[:, i:i + 1], op0=ALU.mult, op1=ALU.mult),
            waits=[aser])
        dve_done_own[i] = vser.n
        p.step("scalar", lambda e, i=i, bi=(i % 2) * D: e.activation(
            out=xhst[:, bi:bi + D], in_=xin[:, 0:D], func=AF.Identity,
            scale=rstd[:, i:i + 1], bias=bcol[:, i:i + 1]),
            waits=[vser] + ([] if i < 2 else [(dxh, 16 * i)]))
        act_done_own[i] = aser.n
        p.dma("gpsimd", lambda e, i=i, bi=(i % 2) * D: e.dma_start(
            out=xhat_d[i * 128:(i + 1) * 128, :], in_=xhst[:, bi:bi + D]),
            waits=[(aser, aser.n)], inc=dxh)

        ln_tile(i, x_prev, xpin, dxp, sumxp, sumsqp, mcolp, rstdp, msqt_p,
                lnt_p,
                lambda j: [] if j < 1 else [(vser, dve_done_prev[j - 1]),
                                            (aser, act_done_prev[j - 1])])
        p.step("vector", lambda e, i=i: e.tensor_single_scalar(
            out=rstdpb[:, i:i + 1], in_=rstdp[:, i:i + 1], scalar=0.0,
            op=ALU.add), waits=[aser])
        p.step("vector", lambda e, i=i: e.scalar_tensor_tensor(
            out=rmb[:, i:i + 1], in0=mcolp[:, i:i + 1], scalar=1.0,
            in1=rstdp[:, i:i + 1], op0=ALU.mult, op1=ALU.mult))
        dve_done_prev[i] = vser.n
        p.step("scalar", lambda e, i=i: e.activation(
            out=xpbf[:, 0:D], in_=xpin[:, 0:D], func=AF.Identity),
            waits=([] if i < 1 else [(t_sx, pe_done_prev[i - 1])]))
        act_done_prev[i] = aser.n
        for hf in range(2):
            p.step("tensor", lambda e, i=i, hf=hf: e.matmul(
                out=psx[:, hf * 512:(hf + 1) * 512],
                lhsT=rstdpb[:, i:i + 1],
                rhs=xpbf[:, hf * 512:(hf + 1) * 512],
                start=(i == 0), stop=(i == NTT - 1)),
                waits=[vser, aser])
        p.step("tensor", lambda e, i=i: e.matmul(
            out=psc[:], lhsT=rmb[:, i:i + 1],
            rhs=cbf[:, CB_ONESC:CB_ONESC + 1],
            start=(i == 0), stop=(i == NTT - 1)),
            waits=[(dcst, 48)], inc=t_sx)
        pe_done_prev[i] = t_sx.n

    for b in range(NB):
        p.dma("sync", lambda e, b=b: e.dma_start(
            out=xhat[:, b * TOK:(b + 1) * TOK],
            in_=xhat_d[:, b * 128:(b + 1) * 128], transpose=True),
            waits=[(dxh, 16 * NTT)], inc=dtrx)

    p.step("vector", lambda e: e.tensor_copy(out=scsb[:], in_=psc[:]),
           waits=[(t_sx, NTT)])
    p.step("vector", lambda e: e.tensor_scalar(
        out=sxrow[:], in0=psx[:], scalar1=scsb[0:1, 0:1], scalar2=None,
        op0=ALU.subtract))
    v_after_sxrow = vser.n
    ptc = ps("ptc", [128, 16], BF16)
    for b in range(NB):
        p.step("tensor", lambda e, b=b: e.transpose(
            out=ptc[:, 2 * b:2 * b + 1],
            in_=sxrow[0:1, b * 128:(b + 1) * 128],
            identity=cbf[0:1, CB_ONESC:CB_ONESC + 1]),
            waits=[(vser, v_after_sxrow), (dcst, 48)],
            inc=(t_sx if b == NB - 1 else None))
    p.step("vector", lambda e: e.tensor_copy(out=sxcol[:], in_=ptc[:, 0:16:2]),
           waits=[(t_sx, NTT + 1)])
    p.step("vector", lambda e: e.tensor_single_scalar(
        out=sxcol05[:], in_=ptc[:, 0:16:2], scalar=1.0 / 128.0, op=ALU.mult))
    v_after_sxcol = vser.n

    pc0k = ps("pc0k", [128, 8])
    for kc in range(8):
        for b in range(NB):
            p.step("tensor", lambda e, b=b, kc=kc: e.matmul(
                out=pc0k[:, kc:kc + 1],
                lhsT=wqk_sb[:, b * 2048 + 1024 + kc * 128:
                            b * 2048 + 1024 + (kc + 1) * 128],
                rhs=sxcol[:, b:b + 1],
                start=(b == 0), stop=(b == NB - 1)),
                waits=[(vser, v_after_sxcol), (dwq, 16 * 2 * NB)],
                inc=(t_c0 if (b == NB - 1 and kc == 7) else None))
    pc0v = ps("pc0v", [1, D])
    for b in range(NB):
        for hf in range(2):
            p.step("tensor", lambda e, b=b, hf=hf: e.matmul(
                out=pc0v[:, hf * 512:(hf + 1) * 512],
                lhsT=sxcol05[:, b:b + 1],
                rhs=wv_sb[:, b * D + hf * 512: b * D + (hf + 1) * 512],
                start=(b == 0), stop=(b == NB - 1)),
                inc=(t_c0 if (b == NB - 1 and hf == 1) else None))
    p.step("vector", lambda e: e.tensor_copy(out=c0k[:], in_=pc0k[:]),
           waits=[(t_c0, 2)])
    p.step("vector", lambda e: e.tensor_copy(out=c0vrow[:], in_=pc0v[:]))
    v_carry_done = vser.n

    free("psx", "psc", "ptc", "pc0k", "pc0v")
    free("xin", "xpin", "xpbf", "xhst", "sqs")
    pvv = [ps("pvv0", [128, D]), ps("pvv1", [128, D])]
    ev_done = [0] * NTT
    for tt in range(NTT):
        pb = tt % 2
        for b in range(NB):
            for hf in range(2):
                p.step("tensor", lambda e, tt=tt, b=b, hf=hf, pb=pb: e.matmul(
                    out=pvv[pb][:, hf * 512:(hf + 1) * 512],
                    lhsT=xhat[:, b * TOK + tt * 128: b * TOK + (tt + 1) * 128],
                    rhs=wv_sb[:, b * D + hf * 512: b * D + (hf + 1) * 512],
                    start=(b == 0), stop=(b == NB - 1)),
                    waits=(([(dtrx, 16 * NB), (vser, v_carry_done)]
                            if b == 0 and hf == 0 else [])
                           + ([(vser, ev_done[tt - 2])]
                              if (tt >= 2 and b == 0 and hf == 0) else [])),
                    inc=(t_v if (b == NB - 1 and hf == 1) else None))
        p.step("vector", lambda e, tt=tt, pb=pb: e.tensor_copy(
            out=V[:, tt * D:(tt + 1) * D], in_=pvv[pb][:]),
            waits=[(t_v, tt + 1)])
        ev_done[tt] = vser.n

    eqk_done = [0] * 64
    pqk = [ps("pqk0", [128, 512]), ps("pqk1", [128, 512])]
    n_qk = 0
    for tg in range(4):
        for oc in range(16):
            pb = n_qk % 2
            for b in range(NB):
                p.step("tensor", lambda e, oc=oc, b=b, tg=tg, pb=pb: e.matmul(
                    out=pqk[pb][:],
                    lhsT=wqk_sb[:, b * 2048 + oc * 128:
                                b * 2048 + (oc + 1) * 128],
                    rhs=xhat[:, b * TOK + tg * 512: b * TOK + (tg + 1) * 512],
                    start=(b == 0), stop=(b == NB - 1)),
                    waits=([(vser if pb == 0 else aser,
                             eqk_done[n_qk - 2])] if (n_qk >= 2 and b == 0)
                           else []),
                    inc=(t_qk if b == NB - 1 else None))
            dst = Q if oc < 8 else K
            dj = oc % 8
            if pb == 0:
                p.step("vector", lambda e, dst=dst, dj=dj, tg=tg, pb=pb:
                       e.tensor_copy(
                           out=dst[:, dj * TOK + tg * 512:
                                   dj * TOK + (tg + 1) * 512],
                           in_=pqk[pb][:]), waits=[(t_qk, n_qk + 1)])
                eqk_done[n_qk] = vser.n
            else:
                p.step("scalar", lambda e, dst=dst, dj=dj, tg=tg, pb=pb:
                       e.activation(
                           out=dst[:, dj * TOK + tg * 512:
                                   dj * TOK + (tg + 1) * 512],
                           in_=pqk[pb][:], func=AF.Identity),
                       waits=[(t_qk, n_qk + 1)])
                eqk_done[n_qk] = aser.n
            n_qk += 1
    v_p1_done = vser.n
    a_p1_done = aser.n


    free("pqk0", "pqk1", "pvv0", "pvv1")
    if "dbg_xhat" in dbg_d:
        free("wv_sb")
    else:
        free("wv_sb", "wqk_sb", "xhat")

    t_cv = S("t_cv")
    t_log = S("t_log")
    t_att = S("t_att")
    dsg = S("dsg")
    dcv = S("dcv")
    p2rdy = barrier("p2rdy", [(t_qk, 64), (dtrx, 16 * NB), (dwq, 16 * 2 * NB),
                              (dxh, 16 * NTT), (dxo, 16 * NTT),
                              (dxp, 16 * NTT), (t_v, NTT), (t_sx, NTT + 1),
                              (t_c0, 2), (vser, v_p1_done),
                              (aser, a_p1_done)])

    CK = sb("CK", [128, NCH * 8], F32)
    CKcum = sb("CKcum", [128, NCH * 8], F32)
    CVsb = sb("CVsb", [32, D], BF16)
    CVXsb = sb("CVXsb", [32, D], BF16)
    sgm = sb("sgm", [16, TOK], F32)
    sgfl = sb("sgfl", [1, 2 * 1024], BF16)
    cvfl = sb("cvfl", [1, 2 * 1024], BF16)
    Sm = sb("Sm", [128, 1024], F32)
    Pn = sb("Pn", [128, 1024], BF16)
    PTsb = sb("PTsb", [128, 1024], BF16)
    mx = sb("mx", [128, 16], F32)
    rs = sb("rs", [128, 16], F32)
    rsi = sb("rsi", [128, 16], F32)
    ast = sb("ast", [128, 2 * 512], BF16)
    da = S("da")
    cmask = sb("cmask_sb", [128, 1024], F32)
    p.dma("sync", lambda e: e.dma_start(out=cmask[:], in_=cmask_d),
          waits=[(p2rdy, 1)], inc=dcst)
    CKB = sb("CKB", [128, TOK], BF16)
    Mb = sb("Mb", [128, TOK], BF16)

    for b in range(NB):
        p.step("vector", lambda e, b=b: e.reduce_sum(
            out=CK[:, b:NCH * 8:8],
            in_=K[:, b * TOK:(b + 1) * TOK].rearrange(
                "p (c x) -> p c x", c=NCH), axis=AX.X),
            waits=([(vser, v_p1_done), (aser, a_p1_done), (p2rdy, 1)]
                   if b == 0 else []))
    p.step("vector", lambda e: e.tensor_copy(out=CKcum[:, 0:8], in_=c0k[:]))
    for c in range(1, NCH):
        p.step("vector", lambda e, c=c: e.tensor_tensor(
            out=CKcum[:, c * 8:(c + 1) * 8],
            in0=CKcum[:, (c - 1) * 8:c * 8],
            in1=CK[:, (c - 1) * 8:c * 8], op=ALU.add))
    v_ckcum = vser.n

    pcv = ps("pcv", [32, D])
    for tt in range(NTT):
        for hf in range(2):
            p.step("tensor", lambda e, tt=tt, hf=hf: e.matmul(
                out=pcv[:, hf * 512:(hf + 1) * 512],
                lhsT=cbf[:, CB_CVIND + tt * 32:CB_CVIND + (tt + 1) * 32],
                rhs=V[:, tt * D + hf * 512:tt * D + (hf + 1) * 512],
                start=(tt == 0), stop=(tt == NTT - 1)),
                waits=([(t_v, NTT), (p2rdy, 1), (vser, v_p1_done),
                        (aser, a_p1_done)] if tt == 0 and hf == 0 else []),
                inc=(t_cv if (tt == NTT - 1 and hf == 1) else None))
    p.step("vector", lambda e: e.tensor_copy(out=CVsb[:], in_=pcv[:]),
           waits=[(t_cv, 1)])
    v_cvsb = vser.n
    pcvx = ps("pcvx", [32, D])
    for hf in range(2):
        p.step("tensor", lambda e, hf=hf: e.matmul(
            out=pcvx[:, hf * 512:(hf + 1) * 512],
            lhsT=cbf[0:32, CB_TRI:CB_TRI + 32],
            rhs=CVsb[:, hf * 512:(hf + 1) * 512],
            start=True, stop=False),
            waits=[(vser, v_cvsb)])
        p.step("tensor", lambda e, hf=hf: e.matmul(
            out=pcvx[:, hf * 512:(hf + 1) * 512],
            lhsT=cbf[0:1, CB_ONESR32:CB_ONESR32 + 32],
            rhs=c0vrow[0:1, hf * 512:(hf + 1) * 512],
            start=False, stop=True),
            waits=[(vser, v_carry_done)],
            inc=(t_cv if hf == 1 else None))
    p.step("vector", lambda e: e.tensor_copy(out=CVXsb[:], in_=pcvx[:]),
           waits=[(t_cv, 2)])
    v_cvx = vser.n
    free("pcv", "pcvx")

    plog = [ps("plog%d" % g, [16, 512]) for g in range(4)]
    mdone = [0] * NB
    for b in range(NB):
        p.step("vector", lambda e, b=b: e.tensor_copy(
            out=CKB[:],
            in_=CKcum[:, b:NCH * 8:8].unsqueeze(2)
                .broadcast_to([128, NCH, CS])),
            waits=([(t_log, t_log.n)] if b > 0 else []))
        p.step("vector", lambda e, b=b: e.tensor_tensor(
            out=Mb[:], in0=Q[:, b * TOK:(b + 1) * TOK], in1=CKB[:],
            op=ALU.mult))
        mdone[b] = vser.n
        for g in range(4):
            p.step("tensor", lambda e, b=b, g=g: e.matmul(
                out=plog[g][:],
                lhsT=cbf[:, CB_LIND + b * 16:CB_LIND + (b + 1) * 16],
                rhs=Mb[:, g * 512:(g + 1) * 512],
                start=(b == 0), stop=(b == NB - 1)),
                waits=[(vser, mdone[b])],
                inc=(t_log if g == 3 else None))
    for g in range(4):
        p.step("scalar", lambda e, g=g: e.activation(
            out=sgm[:, g * 512:(g + 1) * 512], in_=plog[g][:], func=AF.Exp,
            scale=-SCALE / 64.0), waits=[(t_log, NB), (p2rdy, 1)])
    p.step("vector", lambda e: e.tensor_single_scalar(
        out=sgm[:], in_=sgm[:], scalar=1.0, op=ALU.add), waits=[aser])
    p.step("vector", lambda e: e.reciprocal(out=sgm[:], in_=sgm[:]))
    v_sig = vser.n
    free("plog0", "plog1", "plog2", "plog3")
    free("Mb", "CKB")

    dwl = S("dwl")
    dqw = S("dqw")
    t_wq = S("t_wq")
    wstg = sb("wstg", [128, INNER], F32)
    abuf = sb("abuf", [128, INNER], BF16)
    wabs = sb("wabs", [128, 27], F32)
    pwq = ps("pwq", [128, 8])
    pw1 = pwq[:, 0:4]
    pwt = pwq[0:1, 4:8]
    wsrc = [(wgT, wgq_d, 0), (wuT, wuq_d, 1), (wdT, wdq_d, 2)]

    def wtile_aps(mat, t):
        srcT, dstT, mi = wsrc[mat]
        if mi < 2:
            return (srcT[t * 128:(t + 1) * 128, :],
                    dstT[t * 128:(t + 1) * 128, :])
        return (srcT[t * 512:(t + 1) * 512, :].rearrange(
                    "(a p) f -> p a f", p=128),
                dstT[t * 512:(t + 1) * 512, :].rearrange(
                    "(a p) f -> p a f", p=128))

    def wq_p1_tile(mat, t, eng="gpsimd"):
        p.dma("gpsimd", lambda e, mat=mat, t=t: e.dma_start(
            out=(wstg[:].rearrange("p (a f) -> p a f", a=4) if mat == 2
                 else wstg[:]),
            in_=wtile_aps(mat, t)[0]),
            waits=[pser, vser, aser], inc=dwl)
        p.step("scalar", lambda e, mat=mat, t=t: e.activation(
            out=wstg[:], in_=wstg[:], func=AF.Abs,
            accum_out=wabs[:, mat * 9 + t:mat * 9 + t + 1]), waits=[dwl])

    def wq_thr(mat):
        p.step("vector", lambda e, mat=mat: e.reduce_sum(
            out=wabs[:, mat * 9 + 8:mat * 9 + 9],
            in_=wabs[:, mat * 9:mat * 9 + 8], axis=AX.X),
            waits=[aser, pser])
        vv = vser.n
        p.step("tensor", lambda e, mat=mat: e.matmul(
            out=pwt[0:1, mat:mat + 1], lhsT=wabs[:, mat * 9 + 8:mat * 9 + 9],
            rhs=cst[:, 1:2],
            start=True, stop=True), waits=[(vser, vv)], inc=t_wq)
        p.step("vector", lambda e, mat=mat: e.tensor_scalar(
            out=wsc[0:1, mat:mat + 1], in0=pwt[0:1, mat:mat + 1],
            scalar1=1.0 / (D * INNER), scalar2=1e-5,
            op0=ALU.mult, op1=ALU.max), waits=[(t_wq, t_wq.n)])
        p.step("vector", lambda e, mat=mat: e.tensor_single_scalar(
            out=wsc[0:1, 3 + mat:4 + mat], in_=wsc[0:1, mat:mat + 1],
            scalar=0.5, op=ALU.mult))
        vv2 = vser.n
        p.step("tensor", lambda e, mat=mat: e.matmul(
            out=pw1[:, mat:mat + 1], lhsT=crow[:],
            rhs=wsc[0:1, mat:mat + 1],
            start=True, stop=True), waits=[(vser, vv2)], inc=t_wq)
        p.step("vector", lambda e, mat=mat: e.tensor_copy(
            out=wmcol[:, mat:mat + 1], in_=pw1[:, mat:mat + 1]),
            waits=[(t_wq, t_wq.n)])
        p.step("tensor", lambda e, mat=mat: e.matmul(
            out=pw1[:, 3:4], lhsT=crow[:], rhs=wsc[0:1, 3 + mat:4 + mat],
            start=True, stop=True), waits=[vser], inc=t_wq)
        p.step("vector", lambda e, mat=mat: e.tensor_copy(
            out=thrcol[:, mat:mat + 1], in_=pw1[:, 3:4]),
            waits=[(t_wq, t_wq.n)])
        p.step("vector", lambda e, mat=mat: e.tensor_single_scalar(
            out=nthrcol[:, mat:mat + 1], in_=thrcol[:, mat:mat + 1],
            scalar=-1.0, op=ALU.mult))

    def wq_p2_tile(mat, t, eng="gpsimd"):
        p.dma("gpsimd", lambda e, mat=mat, t=t: e.dma_start(
            out=(wstg[:].rearrange("p (a f) -> p a f", a=4) if mat == 2
                 else wstg[:]),
            in_=wtile_aps(mat, t)[0]),
            waits=[pser, vser, aser], inc=dwl)
        p.step(eng, lambda e, mat=mat: e.tensor_scalar(
            out=abuf[:], in0=wstg[:], scalar1=nthrcol[:, mat:mat + 1],
            scalar2=None, op0=ALU.is_lt), waits=[dwl, dqw])
        p.step(eng, lambda e, mat=mat: e.scalar_tensor_tensor(
            out=abuf[:], in0=wstg[:], scalar=thrcol[:, mat:mat + 1],
            in1=abuf[:], op0=ALU.is_gt, op1=ALU.subtract))
        sem = pser if eng == "gpsimd" else vser
        vv = sem.n
        p.dma("gpsimd", lambda e, mat=mat, t=t, sem=sem, vv=vv: e.dma_start(
            out=wtile_aps(mat, t)[1],
            in_=(abuf[:].rearrange("p (a f) -> p a f", a=4) if mat == 2
                 else abuf[:])), waits=[(sem, vv)], inc=dqw)

    WQE = _os.environ.get("WQ_ENG", "vector")
    wq_sched = {}
    for t in range(4):
        wq_sched.setdefault(t, []).extend(
            [lambda m=0, k=2 * t: wq_p1_tile(m, k),
             lambda m=0, k=2 * t + 1: wq_p1_tile(m, k)])
    wq_sched.setdefault(4, []).append(lambda: wq_thr(0))
    for t in range(4):
        wq_sched.setdefault(4 + t, []).extend(
            [lambda m=1, k=2 * t: wq_p1_tile(m, k),
             lambda m=1, k=2 * t + 1: wq_p1_tile(m, k)])
    wq_sched.setdefault(8, []).append(lambda: wq_thr(1))
    for t in range(5, 13):
        wq_sched.setdefault(t, []).extend(
            [lambda m=0, k=2 * (t - 5): wq_p2_tile(m, k, WQE),
             lambda m=0, k=2 * (t - 5) + 1: wq_p2_tile(m, k, WQE)]
            if t < 9 else
            [lambda m=1, k=2 * (t - 9): wq_p2_tile(m, k, WQE),
             lambda m=1, k=2 * (t - 9) + 1: wq_p2_tile(m, k, WQE)])
    for t in range(4):
        wq_sched.setdefault(12 + t, []).extend(
            [lambda m=2, k=2 * t: wq_p1_tile(m, k, "vector"),
             lambda m=2, k=2 * t + 1: wq_p1_tile(m, k, "vector")])
    wq_tail = ([lambda: wq_thr(2)]
               + [lambda m=2, k=k: wq_p2_tile(m, k, "vector")
                  for k in range(8)])

    rank1_done = [0] * NCH
    sgcv_issued = [None] * NCH

    def issue_flat_dma(c, gate_batch):
        w = [(vser, max(v_sig, v_cvx)), (p2rdy, 1)]
        if c >= 2:
            w.append((t_att, rank1_done[c - 2]))
        bufc = c % 2
        p.dma("gpsimd", lambda e, c=c, bufc=bufc: e.dma_start(
            out=sgfl[0:1, bufc * 1024:(bufc + 1) * 1024].rearrange(
                "o (h x) -> o h x", h=16),
            in_=sgm[:, c * CS:(c + 1) * CS]), waits=w, inc=dsg)
        p.dma("gpsimd", lambda e, c=c, bufc=bufc: e.dma_start(
            out=cvfl[0:1, bufc * 1024:(bufc + 1) * 1024],
            in_=CVXsb[c:c + 1, :]), waits=list(w), inc=dcv)
        sgcv_issued[c] = dsg.n

    for c in (0, 1):
        issue_flat_dma(c, None)

    _ps2 = _os.environ.get("S_SINGLE", "0") != "1"
    psS = ([ps("psSa", [128, 1024]), ps("psSb", [128, 1024])] if _ps2
           else [ps("psSa", [128, 1024])] * 2)
    psPT = ps("psPT", [128, 1024], BF16)
    psA = [ps("psA0", [128, 512]), ps("psA1", [128, 512])]
    s_free_at = [vser.n, vser.n]
    a_chunk0 = aser.n
    pt_free_at = 0
    a_free_at = [0, 0]

    for t in range(NCH // 2):
        c0 = 2 * t
        sp = (t % 2) if _ps2 else 0
        nS = 0
        for c2 in (0, 1):
            c = c0 + c2
            for j in range(NB):
                for hp in (0, 1):
                    nS += 1
                    p.step("tensor", lambda e, c=c, c2=c2, j=j, hp=hp, sp=sp:
                           e.matmul(
                        out=psS[sp][hp * 64:(hp + 1) * 64,
                                    c2 * 512 + j * 64:c2 * 512 + (j + 1) * 64],
                        lhsT=Q[hp * 64:(hp + 1) * 64,
                               j * TOK + c * CS:j * TOK + (c + 1) * CS],
                        rhs=K[hp * 64:(hp + 1) * 64,
                              j * TOK + c * CS:j * TOK + (c + 1) * CS],
                        start=True, stop=True,
                        tile_position=(hp * 64, hp * 64)),
                        waits=(([(vser, s_free_at[sp])]
                                + ([(aser, a_chunk0)] if t == 0 else []))
                               if nS == 1 else []),
                        inc=(t_att if nS == 32 else None))
        tS = t_att.n
        p.step("vector", lambda e, sp=sp: e.tensor_tensor(
            out=Sm[:], in0=psS[sp][:], in1=cmask[:], op=ALU.add),
            waits=[(t_att, tS), dcst])
        s_free_at[sp] = vser.n
        p.step("vector", lambda e: e.reduce_max(
            out=mx[:], in_=Sm[:].rearrange("p (g x) -> p g x", g=16),
            axis=AX.X))
        p.step("vector", lambda e: e.tensor_tensor(
            out=Sm[:].rearrange("p (g x) -> p g x", g=16),
            in0=Sm[:].rearrange("p (g x) -> p g x", g=16),
            in1=mx[:].unsqueeze(2).broadcast_to([128, 16, CS]),
            op=ALU.subtract))
        v_sub = vser.n
        p.step("scalar", lambda e: e.activation(
            out=Sm[:], in_=Sm[:], func=AF.Exp, scale=SCALE),
            waits=[(vser, v_sub)])
        p.step("vector", lambda e: e.reduce_sum(
            out=rs[:], in_=Sm[:].rearrange("p (g x) -> p g x", g=16),
            axis=AX.X), waits=[aser])
        p.step("vector", lambda e: e.reciprocal(out=rsi[:], in_=rs[:]))
        p.step("vector", lambda e: e.tensor_tensor(
            out=Pn[:].rearrange("p (g x) -> p g x", g=16),
            in0=Sm[:].rearrange("p (g x) -> p g x", g=16),
            in1=rsi[:].unsqueeze(2).broadcast_to([128, 16, CS]),
            op=ALU.mult))
        v_pn = vser.n
        nT = 0
        for c2 in (0, 1):
            for j in range(NB):
                nT += 1
                p.step("tensor", lambda e, c2=c2, j=j: e.transpose(
                    out=psPT[c2 * 64:(c2 + 1) * 64, j * 128:(j + 1) * 128],
                    in_=Pn[:, c2 * 512 + j * 64:c2 * 512 + (j + 1) * 64],
                    identity=cbf[:, CB_IDN:CB_IDN + 128],
                    tile_position=(0, c2 * 64)),
                    waits=([(vser, max(v_pn, pt_free_at))] if nT == 1 else []),
                    inc=(t_att if nT == 16 else None))
        tT = t_att.n
        p.step("vector", lambda e: e.tensor_copy(out=PTsb[:], in_=psPT[:]),
               waits=[(t_att, tT)])
        pt_free_at = vser.n
        for c2 in (0, 1):
            c = c0 + c2
            first = True
            for j in range(NB):
                for he in (0, 1):
                    p.step("tensor", lambda e, c=c, c2=c2, j=j, he=he: e.matmul(
                        out=psA[c2][he * 64:(he + 1) * 64,
                                    j * 64:(j + 1) * 64],
                        lhsT=V[c2 * 64:(c2 + 1) * 64,
                               (c // 2) * D + j * 128 + he * 64:
                               (c // 2) * D + j * 128 + (he + 1) * 64],
                        rhs=PTsb[c2 * 64:(c2 + 1) * 64,
                                 j * 128 + he * 64:j * 128 + (he + 1) * 64],
                        start=True, stop=False,
                        tile_position=(c2 * 64, he * 64)),
                        waits=([(vser, pt_free_at), (aser, a_free_at[c2]),
                                dsg, dcv] if first else []))
                    first = False
                    p.step("tensor", lambda e, c=c, c2=c2, j=j, he=he: e.matmul(
                        out=psA[c2][he * 64:(he + 1) * 64,
                                    j * 64:(j + 1) * 64],
                        lhsT=cvfl[0:1, (c % 2) * 1024 + j * 128 + he * 64:
                                  (c % 2) * 1024 + j * 128 + (he + 1) * 64],
                        rhs=sgfl[0:1, (c % 2) * 1024 + (2 * j + he) * 64:
                                 (c % 2) * 1024 + (2 * j + he + 1) * 64],
                        start=False, stop=True,
                        tile_position=(0, he * 64)),
                        inc=(t_att if (j == NB - 1 and he == 1) else None))
            rank1_done[c] = t_att.n
            p.step("scalar", lambda e, c=c, c2=c2: e.activation(
                out=ast[:, c2 * 512:(c2 + 1) * 512],
                in_=psA[c2][:], func=AF.Identity),
                waits=[(t_att, rank1_done[c])] + ([da] if t >= 1 else []))
            a_free_at[c2] = aser.n
            p.dma("gpsimd", lambda e, c=c, c2=c2: e.dma_start(
                out=a_d[:, c * CS:(c + 1) * CS].rearrange(
                    "(j p) q -> p j q", p=128),
                in_=ast[:, c2 * 512:(c2 + 1) * 512].rearrange(
                    "p (j q) -> p j q", j=NB)),
                waits=[aser], inc=da)
        for c in (c0 + 2, c0 + 3):
            if c < NCH:
                issue_flat_dma(c, t)
        if _os.environ.get("WQ_OFF", "0") != "1":
            for fn in wq_sched.get(t, []):
                fn()
    if _os.environ.get("WQ_OFF", "0") != "1":
        for fn in wq_tail:
            fn()
    v_wq_done = vser.n
    wq_bar = barrier("wq_bar", [(dqw, dqw.n), (dwl, dwl.n),
                                (vser, v_wq_done), (t_wq, t_wq.n)])
    v_att_done = vser.n
    a_att_done = aser.n
    att_bar = barrier("att_bar", [(vser, v_att_done), (aser, a_att_done),
                                  (da, da.n), dsg, dcv,
                                  (t_att, t_att.n)])

    if "dbg_stats" in dbg_d:
        for idx, tns in enumerate([mcol, sumsq, rstd, sumx, lnt_o, bcol]):
            p.dma("gpsimd", lambda e, idx=idx, tns=tns: e.dma_start(
                out=dbg_d["dbg_stats"][:, idx * 16:(idx + 1) * 16], in_=tns[:]),
                waits=[(aser, act_done_own[NTT - 1]),
                       (vser, dve_done_own[NTT - 1])], inc=dbg_done)
    if "dbg_xhd" in dbg_d:
        p.dma("gpsimd", lambda e: e.dma_start(
            out=dbg_d["dbg_xhd"], in_=xhat_d), waits=[(dxh, 16 * NTT)],
            inc=dbg_done)
    if "dbg_xhat" in dbg_d:
        p.dma("gpsimd", lambda e: e.dma_start(
            out=dbg_d["dbg_xhat"], in_=xhat[:]), waits=[(dtrx, 16 * NB)],
            inc=dbg_done)
    for nm, srct in [("dbg_q", Q), ("dbg_k", K), ("dbg_v", V)]:
        if nm in dbg_d:
            p.dma("gpsimd", lambda e, nm=nm, srct=srct: e.dma_start(
                out=dbg_d[nm], in_=srct[:]),
                waits=[(vser, v_p1_done), (aser, a_p1_done)], inc=dbg_done)
    if "dbg_a" in dbg_d:
        p.dma("gpsimd", lambda e: e.dma_start(
            out=dbg_d["dbg_a"].rearrange("p (j x) -> (j p) x", j=NB), in_=a_d),
            waits=[(att_bar, 1)], inc=dbg_done)
    if "dbg_sig" in dbg_d:
        p.dma("gpsimd", lambda e: e.dma_start(out=dbg_d["dbg_sig"], in_=sgm[:]),
              waits=[(vser, v_sig)], inc=dbg_done)
    if "dbg_cvx" in dbg_d:
        p.dma("gpsimd", lambda e: e.dma_start(out=dbg_d["dbg_cvx"],
                                              in_=CVXsb[:]),
              waits=[(vser, v_cvx)], inc=dbg_done)
    if "dbg_c0k" in dbg_d:
        p.dma("gpsimd", lambda e: e.dma_start(out=dbg_d["dbg_c0k"], in_=c0k[:]),
              waits=[(vser, v_carry_done)], inc=dbg_done)
    if "dbg_c0v" in dbg_d:
        p.dma("gpsimd", lambda e: e.dma_start(out=dbg_d["dbg_c0v"],
                                              in_=c0vrow[:]),
              waits=[(vser, v_carry_done)], inc=dbg_done)


    if _ps2:
        free("psA0", "psA1", "psPT", "psSb", "psSa")
    else:
        free("psA0", "psA1", "psPT", "psSa")
    free("pwq", "wabs", "abuf", "wstg")
    free("ast", "rsi", "rs", "mx", "PTsb", "Pn", "Sm", "cvfl", "sgfl",
         "sgm", "CVXsb", "CVsb", "CKcum", "CK", "cmask_sb")
    free("V", "K", "Q")
    dr2 = S("dr2")
    dxr = S("dxr")
    dal = S("dal")
    dwo = S("dwo")
    dwgl = S("dwgl")
    dwul = S("dwul")
    dx2q = S("dx2q")
    t_wo = S("t_wo")
    Wgq_sb = sb("Wgq_sb", [128, NB * INNER], BF16)
    A_sb = sb("A_sb", [128, NB * TOK], BF16)
    wout_sb = sb("wout_sb", [128, NB * D], BF16)
    xre = sb("xre", [128, 2 * D], F32)
    r2st = sb("r2st", [128, 2 * D], F32)
    xh2 = sb("xh2", [128, D], F32)
    xh2b = sb("xh2b", [128, D], F32)
    x2qst = sb("x2qst", [128, 2 * D], BF16)
    psW = [ps("psW0", [128, D]), ps("psW1", [128, D])]
    for b in range(NB):
        p.dma("gpsimd", lambda e, b=b: e.dma_start(
            out=A_sb[:, b * TOK:(b + 1) * TOK],
            in_=a_d[b * 128:(b + 1) * 128, :]),
            waits=[(att_bar, 1), (wq_bar, 1)], inc=dal)
        p.dma("gpsimd", lambda e, b=b: e.dma_start(
            out=wout_sb[:, b * D:(b + 1) * D],
            in_=wout[b * 128:(b + 1) * 128, :]),
            waits=[(wq_bar, 1)], inc=dwo)
        p.dma("gpsimd", lambda e, b=b: e.dma_start(
            out=Wgq_sb[:, b * INNER:(b + 1) * INNER],
            in_=wgq_d[b * 128:(b + 1) * 128, :]),
            waits=[(wq_bar, 1)], inc=dwgl)
    r2_evac_v = [0] * NTT
    for tt in range(NTT):
        pb = tt % 2
        p.dma("sync", lambda e, tt=tt, pb=pb: e.dma_start(
            out=xre[:, pb * D:(pb + 1) * D],
            in_=x_own[tt * 128:(tt + 1) * 128, :]),
            waits=([(wq_bar, 1)] if tt < 1 else [(vser, r2_evac_v[tt - 1])]),
            inc=dxr)
        for b in range(NB):
            for hf in range(2):
                p.step("tensor", lambda e, tt=tt, b=b, hf=hf, pb=pb: e.matmul(
                    out=psW[pb][:, hf * 512:(hf + 1) * 512],
                    lhsT=A_sb[:, b * TOK + tt * 128:b * TOK + (tt + 1) * 128],
                    rhs=wout_sb[:, b * D + hf * 512:b * D + (hf + 1) * 512],
                    start=(b == 0), stop=(b == NB - 1)),
                    waits=(([(dal, 16 * NB), (dwo, 16 * NB), (wq_bar, 1)]
                            if tt == 0 else [])
                           + ([(vser, r2_evac_v[tt - 2])]
                              if (tt >= 2 and b == 0 and hf == 0) else [])),
                    inc=(t_wo if (b == NB - 1 and hf == 1) else None))
        p.step("vector", lambda e, tt=tt, pb=pb: e.scalar_tensor_tensor(
            out=r2st[:, pb * D:(pb + 1) * D], in0=psW[pb][:], scalar=1.0,
            in1=xre[:, pb * D:(pb + 1) * D], op0=ALU.mult, op1=ALU.add),
            waits=[(t_wo, tt + 1), (dxr, 16 * (tt + 1)), dr2])
        p.dma("gpsimd", lambda e, tt=tt, pb=pb: e.dma_start(
            out=r2_d[tt * 128:(tt + 1) * 128, :],
            in_=r2st[:, pb * D:(pb + 1) * D]),
            waits=[vser], inc=dr2)
        i = tt
        p.step("scalar", lambda e, i=i, pb=pb: e.activation(
            out=xh2[:], in_=r2st[:, pb * D:(pb + 1) * D], func=AF.Square,
            accum_out=sq2c[:, i:i + 1]), waits=[vser])
        p.step("vector", lambda e, i=i, pb=pb: e.reduce_sum(
            out=m2c[:, i:i + 1], in_=r2st[:, pb * D:(pb + 1) * D],
            axis=AX.X))
        p.step("vector", lambda e, i=i: e.tensor_single_scalar(
            out=m2c[:, i:i + 1], in_=m2c[:, i:i + 1],
            scalar=1.0 / 1024.0, op=ALU.mult))
        p.step("vector", lambda e, i=i: e.tensor_tensor(
            out=msq2c[:, i:i + 1], in0=m2c[:, i:i + 1],
            in1=m2c[:, i:i + 1], op=ALU.mult))
        p.step("vector", lambda e, i=i: e.scalar_tensor_tensor(
            out=sq2c[:, i:i + 1], in0=sq2c[:, i:i + 1],
            scalar=1.0 / 1024.0, in1=msq2c[:, i:i + 1],
            op0=ALU.mult, op1=ALU.subtract), waits=[aser])
        p.step("scalar", lambda e, i=i: e.activation(
            out=ln2c[:, i:i + 1], in_=sq2c[:, i:i + 1], func=AF.Ln,
            bias=1e-5), waits=[vser])
        p.step("scalar", lambda e, i=i: e.activation(
            out=rstd2c[:, i:i + 1], in_=ln2c[:, i:i + 1], func=AF.Exp,
            scale=-0.5))
        p.step("vector", lambda e, i=i: e.scalar_tensor_tensor(
            out=bc2c[:, i:i + 1], in0=m2c[:, i:i + 1], scalar=-1.0,
            in1=rstd2c[:, i:i + 1], op0=ALU.mult, op1=ALU.mult),
            waits=[aser])
        p.step("vector", lambda e, i=i, pb=pb: e.tensor_scalar(
            out=xh2[:], in0=r2st[:, pb * D:(pb + 1) * D],
            scalar1=rstd2c[:, i:i + 1], scalar2=bc2c[:, i:i + 1],
            op0=ALU.mult, op1=ALU.add), waits=[aser])
        p.step("vector", lambda e, i=i: e.reduce_max(
            out=amc[:, i:i + 1], in_=xh2[:], axis=AX.X,
            apply_absolute_value=True))
        p.step("vector", lambda e, i=i: e.tensor_single_scalar(
            out=amc[:, i:i + 1], in_=amc[:, i:i + 1], scalar=1e-5,
            op=ALU.max))
        p.step("vector", lambda e, i=i: e.reciprocal(
            out=s1c[:, i:i + 1], in_=amc[:, i:i + 1]))
        p.step("vector", lambda e, i=i: e.tensor_single_scalar(
            out=s1c[:, i:i + 1], in_=s1c[:, i:i + 1], scalar=127.0,
            op=ALU.mult))
        p.step("vector", lambda e, i=i: e.tensor_single_scalar(
            out=rs1c[:, i:i + 1], in_=amc[:, i:i + 1],
            scalar=1.0 / 127.0, op=ALU.mult))
        p.step("vector", lambda e, i=i: e.tensor_tensor(
            out=gscc[:, i:i + 1], in0=rs1c[:, i:i + 1],
            in1=wmcol[:, 0:1], op=ALU.mult))
        p.step("vector", lambda e, i=i: e.tensor_tensor(
            out=uscc[:, i:i + 1], in0=rs1c[:, i:i + 1],
            in1=wmcol[:, 1:2], op=ALU.mult))
        p.step("scalar", lambda e, i=i: e.activation(
            out=xh2b[:], in_=xh2[:], func=AF.Identity,
            scale=s1c[:, i:i + 1], bias=mgc[:]), waits=[vser])
        p.step("scalar", lambda e, i=i, pb=pb: e.activation(
            out=x2qst[:, pb * D:(pb + 1) * D], in_=xh2b[:],
            func=AF.Identity, bias=nmgc[:]), waits=[dx2q])
        r2_evac_v[tt] = vser.n
        p.dma("gpsimd", lambda e, i=i, pb=pb: e.dma_start(
            out=x2q_d[i * 128:(i + 1) * 128, :],
            in_=x2qst[:, pb * D:(pb + 1) * D]),
            waits=[aser], inc=dx2q)
    p3_bar = barrier("p3_bar", [(dr2, 16 * NTT), (dx2q, 16 * NTT),
                                (t_wo, NTT), (vser, vser.n), (aser, aser.n)])
    free("psW0", "psW1", "x2qst", "xh2b", "xh2", "r2st", "xre", "wout_sb",
         "A_sb")

    dtx2 = S("dtx2")
    dhq = S("dhq")
    t_gu = S("t_gu")
    Wuq_sb = sb("Wuq_sb", [128, NB * INNER], BF16)
    for b in range(NB):
        p.dma("gpsimd", lambda e, b=b: e.dma_start(
            out=Wuq_sb[:, b * INNER:(b + 1) * INNER],
            in_=wuq_d[b * 128:(b + 1) * 128, :]),
            waits=[(p3_bar, 1)], inc=dwul)
    X2qT = sb("X2qT", [128, 2 * NB * 512], BF16)
    Hh = sb("Hh", [128, 5 * INNER], BF16)
    gbuf = sb("gbuf", [128, 3 * 512], BF16)
    hqs = sb("hqs", [128, D], F32)
    psG = [ps("psG0", [128, 512]), ps("psG1", [128, 512]),
           ps("psG2", [128, 512])]
    psU = [ps("psU0", [128, 512]), ps("psU1", [128, 512]),
           ps("psU2", [128, 512])]
    n_gu = 0
    gu_evac_done = [0] * 128
    aq2_done_v = [0] * NTT

    dtx2_done = [0] * 4

    def emit_transposes(q):
        qp = q % 2
        for b in range(NB):
            p.dma("sync", lambda e, q=q, b=b, qp=qp: e.dma_start(
                out=X2qT[:, qp * NB * 512 + b * 512:
                         qp * NB * 512 + (b + 1) * 512],
                in_=x2q_d[q * 512:(q + 1) * 512, b * 128:(b + 1) * 128],
                transpose=True),
                waits=[(p3_bar, 1)]
                      + ([(dtx2, 16 * NB * q)] if (q >= 1 and b == 0) else [])
                      + ([(t_gu, 32 * (q - 1))] if q >= 2 else []),
                inc=dtx2)
        dtx2_done[q] = dtx2.n

    emit_transposes(0)
    for i in range(NTT):
        q, t4 = i // 4, i % 4
        qp = q % 2
        if t4 == 0 and q + 1 < 4:
            emit_transposes(q + 1)
        for ic in range(8):
            pb = n_gu % (3 if _os.environ.get("GU2", "0") != "1" else 2)
            for b in range(NB):
                p.step("tensor", lambda e, qp=qp, t4=t4, ic=ic, b=b, pb=pb:
                       e.matmul(
                    out=psG[pb][:],
                    lhsT=X2qT[:, qp * NB * 512 + b * 512 + t4 * 128:
                              qp * NB * 512 + b * 512 + (t4 + 1) * 128],
                    rhs=Wgq_sb[:, b * INNER + ic * 512:
                               b * INNER + (ic + 1) * 512],
                    start=(b == 0), stop=(b == NB - 1)),
                    waits=(([(dtx2, dtx2_done[q]),
                             (dwgl, 16 * NB), (dwul, 16 * NB)]
                            if b == 0 and ic == 0 and t4 == 0 else [])
                           + ([(vser, gu_evac_done[n_gu - 3])]
                              if (n_gu >= 3 and b == 0) else [])))
                p.step("tensor", lambda e, qp=qp, t4=t4, ic=ic, b=b, pb=pb:
                       e.matmul(
                    out=psU[pb][:],
                    lhsT=X2qT[:, qp * NB * 512 + b * 512 + t4 * 128:
                              qp * NB * 512 + b * 512 + (t4 + 1) * 128],
                    rhs=Wuq_sb[:, b * INNER + ic * 512:
                               b * INNER + (ic + 1) * 512],
                    start=(b == 0), stop=(b == NB - 1)),
                    inc=(t_gu if b == NB - 1 else None))
            p.step("scalar", lambda e, i=i, ic=ic, pb=pb: e.activation(
                out=gbuf[:, pb * 512:(pb + 1) * 512], in_=psG[pb][:],
                func=AF.Silu, scale=gscc[:, i:i + 1]),
                waits=[(t_gu, t_gu.n)])
            p.step("vector", lambda e, i=i, ic=ic, pb=pb:
                   e.scalar_tensor_tensor(
                out=Hh[:, (i % 5) * INNER + ic * 512:
                       (i % 5) * INNER + (ic + 1) * 512],
                in0=psU[pb][:], scalar=uscc[:, i:i + 1],
                in1=gbuf[:, pb * 512:(pb + 1) * 512],
                op0=ALU.mult, op1=ALU.mult),
                waits=[aser] + ([dhq] if ic == 0 else []))
            gu_evac_done[n_gu] = vser.n
            n_gu += 1
        sl = (i % 5) * INNER
        p.step("vector", lambda e, sl=sl, i=i: e.reduce_max(
            out=amc[:, i:i + 1], in_=Hh[:, sl:sl + INNER], axis=AX.X,
            apply_absolute_value=True))
        p.step("vector", lambda e, i=i: e.tensor_single_scalar(
            out=amc[:, i:i + 1], in_=amc[:, i:i + 1], scalar=1e-5,
            op=ALU.max))
        p.step("vector", lambda e, i=i: e.reciprocal(
            out=s2c[:, i:i + 1], in_=amc[:, i:i + 1]))
        p.step("vector", lambda e, i=i: e.tensor_single_scalar(
            out=s2c[:, i:i + 1], in_=s2c[:, i:i + 1], scalar=127.0,
            op=ALU.mult))
        p.step("vector", lambda e, i=i: e.tensor_single_scalar(
            out=dscc[:, i:i + 1], in_=amc[:, i:i + 1],
            scalar=1.0 / 127.0, op=ALU.mult))
        p.step("vector", lambda e, i=i: e.tensor_tensor(
            out=dscc[:, i:i + 1], in0=dscc[:, i:i + 1],
            in1=wmcol[:, 2:3], op=ALU.mult))
        for hqc in range(4):
            p.step("scalar", lambda e, sl=sl, i=i, hqc=hqc: e.activation(
                out=hqs[:], in_=Hh[:, sl + hqc * D:sl + (hqc + 1) * D],
                func=AF.Identity, scale=s2c[:, i:i + 1], bias=mgc[:]),
                waits=([vser] if hqc == 0 else []))
            p.step("scalar", lambda e, sl=sl, i=i, hqc=hqc: e.activation(
                out=Hh[:, sl + hqc * D:sl + (hqc + 1) * D], in_=hqs[:],
                func=AF.Identity, bias=nmgc[:]))
        aq2_done_v[i] = vser.n
        p.dma("gpsimd", lambda e, sl=sl, i=i: e.dma_start(
            out=hq_d[i * 128:(i + 1) * 128, :],
            in_=Hh[:, sl:sl + INNER]),
            waits=[aser], inc=dhq)
    p5a_bar = barrier("p5a_bar", [(dhq, 16 * NTT), (vser, vser.n),
                                  (aser, aser.n), (t_gu, t_gu.n)])
    free("psU2", "psU1", "psU0", "psG2", "psG1", "psG0")
    free("hqs", "gbuf", "Hh", "X2qT", "Wuq_sb", "Wgq_sb")

    dwdl = S("dwdl")
    dhql = S("dhql")
    dr2b = S("dr2b")
    dyo = S("dyo")
    t_dn = S("t_dn")
    Wdq_sb = sb("Wdq_sb", [128, NIC * D], BF16)
    HqT = sb("HqT", [128, NIC * 512], BF16)
    r2b = sb("r2b", [128, 2 * D], F32)
    yst = sb("yst", [128, 2 * D], F32)
    psD = [ps("psD0", [128, D]), ps("psD1", [128, D])]
    for t in range(NB):
        p.dma("gpsimd", lambda e, t=t: e.dma_start(
            out=Wdq_sb[:, t * 4 * D:(t + 1) * 4 * D].rearrange(
                "p (a f) -> p a f", a=4),
            in_=wdq_d[t * 512:(t + 1) * 512, :].rearrange(
                "(a p) f -> p a f", p=128)),
            waits=[(p5a_bar, 1)], inc=dwdl)
    y_evac_v = [0] * NTT
    for q in range(4):
        for b32 in range(NIC):
            p.dma("sync", lambda e, q=q, b32=b32: e.dma_start(
                out=HqT[:, b32 * 512:(b32 + 1) * 512],
                in_=hq_d[q * 512:(q + 1) * 512, b32 * 128:(b32 + 1) * 128],
                transpose=True),
                waits=[(p5a_bar, 1)] + ([(t_dn, t_dn.n)] if q >= 1 else []),
                inc=dtx2)
        for t4 in range(4):
            i = q * 4 + t4
            pb = i % 2
            p.dma("sync", lambda e, i=i, pb=pb: e.dma_start(
                out=r2b[:, pb * D:(pb + 1) * D],
                in_=r2_d[i * 128:(i + 1) * 128, :]),
                waits=[(p5a_bar, 1)] + ([(vser, y_evac_v[i - 1])]
                                        if i >= 1 else []), inc=dr2b)
            for b32 in range(NIC):
                for hf in range(2):
                    p.step("tensor", lambda e, q=q, t4=t4, b32=b32, hf=hf,
                           pb=pb: e.matmul(
                        out=psD[pb][:, hf * 512:(hf + 1) * 512],
                        lhsT=HqT[:, b32 * 512 + t4 * 128:
                                 b32 * 512 + (t4 + 1) * 128],
                        rhs=Wdq_sb[:, b32 * D + hf * 512:
                                   b32 * D + (hf + 1) * 512],
                        start=(b32 == 0), stop=(b32 == NIC - 1)),
                        waits=(([(dtx2, dtx2.n), (dwdl, 16 * NB)]
                                if t4 == 0 and b32 == 0 and hf == 0 else [])
                               + ([(vser, y_evac_v[i - 2])]
                                  if (i >= 2 and b32 == 0 and hf == 0)
                                  else [])),
                        inc=(t_dn if (b32 == NIC - 1 and hf == 1) else None))
            p.step("vector", lambda e, i=i, pb=pb: e.scalar_tensor_tensor(
                out=yst[:, pb * D:(pb + 1) * D], in0=psD[pb][:],
                scalar=dscc[:, i:i + 1], in1=r2b[:, pb * D:(pb + 1) * D],
                op0=ALU.mult, op1=ALU.add),
                waits=[(t_dn, i + 1), (dr2b, 16 * (i + 1)), dyo])
            y_evac_v[i] = vser.n
            p.dma("gpsimd", lambda e, i=i, pb=pb: e.dma_start(
                out=y[i * 128:(i + 1) * 128, :],
                in_=yst[:, pb * D:(pb + 1) * D]),
                waits=[vser], inc=dyo)

    p.step("sync", None, waits=[dbg_done, dxh, dwq, dtrx, dxo, dxp,
                                dsg, dcv, (dyo, 16 * NTT), dwl, dqw, dal,
                                dwo, dxr, dr2, dwgl, dwul, dx2q,
                                dtx2, dhq, dwdl, dr2b])

    block = es.enter_context(nc.Block())
    p.emit(block)
    es.close()
    free_all()
    return nc


def make_inputs(inputs):
    x = np.ascontiguousarray(np.asarray(inputs["L0_attn_x"], dtype=np.float32)
                             .reshape(4 * 4096, D))
    wqkv_a = np.ascontiguousarray(np.asarray(inputs["L0_attn_W_qkv"], np.float32))
    wout_a = np.ascontiguousarray(np.asarray(inputs["L0_attn_W_out"], np.float32))
    wgT = np.ascontiguousarray(np.asarray(inputs["L0_ffn_w_gate"], np.float32).T)
    wuT = np.ascontiguousarray(np.asarray(inputs["L0_ffn_w_up"], np.float32).T)
    wdT = np.ascontiguousarray(np.asarray(inputs["L0_ffn_w_down"], np.float32).T)
    cst, cbf, cmask = host_consts()
    crow = np.ones((1, 128), dtype=np.float32)
    zeros = np.zeros((TOK, D), dtype=np.float32)
    in_maps = []
    for c in range(8):
        xo = x[c * TOK:(c + 1) * TOK]
        xp = zeros if c % 2 == 0 else x[(c - 1) * TOK:c * TOK]
        in_maps.append(dict(x_own=np.ascontiguousarray(xo),
                            x_prev=np.ascontiguousarray(xp),
                            wqkv=wqkv_a, wout=wout_a, wgT=wgT, wuT=wuT,
                            wdT=wdT, cst=cst, cbf=cbf, cmask=cmask,
                            crow=crow))
    return in_maps


def kernel(**inputs):
    in_maps = make_inputs(inputs)
    nc = build_program()
    res = run_bass_kernel_spmd(nc, in_maps, core_ids=list(range(8)))
    out = np.concatenate([r["y"] for r in res.results], axis=0)
    return out.reshape(4, 4096, D)


# version 10
# speedup vs baseline: 1.7162x; 1.0672x over previous
import os as _os
import numpy as np
import ml_dtypes
import concourse.bass as bass
import concourse.mybir as mybir
from concourse.bass_utils import run_bass_kernel_spmd

F32 = mybir.dt.float32
BF16 = mybir.dt.bfloat16
I32 = mybir.dt.int32
ALU = mybir.AluOpType
AF = mybir.ActivationFunctionType
AX = mybir.AxisListType

TOK = 2048
D = 1024
NB = 8
NTT = TOK // 128
CS = 64
NCH = TOK // CS
H = 16
HD = 64
SCALE = HD ** -0.5
INNER = 4096
NIC = INNER // 128
NEG = -65000.0

CONSTS = [0.0, 1.0, 0.75, 0.5, 1e-5, 1.0 / 1024.0, -0.5, SCALE,
          -SCALE / 64.0, 127.0, 256.0, -256.0, 1.0 / 128.0, -1.0,
          9.313225746154785e-10, 12582912.0, 6755399441055744.0]


class Sem:
    def __init__(self, es, nc, name):
        self.h = es.enter_context(nc.semaphore(name))
        self.n = 0


class Prog:

    def __init__(self):
        self.steps = []
        self.ser = {}
        self.selfwait = {"vector": True, "gpsimd": True, "scalar": True}

    def set_serial(self, eng, sem):
        self.ser[eng] = sem

    def step(self, eng, fn, waits=(), inc=None, inck=1, is_dma=False):
        ws = []
        for w in waits:
            if isinstance(w, Sem):
                ws.append((w, w.n))
            else:
                ws.append((w[0], w[1]))
        auto = None
        ser = None
        if not is_dma and eng in self.ser:
            s = self.ser[eng]
            if self.selfwait.get(eng):
                auto = (s, s.n)
            s.n += 1
            ser = s
        self.steps.append((eng, fn, ws, inc, inck, auto, ser))
        if inc is not None:
            inc.n += inck

    def dma(self, eng, fn, waits=(), inc=None):
        self.step(eng, fn, waits=waits, inc=inc, inck=16, is_dma=True)

    def emit(self, block):
        def mk(engname):
            def body(eng):
                for (e, fn, ws, inc, inck, auto, ser) in self.steps:
                    if e != engname:
                        continue
                    if auto is not None and auto[1] > 0:
                        eng.wait_ge(auto[0].h, auto[1])
                    for (s, v) in ws:
                        if v > 0:
                            eng.wait_ge(s.h, v)
                    inst = fn(eng) if fn is not None else None
                    if inst is not None:
                        if ser is not None:
                            inst.then_inc(ser.h, 1)
                        if inc is not None:
                            inst.then_inc(inc.h, inck)
            return body

        block.sync(mk("sync"))
        block.vector(mk("vector"))
        block.scalar(mk("scalar"))
        block.tensor(mk("tensor"))
        block.gpsimd(mk("gpsimd"))


def host_consts():
    cst = np.tile(np.array(CONSTS, dtype=np.float32), (128, 1))
    idn = np.eye(128, dtype=np.float32).astype(ml_dtypes.bfloat16)
    lind = np.zeros((128, 8 * 16), dtype=ml_dtypes.bfloat16)
    for j in range(8):
        lind[0:64, 16 * j + 2 * j] = 1.0
        lind[64:128, 16 * j + 2 * j + 1] = 1.0
    cvind = np.zeros((128, 16 * 32), dtype=ml_dtypes.bfloat16)
    for tt in range(16):
        cvind[0:64, 32 * tt + 2 * tt] = 1.0 / 128.0
        cvind[64:128, 32 * tt + 2 * tt + 1] = 1.0 / 128.0
    tri = np.zeros((128, 32), dtype=ml_dtypes.bfloat16)
    tri[0:32, :] = np.tril(np.ones((32, 32)), -1).T.astype(ml_dtypes.bfloat16)
    ones_col = np.ones((128, 1), dtype=ml_dtypes.bfloat16)
    ones_row32 = np.zeros((128, 32), dtype=ml_dtypes.bfloat16)
    ones_row32[0, :] = 1.0
    ones_row128 = np.zeros((128, 128), dtype=ml_dtypes.bfloat16)
    ones_row128[0, :] = 1.0
    mstk = np.zeros((128, 128), dtype=ml_dtypes.bfloat16)
    mstk[0:64, 0:64] = np.eye(64)
    mstk[0:64, 64:128] = np.eye(64)
    m64b = np.zeros((128, 64), dtype=ml_dtypes.bfloat16)
    m64b[0:64, :] = np.where(np.triu(np.ones((64, 64), dtype=bool), 1),
                             NEG, 0.0)
    cbf = np.concatenate([idn, lind, cvind, tri, ones_col, ones_row32,
                          ones_row128, mstk, m64b], axis=1)
    m64 = np.where(np.triu(np.ones((64, 64), dtype=bool), 1), NEG, 0.0)
    blk = np.concatenate([m64, m64], axis=0)
    cmask = np.tile(blk, (1, 16)).astype(np.float32)
    return cst, cbf, cmask


CB_IDN = 0
CB_LIND = 128
CB_CVIND = 256
CB_TRI = 768
CB_ONESC = 800
CB_ONESR32 = 801
CB_ONESR128 = 833
CB_MSTK = 961
CB_M64 = 1089


def build_program(dbg=()):
    nc = bass.Bass("TRN2", target_bir_lowering=False, debug=False)

    x_own = nc.dram_tensor("x_own", [TOK, D], F32, kind="ExternalInput").ap()
    x_prev = nc.dram_tensor("x_prev", [TOK, D], F32, kind="ExternalInput").ap()
    wqkv = nc.dram_tensor("wqkv", [D, 3 * D], F32, kind="ExternalInput").ap()
    wout = nc.dram_tensor("wout", [D, D], F32, kind="ExternalInput").ap()
    wgT = nc.dram_tensor("wgT", [D, INNER], F32, kind="ExternalInput").ap()
    wuT = nc.dram_tensor("wuT", [D, INNER], F32, kind="ExternalInput").ap()
    wdT = nc.dram_tensor("wdT", [INNER, D], F32, kind="ExternalInput").ap()
    cst_d = nc.dram_tensor("cst", [128, len(CONSTS)], F32, kind="ExternalInput").ap()
    cmask_d = nc.dram_tensor("cmask", [128, 1024], F32, kind="ExternalInput").ap()
    cbf_d = nc.dram_tensor("cbf", [128, 1153], BF16, kind="ExternalInput").ap()

    y = nc.dram_tensor("y", [TOK, D], F32, kind="ExternalOutput").ap()

    xhat_d = nc.dram_tensor("xhat_d", [TOK, D], BF16, kind="Internal").ap()
    a_d = nc.dram_tensor("a_d", [D, TOK], BF16, kind="Internal").ap()
    x2q_d = nc.dram_tensor("x2q_d", [TOK, D], BF16, kind="Internal").ap()
    hq_d = nc.dram_tensor("hq_d", [TOK, INNER], BF16, kind="Internal").ap()
    wgq_d = nc.dram_tensor("wgq_d", [D, INNER], BF16, kind="Internal").ap()
    wuq_d = nc.dram_tensor("wuq_d", [D, INNER], BF16, kind="Internal").ap()
    wdq_d = nc.dram_tensor("wdq_d", [INNER, D], BF16, kind="Internal").ap()
    r2_d = nc.dram_tensor("r2_d", [TOK, D], F32, kind="Internal").ap()
    crow_d = nc.dram_tensor("crow", [1, 128], F32, kind="ExternalInput").ap()

    dbg_d = {}
    for name, shape, dt in [
        ("dbg_xhat", [128, 16384], F32), ("dbg_q", [128, 16384], F32),
        ("dbg_k", [128, 16384], F32), ("dbg_v", [128, 16384], F32),
        ("dbg_c0k", [128, 8], F32), ("dbg_c0v", [1, D], F32),
        ("dbg_a", [128, 16384], F32), ("dbg_sig", [16, TOK], F32),
        ("dbg_cvx", [32, D], F32), ("dbg_stats", [128, 96], F32),
        ("dbg_xhd", [TOK, D], BF16),
    ]:
        if name in dbg:
            dbg_d[name] = nc.dram_tensor(name, shape, dt, kind="ExternalOutput").ap()

    import contextlib
    es = contextlib.ExitStack()
    galloc = []

    def _alloc(g, name):
        galloc.append((name, g))
        return g.__enter__()

    def sb(name, shape, dt):
        return _alloc(nc.sbuf_tensor(name, shape, dt), name)

    def ps(name, shape, dt=F32):
        return _alloc(nc.psum_tensor(name, shape, dt), name)

    def free(*names):
        want = set(names)
        for _ in names:
            nm, g = galloc.pop()
            assert nm in want, f"non-LIFO free: {nm} vs {want}"
            g.__exit__(None, None, None)

    def free_all():
        while galloc:
            galloc.pop()[1].__exit__(None, None, None)

    p = Prog()
    sems = {}

    def S(name):
        if name not in sems:
            sems[name] = Sem(es, nc, name)
        return sems[name]

    def barrier(nm, waits):
        bsem = S(nm)
        p.step("sync", lambda e: e.nop(), waits=waits, inc=bsem, is_dma=True,
               inck=1)
        return bsem

    vser = S("vser")
    aser = S("aser")
    pser = S("pser")
    p.set_serial("vector", vser)
    p.set_serial("scalar", aser)
    p.set_serial("gpsimd", pser)

    cst = sb("cst_sb", [128, len(CONSTS)], F32)
    for i, v in enumerate(CONSTS):
        nc.const_aps.register(F32, v, cst[:, i:i + 1])
    cbf = sb("cbf_sb", [128, 1153], BF16)

    sxcol = sb("sxcol", [128, 8], BF16)
    sxcol05 = sb("sxcol05", [128, 8], BF16)
    c0k = sb("c0k", [128, 8], F32)
    c0vrow = sb("c0vrow", [1, D], BF16)
    bcast_b = sb("bcast_b", [128, 1], F32)
    det_i = sb("det_i", [128, 1], I32)

    sumx = sb("sumx", [128, NTT], F32)
    sumsq = sb("sumsq", [128, NTT], F32)
    mcol = sb("mcol", [128, NTT], F32)
    rstd = sb("rstd", [128, NTT], F32)
    bcol = sb("bcol", [128, NTT], F32)
    sumxp = sb("sumxp", [128, NTT], F32)
    sumsqp = sb("sumsqp", [128, NTT], F32)
    mcolp = sb("mcolp", [128, NTT], F32)
    rstdp = sb("rstdp", [128, NTT], F32)
    rstdpb = sb("rstdpb", [128, NTT], BF16)
    rmb = sb("rmb", [128, NTT], BF16)
    msqt_o = sb("msqt_o", [128, NTT], F32)
    lnt_o = sb("lnt_o", [128, NTT], F32)
    msqt_p = sb("msqt_p", [128, NTT], F32)
    lnt_p = sb("lnt_p", [128, NTT], F32)
    scsb = sb("scsb", [1, 1], F32)
    sxrow = sb("sxrow", [1, D], BF16)
    crow = sb("crow_sb", [1, 128], F32)
    shiftb = sb("shiftb", [128, 1], F32)
    mgc = sb("mgc", [128, 1], F32)
    nmgc = sb("nmgc", [128, 1], F32)
    wmcol = sb("wmcol", [128, 3], F32)
    thrcol = sb("thrcol", [128, 3], F32)
    nthrcol = sb("nthrcol", [128, 3], F32)
    wsc = sb("wsc", [1, 8], F32)
    m2c = sb("m2c", [128, NTT], F32)
    sq2c = sb("sq2c", [128, NTT], F32)
    msq2c = sb("msq2c", [128, NTT], F32)
    ln2c = sb("ln2c", [128, NTT], F32)
    rstd2c = sb("rstd2c", [128, NTT], F32)
    bc2c = sb("bc2c", [128, NTT], F32)
    s1c = sb("s1c", [128, NTT], F32)
    rs1c = sb("rs1c", [128, NTT], F32)
    s2c = sb("s2c", [128, NTT], F32)
    gscc = sb("gscc", [128, NTT], F32)
    uscc = sb("uscc", [128, NTT], F32)
    dscc = sb("dscc", [128, NTT], F32)
    amc = sb("amc", [128, NTT], F32)

    Q = sb("Q", [128, NB * TOK], BF16)
    K = sb("K", [128, NB * TOK], BF16)
    V = sb("V", [128, NTT * D], BF16)
    xhat = sb("xhat", [128, NB * TOK], BF16)
    wqk_sb = sb("wqk_sb", [128, NB * 2048], BF16)
    wv_sb = sb("wv_sb", [128, NB * D], BF16)
    xin = sb("xin", [128, D], F32)
    xpin = sb("xpin", [128, D], F32)
    xpbf = sb("xpbf", [128, D], BF16)
    xhst = sb("xhst", [128, 2 * D], BF16)
    sqs = sb("sqs", [128, D], F32)

    dxo = S("dxo")
    dxp = S("dxp")
    dcst = S("dcst")
    dxh = S("dxh")
    dtrx = S("dtrx")
    dwq = S("dwq")
    t_sx = S("t_sx")
    t_c0 = S("t_c0")
    t_v = S("t_v")
    t_qk = S("t_qk")
    dbg_done = S("dbg_done")

    p.dma("sync", lambda e: e.dma_start(out=cst[:], in_=cst_d), inc=dcst)
    p.dma("sync", lambda e: e.dma_start(out=cbf[:], in_=cbf_d), inc=dcst)
    p.dma("sync", lambda e: e.dma_start(out=crow[:], in_=crow_d), inc=dcst)
    p.step("vector", lambda e: e.tensor_single_scalar(
        out=det_i[:], in_=cst[:, 2:3], scalar=0.0, op=ALU.add),
        waits=[dcst])
    p.step("vector", lambda e: e.tensor_scalar(
        out=bcast_b[:], in0=det_i[:], scalar1=-0.5, scalar2=0.5,
        op0=ALU.mult, op1=ALU.add))
    p.step("vector", lambda e: e.tensor_single_scalar(
        out=shiftb[:], in_=bcast_b[:], scalar=256.0, op=ALU.add))
    p.step("vector", lambda e: e.tensor_single_scalar(
        out=mgc[:], in_=cst[:, 1:2], scalar=9.313225746154785e-10,
        op=ALU.add), waits=[dcst])
    p.step("vector", lambda e: e.tensor_scalar(
        out=mgc[:], in0=mgc[:], scalar1=1.0, scalar2=1073741824.0,
        op0=ALU.subtract, op1=ALU.mult))
    p.step("vector", lambda e: e.tensor_scalar(
        out=mgc[:], in0=mgc[:],
        scalar1=6755399441055744.0 - 12582912.0, scalar2=12582912.0,
        op0=ALU.mult, op1=ALU.add))
    p.step("vector", lambda e: e.tensor_single_scalar(
        out=nmgc[:], in_=mgc[:], scalar=-1.0, op=ALU.mult))

    for b in range(NB):
        p.dma("gpsimd", lambda e, b=b: e.dma_start(
            out=wqk_sb[:, b * 2048:(b + 1) * 2048],
            in_=wqkv[b * 128:(b + 1) * 128, 0:2048]), inc=dwq)
        p.dma("gpsimd", lambda e, b=b: e.dma_start(
            out=wv_sb[:, b * D:(b + 1) * D],
            in_=wqkv[b * 128:(b + 1) * 128, 2048:3072]), inc=dwq)

    psx = ps("psx", [1, D])
    psc = ps("psc", [1, 1])
    dve_done_own = [0] * NTT
    dve_done_prev = [0] * NTT
    act_done_own = [0] * NTT
    act_done_prev = [0] * NTT
    pe_done_prev = [0] * NTT

    def ln_tile(i, src_d, buf, dsem, sumx_, sumsq_, m_, rstd_, msqt, lnt,
                dma_gate):
        bi = 0
        p.dma("sync", lambda e: e.dma_start(
            out=buf[:, bi:bi + D], in_=src_d[i * 128:(i + 1) * 128, :]),
            waits=dma_gate(i), inc=dsem)
        p.step("scalar", lambda e: e.activation(
            out=sqs[:], in_=buf[:, bi:bi + D], func=AF.Square,
            accum_out=sumsq_[:, i:i + 1]),
            waits=[(dsem, 16 * (i + 1)), dcst])
        p.step("vector", lambda e: e.reduce_sum(
            out=sumx_[:, i:i + 1], in_=buf[:, bi:bi + D], axis=AX.X),
            waits=[(dsem, 16 * (i + 1))])
        p.step("vector", lambda e: e.tensor_single_scalar(
            out=m_[:, i:i + 1], in_=sumx_[:, i:i + 1],
            scalar=1.0 / 1024.0, op=ALU.mult))
        p.step("vector", lambda e: e.tensor_tensor(
            out=msqt[:, i:i + 1], in0=m_[:, i:i + 1], in1=m_[:, i:i + 1],
            op=ALU.mult))
        p.step("vector", lambda e: e.scalar_tensor_tensor(
            out=sumsq_[:, i:i + 1], in0=sumsq_[:, i:i + 1],
            scalar=1.0 / 1024.0, in1=msqt[:, i:i + 1],
            op0=ALU.mult, op1=ALU.subtract), waits=[aser])
        p.step("scalar", lambda e: e.activation(
            out=lnt[:, i:i + 1], in_=sumsq_[:, i:i + 1], func=AF.Ln,
            bias=1e-5), waits=[vser])
        p.step("scalar", lambda e: e.activation(
            out=rstd_[:, i:i + 1], in_=lnt[:, i:i + 1], func=AF.Exp,
            scale=-0.5))

    for i in range(NTT):
        ln_tile(i, x_own, xin, dxo, sumx, sumsq, mcol, rstd, msqt_o, lnt_o,
                lambda j: [] if j < 1 else [(vser, dve_done_own[j - 1]),
                                            (aser, act_done_own[j - 1])])
        p.step("vector", lambda e, i=i: e.scalar_tensor_tensor(
            out=bcol[:, i:i + 1], in0=mcol[:, i:i + 1], scalar=-1.0,
            in1=rstd[:, i:i + 1], op0=ALU.mult, op1=ALU.mult),
            waits=[aser])
        dve_done_own[i] = vser.n
        p.step("scalar", lambda e, i=i, bi=(i % 2) * D: e.activation(
            out=xhst[:, bi:bi + D], in_=xin[:, 0:D], func=AF.Identity,
            scale=rstd[:, i:i + 1], bias=bcol[:, i:i + 1]),
            waits=[vser] + ([] if i < 2 else [(dxh, 16 * i)]))
        act_done_own[i] = aser.n
        p.dma("gpsimd", lambda e, i=i, bi=(i % 2) * D: e.dma_start(
            out=xhat_d[i * 128:(i + 1) * 128, :], in_=xhst[:, bi:bi + D]),
            waits=[(aser, aser.n)], inc=dxh)

        ln_tile(i, x_prev, xpin, dxp, sumxp, sumsqp, mcolp, rstdp, msqt_p,
                lnt_p,
                lambda j: [] if j < 1 else [(vser, dve_done_prev[j - 1]),
                                            (aser, act_done_prev[j - 1])])
        p.step("vector", lambda e, i=i: e.tensor_single_scalar(
            out=rstdpb[:, i:i + 1], in_=rstdp[:, i:i + 1], scalar=0.0,
            op=ALU.add), waits=[aser])
        p.step("vector", lambda e, i=i: e.scalar_tensor_tensor(
            out=rmb[:, i:i + 1], in0=mcolp[:, i:i + 1], scalar=1.0,
            in1=rstdp[:, i:i + 1], op0=ALU.mult, op1=ALU.mult))
        dve_done_prev[i] = vser.n
        p.step("scalar", lambda e, i=i: e.activation(
            out=xpbf[:, 0:D], in_=xpin[:, 0:D], func=AF.Identity),
            waits=([] if i < 1 else [(t_sx, pe_done_prev[i - 1])]))
        act_done_prev[i] = aser.n
        for hf in range(2):
            p.step("tensor", lambda e, i=i, hf=hf: e.matmul(
                out=psx[:, hf * 512:(hf + 1) * 512],
                lhsT=rstdpb[:, i:i + 1],
                rhs=xpbf[:, hf * 512:(hf + 1) * 512],
                start=(i == 0), stop=(i == NTT - 1)),
                waits=[vser, aser])
        p.step("tensor", lambda e, i=i: e.matmul(
            out=psc[:], lhsT=rmb[:, i:i + 1],
            rhs=cbf[:, CB_ONESC:CB_ONESC + 1],
            start=(i == 0), stop=(i == NTT - 1)),
            waits=[(dcst, 48)], inc=t_sx)
        pe_done_prev[i] = t_sx.n

    for b in range(NB):
        p.dma("sync", lambda e, b=b: e.dma_start(
            out=xhat[:, b * TOK:(b + 1) * TOK],
            in_=xhat_d[:, b * 128:(b + 1) * 128], transpose=True),
            waits=[(dxh, 16 * NTT)], inc=dtrx)

    p.step("vector", lambda e: e.tensor_copy(out=scsb[:], in_=psc[:]),
           waits=[(t_sx, NTT)])
    p.step("vector", lambda e: e.tensor_scalar(
        out=sxrow[:], in0=psx[:], scalar1=scsb[0:1, 0:1], scalar2=None,
        op0=ALU.subtract))
    v_after_sxrow = vser.n
    ptc = ps("ptc", [128, 16], BF16)
    for b in range(NB):
        p.step("tensor", lambda e, b=b: e.transpose(
            out=ptc[:, 2 * b:2 * b + 1],
            in_=sxrow[0:1, b * 128:(b + 1) * 128],
            identity=cbf[0:1, CB_ONESC:CB_ONESC + 1]),
            waits=[(vser, v_after_sxrow), (dcst, 48)],
            inc=(t_sx if b == NB - 1 else None))
    p.step("vector", lambda e: e.tensor_copy(out=sxcol[:], in_=ptc[:, 0:16:2]),
           waits=[(t_sx, NTT + 1)])
    p.step("vector", lambda e: e.tensor_single_scalar(
        out=sxcol05[:], in_=ptc[:, 0:16:2], scalar=1.0 / 128.0, op=ALU.mult))
    v_after_sxcol = vser.n

    pc0k = ps("pc0k", [128, 8])
    for kc in range(8):
        for b in range(NB):
            p.step("tensor", lambda e, b=b, kc=kc: e.matmul(
                out=pc0k[:, kc:kc + 1],
                lhsT=wqk_sb[:, b * 2048 + 1024 + kc * 128:
                            b * 2048 + 1024 + (kc + 1) * 128],
                rhs=sxcol[:, b:b + 1],
                start=(b == 0), stop=(b == NB - 1)),
                waits=[(vser, v_after_sxcol), (dwq, 16 * 2 * NB)],
                inc=(t_c0 if (b == NB - 1 and kc == 7) else None))
    pc0v = ps("pc0v", [1, D])
    for b in range(NB):
        for hf in range(2):
            p.step("tensor", lambda e, b=b, hf=hf: e.matmul(
                out=pc0v[:, hf * 512:(hf + 1) * 512],
                lhsT=sxcol05[:, b:b + 1],
                rhs=wv_sb[:, b * D + hf * 512: b * D + (hf + 1) * 512],
                start=(b == 0), stop=(b == NB - 1)),
                inc=(t_c0 if (b == NB - 1 and hf == 1) else None))
    p.step("vector", lambda e: e.tensor_copy(out=c0k[:], in_=pc0k[:]),
           waits=[(t_c0, 2)])
    p.step("vector", lambda e: e.tensor_copy(out=c0vrow[:], in_=pc0v[:]))
    v_carry_done = vser.n

    free("psx", "psc", "ptc", "pc0k", "pc0v")
    free("xin", "xpin", "xpbf", "xhst", "sqs")
    pvv = [ps("pvv0", [128, D]), ps("pvv1", [128, D])]
    ev_done = [0] * NTT
    for tt in range(NTT):
        pb = tt % 2
        for b in range(NB):
            for hf in range(2):
                p.step("tensor", lambda e, tt=tt, b=b, hf=hf, pb=pb: e.matmul(
                    out=pvv[pb][:, hf * 512:(hf + 1) * 512],
                    lhsT=xhat[:, b * TOK + tt * 128: b * TOK + (tt + 1) * 128],
                    rhs=wv_sb[:, b * D + hf * 512: b * D + (hf + 1) * 512],
                    start=(b == 0), stop=(b == NB - 1)),
                    waits=(([(dtrx, 16 * NB), (vser, v_carry_done)]
                            if b == 0 and hf == 0 else [])
                           + ([(vser, ev_done[tt - 2])]
                              if (tt >= 2 and b == 0 and hf == 0) else [])),
                    inc=(t_v if (b == NB - 1 and hf == 1) else None))
        p.step("vector", lambda e, tt=tt, pb=pb: e.tensor_copy(
            out=V[:, tt * D:(tt + 1) * D], in_=pvv[pb][:]),
            waits=[(t_v, tt + 1)])
        ev_done[tt] = vser.n

    eqk_done = [0] * 64
    pqk = [ps("pqk0", [128, 512]), ps("pqk1", [128, 512])]
    n_qk = 0
    for tg in range(4):
        for oc in range(16):
            pb = n_qk % 2
            for b in range(NB):
                p.step("tensor", lambda e, oc=oc, b=b, tg=tg, pb=pb: e.matmul(
                    out=pqk[pb][:],
                    lhsT=wqk_sb[:, b * 2048 + oc * 128:
                                b * 2048 + (oc + 1) * 128],
                    rhs=xhat[:, b * TOK + tg * 512: b * TOK + (tg + 1) * 512],
                    start=(b == 0), stop=(b == NB - 1)),
                    waits=([(vser if pb == 0 else aser,
                             eqk_done[n_qk - 2])] if (n_qk >= 2 and b == 0)
                           else []),
                    inc=(t_qk if b == NB - 1 else None))
            dst = Q if oc < 8 else K
            dj = oc % 8
            if pb == 0:
                p.step("vector", lambda e, dst=dst, dj=dj, tg=tg, pb=pb:
                       e.tensor_copy(
                           out=dst[:, dj * TOK + tg * 512:
                                   dj * TOK + (tg + 1) * 512],
                           in_=pqk[pb][:]), waits=[(t_qk, n_qk + 1)])
                eqk_done[n_qk] = vser.n
            else:
                p.step("scalar", lambda e, dst=dst, dj=dj, tg=tg, pb=pb:
                       e.activation(
                           out=dst[:, dj * TOK + tg * 512:
                                   dj * TOK + (tg + 1) * 512],
                           in_=pqk[pb][:], func=AF.Identity),
                       waits=[(t_qk, n_qk + 1)])
                eqk_done[n_qk] = aser.n
            n_qk += 1
    v_p1_done = vser.n
    a_p1_done = aser.n


    free("pqk0", "pqk1", "pvv0", "pvv1")
    if "dbg_xhat" in dbg_d:
        free("wv_sb")
    else:
        free("wv_sb", "wqk_sb", "xhat")

    t_cv = S("t_cv")
    t_log = S("t_log")
    t_att = S("t_att")
    dsg = S("dsg")
    dcv = S("dcv")
    p2rdy = barrier("p2rdy", [(t_qk, 64), (dtrx, 16 * NB), (dwq, 16 * 2 * NB),
                              (dxh, 16 * NTT), (dxo, 16 * NTT),
                              (dxp, 16 * NTT), (t_v, NTT), (t_sx, NTT + 1),
                              (t_c0, 2), (vser, v_p1_done),
                              (aser, a_p1_done)])

    CK = sb("CK", [128, NCH * 8], F32)
    CKcum = sb("CKcum", [128, NCH * 8], F32)
    CVsb = sb("CVsb", [32, D], BF16)
    CVXsb = sb("CVXsb", [32, D], BF16)
    sgm = sb("sgm", [16, TOK], F32)
    sgfl = sb("sgfl", [1, 2 * 1024], BF16)
    cvfl = sb("cvfl", [1, 2 * 1024], BF16)
    Sm = sb("Sm", [128, 1024], F32)
    Pn = sb("Pn", [128, 1024], BF16)
    PTsb = sb("PTsb", [128, 1024], BF16)
    mx = sb("mx", [128, 16], F32)
    rs = sb("rs", [128, 16], F32)
    rsi = sb("rsi", [128, 16], F32)
    ast = sb("ast", [128, 2 * 512], BF16)
    da = S("da")
    cmask = sb("cmask_sb", [128, 1024], F32)
    p.dma("sync", lambda e: e.dma_start(out=cmask[:], in_=cmask_d),
          waits=[(p2rdy, 1)], inc=dcst)
    CKB = sb("CKB", [128, TOK], BF16)
    Mb = sb("Mb", [128, TOK], BF16)

    for b in range(NB):
        p.step("vector", lambda e, b=b: e.reduce_sum(
            out=CK[:, b:NCH * 8:8],
            in_=K[:, b * TOK:(b + 1) * TOK].rearrange(
                "p (c x) -> p c x", c=NCH), axis=AX.X),
            waits=([(vser, v_p1_done), (aser, a_p1_done), (p2rdy, 1)]
                   if b == 0 else []))
    p.step("vector", lambda e: e.tensor_copy(out=CKcum[:, 0:8], in_=c0k[:]))
    for c in range(1, NCH):
        p.step("vector", lambda e, c=c: e.tensor_tensor(
            out=CKcum[:, c * 8:(c + 1) * 8],
            in0=CKcum[:, (c - 1) * 8:c * 8],
            in1=CK[:, (c - 1) * 8:c * 8], op=ALU.add))
    v_ckcum = vser.n

    pcv = ps("pcv", [32, D])
    for tt in range(NTT):
        for hf in range(2):
            p.step("tensor", lambda e, tt=tt, hf=hf: e.matmul(
                out=pcv[:, hf * 512:(hf + 1) * 512],
                lhsT=cbf[:, CB_CVIND + tt * 32:CB_CVIND + (tt + 1) * 32],
                rhs=V[:, tt * D + hf * 512:tt * D + (hf + 1) * 512],
                start=(tt == 0), stop=(tt == NTT - 1)),
                waits=([(t_v, NTT), (p2rdy, 1), (vser, v_p1_done),
                        (aser, a_p1_done)] if tt == 0 and hf == 0 else []),
                inc=(t_cv if (tt == NTT - 1 and hf == 1) else None))
    p.step("vector", lambda e: e.tensor_copy(out=CVsb[:], in_=pcv[:]),
           waits=[(t_cv, 1)])
    v_cvsb = vser.n
    pcvx = ps("pcvx", [32, D])
    for hf in range(2):
        p.step("tensor", lambda e, hf=hf: e.matmul(
            out=pcvx[:, hf * 512:(hf + 1) * 512],
            lhsT=cbf[0:32, CB_TRI:CB_TRI + 32],
            rhs=CVsb[:, hf * 512:(hf + 1) * 512],
            start=True, stop=False),
            waits=[(vser, v_cvsb)])
        p.step("tensor", lambda e, hf=hf: e.matmul(
            out=pcvx[:, hf * 512:(hf + 1) * 512],
            lhsT=cbf[0:1, CB_ONESR32:CB_ONESR32 + 32],
            rhs=c0vrow[0:1, hf * 512:(hf + 1) * 512],
            start=False, stop=True),
            waits=[(vser, v_carry_done)],
            inc=(t_cv if hf == 1 else None))
    p.step("vector", lambda e: e.tensor_copy(out=CVXsb[:], in_=pcvx[:]),
           waits=[(t_cv, 2)])
    v_cvx = vser.n
    free("pcv", "pcvx")

    plog = [ps("plog%d" % g, [16, 512]) for g in range(4)]
    mdone = [0] * NB
    for b in range(NB):
        p.step("vector", lambda e, b=b: e.tensor_copy(
            out=CKB[:],
            in_=CKcum[:, b:NCH * 8:8].unsqueeze(2)
                .broadcast_to([128, NCH, CS])),
            waits=([(t_log, t_log.n)] if b > 0 else []))
        p.step("vector", lambda e, b=b: e.tensor_tensor(
            out=Mb[:], in0=Q[:, b * TOK:(b + 1) * TOK], in1=CKB[:],
            op=ALU.mult))
        mdone[b] = vser.n
        for g in range(4):
            p.step("tensor", lambda e, b=b, g=g: e.matmul(
                out=plog[g][:],
                lhsT=cbf[:, CB_LIND + b * 16:CB_LIND + (b + 1) * 16],
                rhs=Mb[:, g * 512:(g + 1) * 512],
                start=(b == 0), stop=(b == NB - 1)),
                waits=[(vser, mdone[b])],
                inc=(t_log if g == 3 else None))
    for g in range(4):
        p.step("scalar", lambda e, g=g: e.activation(
            out=sgm[:, g * 512:(g + 1) * 512], in_=plog[g][:], func=AF.Exp,
            scale=-SCALE / 64.0), waits=[(t_log, NB), (p2rdy, 1)])
    p.step("vector", lambda e: e.tensor_single_scalar(
        out=sgm[:], in_=sgm[:], scalar=1.0, op=ALU.add), waits=[aser])
    p.step("vector", lambda e: e.reciprocal(out=sgm[:], in_=sgm[:]))
    v_sig = vser.n
    free("plog0", "plog1", "plog2", "plog3")
    free("Mb", "CKB")

    dwl = S("dwl")
    dqw = S("dqw")
    t_wq = S("t_wq")
    wstg = sb("wstg", [128, INNER], F32)
    abuf = sb("abuf", [128, INNER], BF16)
    wabs = sb("wabs", [128, 27], F32)
    pwq = ps("pwq", [128, 8])
    pw1 = pwq[:, 0:4]
    pwt = pwq[0:1, 4:8]
    wsrc = [(wgT, wgq_d, 0), (wuT, wuq_d, 1), (wdT, wdq_d, 2)]

    def wtile_aps(mat, t):
        srcT, dstT, mi = wsrc[mat]
        if mi < 2:
            return (srcT[t * 128:(t + 1) * 128, :],
                    dstT[t * 128:(t + 1) * 128, :])
        return (srcT[t * 512:(t + 1) * 512, :].rearrange(
                    "(a p) f -> p a f", p=128),
                dstT[t * 512:(t + 1) * 512, :].rearrange(
                    "(a p) f -> p a f", p=128))

    def wq_p1_tile(mat, t, eng="gpsimd"):
        p.dma("gpsimd", lambda e, mat=mat, t=t: e.dma_start(
            out=(wstg[:].rearrange("p (a f) -> p a f", a=4) if mat == 2
                 else wstg[:]),
            in_=wtile_aps(mat, t)[0]),
            waits=[pser, vser, aser], inc=dwl)
        p.step("scalar", lambda e, mat=mat, t=t: e.activation(
            out=wstg[:], in_=wstg[:], func=AF.Abs,
            accum_out=wabs[:, mat * 9 + t:mat * 9 + t + 1]), waits=[dwl])

    def wq_thr(mat):
        p.step("vector", lambda e, mat=mat: e.reduce_sum(
            out=wabs[:, mat * 9 + 8:mat * 9 + 9],
            in_=wabs[:, mat * 9:mat * 9 + 8], axis=AX.X),
            waits=[aser, pser])
        vv = vser.n
        p.step("tensor", lambda e, mat=mat: e.matmul(
            out=pwt[0:1, mat:mat + 1], lhsT=wabs[:, mat * 9 + 8:mat * 9 + 9],
            rhs=cst[:, 1:2],
            start=True, stop=True), waits=[(vser, vv)], inc=t_wq)
        p.step("vector", lambda e, mat=mat: e.tensor_scalar(
            out=wsc[0:1, mat:mat + 1], in0=pwt[0:1, mat:mat + 1],
            scalar1=1.0 / (D * INNER), scalar2=1e-5,
            op0=ALU.mult, op1=ALU.max), waits=[(t_wq, t_wq.n)])
        p.step("vector", lambda e, mat=mat: e.tensor_single_scalar(
            out=wsc[0:1, 3 + mat:4 + mat], in_=wsc[0:1, mat:mat + 1],
            scalar=0.5, op=ALU.mult))
        vv2 = vser.n
        p.step("tensor", lambda e, mat=mat: e.matmul(
            out=pw1[:, mat:mat + 1], lhsT=crow[:],
            rhs=wsc[0:1, mat:mat + 1],
            start=True, stop=True), waits=[(vser, vv2)], inc=t_wq)
        p.step("vector", lambda e, mat=mat: e.tensor_copy(
            out=wmcol[:, mat:mat + 1], in_=pw1[:, mat:mat + 1]),
            waits=[(t_wq, t_wq.n)])
        p.step("tensor", lambda e, mat=mat: e.matmul(
            out=pw1[:, 3:4], lhsT=crow[:], rhs=wsc[0:1, 3 + mat:4 + mat],
            start=True, stop=True), waits=[vser], inc=t_wq)
        p.step("vector", lambda e, mat=mat: e.tensor_copy(
            out=thrcol[:, mat:mat + 1], in_=pw1[:, 3:4]),
            waits=[(t_wq, t_wq.n)])
        p.step("vector", lambda e, mat=mat: e.tensor_single_scalar(
            out=nthrcol[:, mat:mat + 1], in_=thrcol[:, mat:mat + 1],
            scalar=-1.0, op=ALU.mult))

    def wq_p2_tile(mat, t, eng="gpsimd"):
        p.dma("gpsimd", lambda e, mat=mat, t=t: e.dma_start(
            out=(wstg[:].rearrange("p (a f) -> p a f", a=4) if mat == 2
                 else wstg[:]),
            in_=wtile_aps(mat, t)[0]),
            waits=[pser, vser, aser], inc=dwl)
        p.step(eng, lambda e, mat=mat: e.tensor_scalar(
            out=abuf[:], in0=wstg[:], scalar1=nthrcol[:, mat:mat + 1],
            scalar2=None, op0=ALU.is_lt), waits=[dwl, dqw])
        p.step(eng, lambda e, mat=mat: e.scalar_tensor_tensor(
            out=abuf[:], in0=wstg[:], scalar=thrcol[:, mat:mat + 1],
            in1=abuf[:], op0=ALU.is_gt, op1=ALU.subtract))
        sem = pser if eng == "gpsimd" else vser
        vv = sem.n
        p.dma("gpsimd", lambda e, mat=mat, t=t, sem=sem, vv=vv: e.dma_start(
            out=wtile_aps(mat, t)[1],
            in_=(abuf[:].rearrange("p (a f) -> p a f", a=4) if mat == 2
                 else abuf[:])), waits=[(sem, vv)], inc=dqw)

    WQE = _os.environ.get("WQ_ENG", "vector")
    wq_sched = {}
    for t in range(4):
        wq_sched.setdefault(t, []).extend(
            [lambda m=0, k=2 * t: wq_p1_tile(m, k),
             lambda m=0, k=2 * t + 1: wq_p1_tile(m, k)])
    wq_sched.setdefault(4, []).append(lambda: wq_thr(0))
    for t in range(4):
        wq_sched.setdefault(4 + t, []).extend(
            [lambda m=1, k=2 * t: wq_p1_tile(m, k),
             lambda m=1, k=2 * t + 1: wq_p1_tile(m, k)])
    wq_sched.setdefault(8, []).append(lambda: wq_thr(1))
    for t in range(5, 13):
        wq_sched.setdefault(t, []).extend(
            [lambda m=0, k=2 * (t - 5): wq_p2_tile(m, k, WQE),
             lambda m=0, k=2 * (t - 5) + 1: wq_p2_tile(m, k, WQE)]
            if t < 9 else
            [lambda m=1, k=2 * (t - 9): wq_p2_tile(m, k, WQE),
             lambda m=1, k=2 * (t - 9) + 1: wq_p2_tile(m, k, WQE)])
    for t in range(4):
        wq_sched.setdefault(12 + t, []).extend(
            [lambda m=2, k=2 * t: wq_p1_tile(m, k, "vector"),
             lambda m=2, k=2 * t + 1: wq_p1_tile(m, k, "vector")])

    rank1_done = [0] * NCH
    sgcv_issued = [None] * NCH

    def issue_flat_dma(c, gate_batch):
        w = [(vser, max(v_sig, v_cvx)), (p2rdy, 1)]
        if c >= 2:
            w.append((t_att, rank1_done[c - 2]))
        bufc = c % 2
        p.dma("gpsimd", lambda e, c=c, bufc=bufc: e.dma_start(
            out=sgfl[0:1, bufc * 1024:(bufc + 1) * 1024].rearrange(
                "o (h x) -> o h x", h=16),
            in_=sgm[:, c * CS:(c + 1) * CS]), waits=w, inc=dsg)
        p.dma("gpsimd", lambda e, c=c, bufc=bufc: e.dma_start(
            out=cvfl[0:1, bufc * 1024:(bufc + 1) * 1024],
            in_=CVXsb[c:c + 1, :]), waits=list(w), inc=dcv)
        sgcv_issued[c] = dsg.n

    for c in (0, 1):
        issue_flat_dma(c, None)

    _ps2 = _os.environ.get("S_SINGLE", "0") != "1"
    psS = ([ps("psSa", [128, 1024]), ps("psSb", [128, 1024])] if _ps2
           else [ps("psSa", [128, 1024])] * 2)
    psPT = ps("psPT", [128, 1024], BF16)
    psA = [ps("psA0", [128, 512]), ps("psA1", [128, 512])]
    s_free_at = [vser.n, vser.n]
    a_chunk0 = aser.n
    pt_free_at = 0
    a_free_at = [0, 0]

    for t in range(NCH // 2):
        c0 = 2 * t
        sp = (t % 2) if _ps2 else 0
        nS = 0
        for c2 in (0, 1):
            c = c0 + c2
            for j in range(NB):
                for hp in (0, 1):
                    nS += 1
                    p.step("tensor", lambda e, c=c, c2=c2, j=j, hp=hp, sp=sp:
                           e.matmul(
                        out=psS[sp][hp * 64:(hp + 1) * 64,
                                    c2 * 512 + j * 64:c2 * 512 + (j + 1) * 64],
                        lhsT=Q[hp * 64:(hp + 1) * 64,
                               j * TOK + c * CS:j * TOK + (c + 1) * CS],
                        rhs=K[hp * 64:(hp + 1) * 64,
                              j * TOK + c * CS:j * TOK + (c + 1) * CS],
                        start=True, stop=True,
                        tile_position=(hp * 64, hp * 64)),
                        waits=(([(vser, s_free_at[sp])]
                                + ([(aser, a_chunk0)] if t == 0 else []))
                               if nS == 1 else []),
                        inc=(t_att if nS == 32 else None))
        tS = t_att.n
        p.step("vector", lambda e, sp=sp: e.tensor_tensor(
            out=Sm[:], in0=psS[sp][:], in1=cmask[:], op=ALU.add),
            waits=[(t_att, tS), dcst])
        s_free_at[sp] = vser.n
        p.step("vector", lambda e: e.reduce_max(
            out=mx[:], in_=Sm[:].rearrange("p (g x) -> p g x", g=16),
            axis=AX.X))
        p.step("vector", lambda e: e.tensor_tensor(
            out=Sm[:].rearrange("p (g x) -> p g x", g=16),
            in0=Sm[:].rearrange("p (g x) -> p g x", g=16),
            in1=mx[:].unsqueeze(2).broadcast_to([128, 16, CS]),
            op=ALU.subtract))
        v_sub = vser.n
        p.step("scalar", lambda e: e.activation(
            out=Sm[:], in_=Sm[:], func=AF.Exp, scale=SCALE),
            waits=[(vser, v_sub)])
        p.step("vector", lambda e: e.reduce_sum(
            out=rs[:], in_=Sm[:].rearrange("p (g x) -> p g x", g=16),
            axis=AX.X), waits=[aser])
        p.step("vector", lambda e: e.reciprocal(out=rsi[:], in_=rs[:]))
        p.step("vector", lambda e: e.tensor_tensor(
            out=Pn[:].rearrange("p (g x) -> p g x", g=16),
            in0=Sm[:].rearrange("p (g x) -> p g x", g=16),
            in1=rsi[:].unsqueeze(2).broadcast_to([128, 16, CS]),
            op=ALU.mult))
        v_pn = vser.n
        nT = 0
        for c2 in (0, 1):
            for j in range(NB):
                nT += 1
                p.step("tensor", lambda e, c2=c2, j=j: e.transpose(
                    out=psPT[c2 * 64:(c2 + 1) * 64, j * 128:(j + 1) * 128],
                    in_=Pn[:, c2 * 512 + j * 64:c2 * 512 + (j + 1) * 64],
                    identity=cbf[:, CB_IDN:CB_IDN + 128],
                    tile_position=(0, c2 * 64)),
                    waits=([(vser, max(v_pn, pt_free_at))] if nT == 1 else []),
                    inc=(t_att if nT == 16 else None))
        tT = t_att.n
        p.step("vector", lambda e: e.tensor_copy(out=PTsb[:], in_=psPT[:]),
               waits=[(t_att, tT)])
        pt_free_at = vser.n
        for c2 in (0, 1):
            c = c0 + c2
            first = True
            for j in range(NB):
                for he in (0, 1):
                    p.step("tensor", lambda e, c=c, c2=c2, j=j, he=he: e.matmul(
                        out=psA[c2][he * 64:(he + 1) * 64,
                                    j * 64:(j + 1) * 64],
                        lhsT=V[c2 * 64:(c2 + 1) * 64,
                               (c // 2) * D + j * 128 + he * 64:
                               (c // 2) * D + j * 128 + (he + 1) * 64],
                        rhs=PTsb[c2 * 64:(c2 + 1) * 64,
                                 j * 128 + he * 64:j * 128 + (he + 1) * 64],
                        start=True, stop=False,
                        tile_position=(c2 * 64, he * 64)),
                        waits=([(vser, pt_free_at), (aser, a_free_at[c2]),
                                dsg, dcv] if first else []))
                    first = False
                    p.step("tensor", lambda e, c=c, c2=c2, j=j, he=he: e.matmul(
                        out=psA[c2][he * 64:(he + 1) * 64,
                                    j * 64:(j + 1) * 64],
                        lhsT=cvfl[0:1, (c % 2) * 1024 + j * 128 + he * 64:
                                  (c % 2) * 1024 + j * 128 + (he + 1) * 64],
                        rhs=sgfl[0:1, (c % 2) * 1024 + (2 * j + he) * 64:
                                 (c % 2) * 1024 + (2 * j + he + 1) * 64],
                        start=False, stop=True,
                        tile_position=(0, he * 64)),
                        inc=(t_att if (j == NB - 1 and he == 1) else None))
            rank1_done[c] = t_att.n
            p.step("scalar", lambda e, c=c, c2=c2: e.activation(
                out=ast[:, c2 * 512:(c2 + 1) * 512],
                in_=psA[c2][:], func=AF.Identity),
                waits=[(t_att, rank1_done[c])] + ([da] if t >= 1 else []))
            a_free_at[c2] = aser.n
            p.dma("gpsimd", lambda e, c=c, c2=c2: e.dma_start(
                out=a_d[:, c * CS:(c + 1) * CS].rearrange(
                    "(j p) q -> p j q", p=128),
                in_=ast[:, c2 * 512:(c2 + 1) * 512].rearrange(
                    "p (j q) -> p j q", j=NB)),
                waits=[aser], inc=da)
        for c in (c0 + 2, c0 + 3):
            if c < NCH:
                issue_flat_dma(c, t)
        if _os.environ.get("WQ_OFF", "0") != "1":
            for fn in wq_sched.get(t, []):
                fn()
    if _os.environ.get("WQ_OFF", "0") != "1":
        wq_thr(2)
    v_att_done = vser.n
    a_att_done = aser.n
    att_bar = barrier("att_bar", [(vser, v_att_done), (aser, a_att_done),
                                  (da, da.n), (dwl, dwl.n), (dqw, dqw.n),
                                  dsg, dcv,
                                  (t_att, t_att.n), (t_wq, t_wq.n)])

    if "dbg_stats" in dbg_d:
        for idx, tns in enumerate([mcol, sumsq, rstd, sumx, lnt_o, bcol]):
            p.dma("gpsimd", lambda e, idx=idx, tns=tns: e.dma_start(
                out=dbg_d["dbg_stats"][:, idx * 16:(idx + 1) * 16], in_=tns[:]),
                waits=[(aser, act_done_own[NTT - 1]),
                       (vser, dve_done_own[NTT - 1])], inc=dbg_done)
    if "dbg_xhd" in dbg_d:
        p.dma("gpsimd", lambda e: e.dma_start(
            out=dbg_d["dbg_xhd"], in_=xhat_d), waits=[(dxh, 16 * NTT)],
            inc=dbg_done)
    if "dbg_xhat" in dbg_d:
        p.dma("gpsimd", lambda e: e.dma_start(
            out=dbg_d["dbg_xhat"], in_=xhat[:]), waits=[(dtrx, 16 * NB)],
            inc=dbg_done)
    for nm, srct in [("dbg_q", Q), ("dbg_k", K), ("dbg_v", V)]:
        if nm in dbg_d:
            p.dma("gpsimd", lambda e, nm=nm, srct=srct: e.dma_start(
                out=dbg_d[nm], in_=srct[:]),
                waits=[(vser, v_p1_done), (aser, a_p1_done)], inc=dbg_done)
    if "dbg_a" in dbg_d:
        p.dma("gpsimd", lambda e: e.dma_start(
            out=dbg_d["dbg_a"].rearrange("p (j x) -> (j p) x", j=NB), in_=a_d),
            waits=[(att_bar, 1)], inc=dbg_done)
    if "dbg_sig" in dbg_d:
        p.dma("gpsimd", lambda e: e.dma_start(out=dbg_d["dbg_sig"], in_=sgm[:]),
              waits=[(vser, v_sig)], inc=dbg_done)
    if "dbg_cvx" in dbg_d:
        p.dma("gpsimd", lambda e: e.dma_start(out=dbg_d["dbg_cvx"],
                                              in_=CVXsb[:]),
              waits=[(vser, v_cvx)], inc=dbg_done)
    if "dbg_c0k" in dbg_d:
        p.dma("gpsimd", lambda e: e.dma_start(out=dbg_d["dbg_c0k"], in_=c0k[:]),
              waits=[(vser, v_carry_done)], inc=dbg_done)
    if "dbg_c0v" in dbg_d:
        p.dma("gpsimd", lambda e: e.dma_start(out=dbg_d["dbg_c0v"],
                                              in_=c0vrow[:]),
              waits=[(vser, v_carry_done)], inc=dbg_done)


    if _ps2:
        free("psA0", "psA1", "psPT", "psSb", "psSa")
    else:
        free("psA0", "psA1", "psPT", "psSa")
    free("pwq", "wabs", "abuf", "wstg")
    free("ast", "rsi", "rs", "mx", "PTsb", "Pn", "Sm", "cvfl", "sgfl",
         "sgm", "CVXsb", "CVsb", "CKcum", "CK", "cmask_sb")
    free("V", "K", "Q")
    dr2 = S("dr2")
    dxr = S("dxr")
    dal = S("dal")
    dwo = S("dwo")
    dwgl = S("dwgl")
    dwul = S("dwul")
    dx2q = S("dx2q")
    t_wo = S("t_wo")
    Wgq_sb = sb("Wgq_sb", [128, NB * INNER], BF16)
    X2qT = sb("X2qT", [128, 2 * NB * 512], BF16)
    A_sb = sb("A_sb", [128, NB * TOK], BF16)
    wout_sb = sb("wout_sb", [128, NB * D], BF16)
    xre = sb("xre", [128, 2 * D], F32)
    r2st = sb("r2st", [128, 2 * D], F32)
    xh2 = sb("xh2", [128, D], F32)
    xh2b = sb("xh2b", [128, D], F32)
    x2qst = sb("x2qst", [128, 2 * D], BF16)
    psW = [ps("psW0", [128, D]), ps("psW1", [128, D])]
    wstg2 = sb("wstg2", [128, INNER], F32)
    tq2 = sb("tq2", [128, INNER], BF16)
    for b in range(NB):
        p.dma("sync", lambda e, b=b: e.dma_start(
            out=A_sb[:, b * TOK:(b + 1) * TOK],
            in_=a_d[b * 128:(b + 1) * 128, :]),
            waits=[(att_bar, 1)], inc=dal)
        p.dma("gpsimd", lambda e, b=b: e.dma_start(
            out=wout_sb[:, b * D:(b + 1) * D],
            in_=wout[b * 128:(b + 1) * 128, :]),
            waits=[(att_bar, 1)], inc=dwo)

    def emit_wgq_loads():
        for b in range(NB):
            p.dma("sync", lambda e, b=b: e.dma_start(
                out=Wgq_sb[:, b * INNER:(b + 1) * INNER],
                in_=wgq_d[b * 128:(b + 1) * 128, :]),
                waits=[(att_bar, 1)], inc=dwgl)

    def wq_p2_tail_tile(k):
        p.dma("gpsimd", lambda e, k=k: e.dma_start(
            out=wstg2[:].rearrange("p (a f) -> p a f", a=4),
            in_=wdT[k * 512:(k + 1) * 512, :].rearrange(
                "(a p) f -> p a f", p=128)),
            waits=[(att_bar, 1), vser, dqw], inc=dwl)
        p.step("vector", lambda e: e.tensor_scalar(
            out=tq2[:], in0=wstg2[:], scalar1=nthrcol[:, 2:3],
            scalar2=None, op0=ALU.is_lt), waits=[dwl, dqw])
        p.step("vector", lambda e: e.scalar_tensor_tensor(
            out=tq2[:], in0=wstg2[:], scalar=thrcol[:, 2:3],
            in1=tq2[:], op0=ALU.is_gt, op1=ALU.subtract))
        vv = vser.n
        p.dma("gpsimd", lambda e, k=k, vv=vv: e.dma_start(
            out=wdq_d[k * 512:(k + 1) * 512, :].rearrange(
                "(a p) f -> p a f", p=128),
            in_=tq2[:].rearrange("p (a f) -> p a f", a=4)),
            waits=[(vser, vv)], inc=dqw)
    dtx2 = S("dtx2")
    t_gu = S("t_gu")
    dtx2_done = [0] * 4

    def emit_transposes(q):
        qp = q % 2
        for b in range(NB):
            p.dma("sync", lambda e, q=q, b=b, qp=qp: e.dma_start(
                out=X2qT[:, qp * NB * 512 + b * 512:
                         qp * NB * 512 + (b + 1) * 512],
                in_=x2q_d[q * 512:(q + 1) * 512, b * 128:(b + 1) * 128],
                transpose=True),
                waits=[(att_bar, 1), (dx2q, 16 * 4 * (q + 1))]
                      + ([(t_gu, 32 * (q - 1))] if q >= 2 else []),
                inc=dtx2)
        dtx2_done[q] = dtx2.n

    r2_evac_v = [0] * NTT
    for tt in range(NTT):
        pb = tt % 2
        p.dma("sync", lambda e, tt=tt, pb=pb: e.dma_start(
            out=xre[:, pb * D:(pb + 1) * D],
            in_=x_own[tt * 128:(tt + 1) * 128, :]),
            waits=([(att_bar, 1)] if tt < 1 else [(vser, r2_evac_v[tt - 1])]),
            inc=dxr)
        for b in range(NB):
            for hf in range(2):
                p.step("tensor", lambda e, tt=tt, b=b, hf=hf, pb=pb: e.matmul(
                    out=psW[pb][:, hf * 512:(hf + 1) * 512],
                    lhsT=A_sb[:, b * TOK + tt * 128:b * TOK + (tt + 1) * 128],
                    rhs=wout_sb[:, b * D + hf * 512:b * D + (hf + 1) * 512],
                    start=(b == 0), stop=(b == NB - 1)),
                    waits=(([(dal, 16 * NB), (dwo, 16 * NB)]
                            if tt == 0 else [])
                           + ([(vser, r2_evac_v[tt - 2])]
                              if (tt >= 2 and b == 0 and hf == 0) else [])),
                    inc=(t_wo if (b == NB - 1 and hf == 1) else None))
        p.step("vector", lambda e, tt=tt, pb=pb: e.scalar_tensor_tensor(
            out=r2st[:, pb * D:(pb + 1) * D], in0=psW[pb][:], scalar=1.0,
            in1=xre[:, pb * D:(pb + 1) * D], op0=ALU.mult, op1=ALU.add),
            waits=[(t_wo, tt + 1), (dxr, 16 * (tt + 1)), dr2])
        p.dma("gpsimd", lambda e, tt=tt, pb=pb: e.dma_start(
            out=r2_d[tt * 128:(tt + 1) * 128, :],
            in_=r2st[:, pb * D:(pb + 1) * D]),
            waits=[vser], inc=dr2)
        i = tt
        p.step("scalar", lambda e, i=i, pb=pb: e.activation(
            out=xh2[:], in_=r2st[:, pb * D:(pb + 1) * D], func=AF.Square,
            accum_out=sq2c[:, i:i + 1]), waits=[vser])
        p.step("vector", lambda e, i=i, pb=pb: e.reduce_sum(
            out=m2c[:, i:i + 1], in_=r2st[:, pb * D:(pb + 1) * D],
            axis=AX.X))
        p.step("vector", lambda e, i=i: e.tensor_single_scalar(
            out=m2c[:, i:i + 1], in_=m2c[:, i:i + 1],
            scalar=1.0 / 1024.0, op=ALU.mult))
        p.step("vector", lambda e, i=i: e.tensor_tensor(
            out=msq2c[:, i:i + 1], in0=m2c[:, i:i + 1],
            in1=m2c[:, i:i + 1], op=ALU.mult))
        p.step("vector", lambda e, i=i: e.scalar_tensor_tensor(
            out=sq2c[:, i:i + 1], in0=sq2c[:, i:i + 1],
            scalar=1.0 / 1024.0, in1=msq2c[:, i:i + 1],
            op0=ALU.mult, op1=ALU.subtract), waits=[aser])
        p.step("scalar", lambda e, i=i: e.activation(
            out=ln2c[:, i:i + 1], in_=sq2c[:, i:i + 1], func=AF.Ln,
            bias=1e-5), waits=[vser])
        p.step("scalar", lambda e, i=i: e.activation(
            out=rstd2c[:, i:i + 1], in_=ln2c[:, i:i + 1], func=AF.Exp,
            scale=-0.5))
        p.step("vector", lambda e, i=i: e.scalar_tensor_tensor(
            out=bc2c[:, i:i + 1], in0=m2c[:, i:i + 1], scalar=-1.0,
            in1=rstd2c[:, i:i + 1], op0=ALU.mult, op1=ALU.mult),
            waits=[aser])
        p.step("vector", lambda e, i=i, pb=pb: e.tensor_scalar(
            out=xh2[:], in0=r2st[:, pb * D:(pb + 1) * D],
            scalar1=rstd2c[:, i:i + 1], scalar2=bc2c[:, i:i + 1],
            op0=ALU.mult, op1=ALU.add), waits=[aser])
        p.step("vector", lambda e, i=i: e.reduce_max(
            out=amc[:, i:i + 1], in_=xh2[:], axis=AX.X,
            apply_absolute_value=True))
        p.step("vector", lambda e, i=i: e.tensor_single_scalar(
            out=amc[:, i:i + 1], in_=amc[:, i:i + 1], scalar=1e-5,
            op=ALU.max))
        p.step("vector", lambda e, i=i: e.reciprocal(
            out=s1c[:, i:i + 1], in_=amc[:, i:i + 1]))
        p.step("vector", lambda e, i=i: e.tensor_single_scalar(
            out=s1c[:, i:i + 1], in_=s1c[:, i:i + 1], scalar=127.0,
            op=ALU.mult))
        p.step("vector", lambda e, i=i: e.tensor_single_scalar(
            out=rs1c[:, i:i + 1], in_=amc[:, i:i + 1],
            scalar=1.0 / 127.0, op=ALU.mult))
        p.step("vector", lambda e, i=i: e.tensor_tensor(
            out=gscc[:, i:i + 1], in0=rs1c[:, i:i + 1],
            in1=wmcol[:, 0:1], op=ALU.mult))
        p.step("vector", lambda e, i=i: e.tensor_tensor(
            out=uscc[:, i:i + 1], in0=rs1c[:, i:i + 1],
            in1=wmcol[:, 1:2], op=ALU.mult))
        p.step("scalar", lambda e, i=i: e.activation(
            out=xh2b[:], in_=xh2[:], func=AF.Identity,
            scale=s1c[:, i:i + 1], bias=mgc[:]), waits=[vser])
        p.step("scalar", lambda e, i=i, pb=pb: e.activation(
            out=x2qst[:, pb * D:(pb + 1) * D], in_=xh2b[:],
            func=AF.Identity, bias=nmgc[:]), waits=[dx2q])
        r2_evac_v[tt] = vser.n
        p.dma("gpsimd", lambda e, i=i, pb=pb: e.dma_start(
            out=x2q_d[i * 128:(i + 1) * 128, :],
            in_=x2qst[:, pb * D:(pb + 1) * D]),
            waits=[aser], inc=dx2q)
        if tt < 8 and _os.environ.get("WQ_OFF", "0") != "1":
            wq_p2_tail_tile(tt)
        if tt == 4:
            emit_transposes(0)
        if tt == 8:
            emit_transposes(1)
            emit_wgq_loads()
    p3_bar = barrier("p3_bar", [(dr2, 16 * NTT), (dx2q, 16 * NTT),
                                (t_wo, NTT), (dqw, dqw.n), (dwl, dwl.n),
                                (vser, vser.n), (aser, aser.n)])
    free("tq2", "wstg2", "psW0", "psW1", "x2qst", "xh2b", "xh2", "r2st",
         "xre", "wout_sb", "A_sb")

    dhq = S("dhq")
    Wuq_sb = sb("Wuq_sb", [128, NB * INNER], BF16)
    for b in range(NB):
        p.dma("sync", lambda e, b=b: e.dma_start(
            out=Wuq_sb[:, b * INNER:(b + 1) * INNER],
            in_=wuq_d[b * 128:(b + 1) * 128, :]),
            waits=[(p3_bar, 1)], inc=dwul)
    Hh = sb("Hh", [128, 5 * INNER], BF16)
    gbuf = sb("gbuf", [128, 3 * 512], BF16)
    hqs = sb("hqs", [128, D], F32)
    psG = [ps("psG0", [128, 512]), ps("psG1", [128, 512]),
           ps("psG2", [128, 512])]
    psU = [ps("psU0", [128, 512]), ps("psU1", [128, 512]),
           ps("psU2", [128, 512])]
    n_gu = 0
    gu_evac_done = [0] * 128
    aq2_done_v = [0] * NTT

    for i in range(NTT):
        q, t4 = i // 4, i % 4
        qp = q % 2
        if t4 == 0 and 2 <= q + 1 < 4:
            emit_transposes(q + 1)
        for ic in range(8):
            pb = n_gu % (3 if _os.environ.get("GU2", "0") != "1" else 2)
            for b in range(NB):
                p.step("tensor", lambda e, qp=qp, t4=t4, ic=ic, b=b, pb=pb:
                       e.matmul(
                    out=psG[pb][:],
                    lhsT=X2qT[:, qp * NB * 512 + b * 512 + t4 * 128:
                              qp * NB * 512 + b * 512 + (t4 + 1) * 128],
                    rhs=Wgq_sb[:, b * INNER + ic * 512:
                               b * INNER + (ic + 1) * 512],
                    start=(b == 0), stop=(b == NB - 1)),
                    waits=(([(dtx2, dtx2_done[q]),
                             (dwgl, 16 * NB), (dwul, 16 * NB)]
                            if b == 0 and ic == 0 and t4 == 0 else [])
                           + ([(vser, gu_evac_done[n_gu - 3])]
                              if (n_gu >= 3 and b == 0) else [])))
                p.step("tensor", lambda e, qp=qp, t4=t4, ic=ic, b=b, pb=pb:
                       e.matmul(
                    out=psU[pb][:],
                    lhsT=X2qT[:, qp * NB * 512 + b * 512 + t4 * 128:
                              qp * NB * 512 + b * 512 + (t4 + 1) * 128],
                    rhs=Wuq_sb[:, b * INNER + ic * 512:
                               b * INNER + (ic + 1) * 512],
                    start=(b == 0), stop=(b == NB - 1)),
                    inc=(t_gu if b == NB - 1 else None))
            p.step("scalar", lambda e, i=i, ic=ic, pb=pb: e.activation(
                out=gbuf[:, pb * 512:(pb + 1) * 512], in_=psG[pb][:],
                func=AF.Silu, scale=gscc[:, i:i + 1]),
                waits=[(t_gu, t_gu.n)])
            p.step("vector", lambda e, i=i, ic=ic, pb=pb:
                   e.scalar_tensor_tensor(
                out=Hh[:, (i % 5) * INNER + ic * 512:
                       (i % 5) * INNER + (ic + 1) * 512],
                in0=psU[pb][:], scalar=uscc[:, i:i + 1],
                in1=gbuf[:, pb * 512:(pb + 1) * 512],
                op0=ALU.mult, op1=ALU.mult),
                waits=[aser] + ([dhq] if ic == 0 else []))
            gu_evac_done[n_gu] = vser.n
            n_gu += 1
        sl = (i % 5) * INNER
        p.step("vector", lambda e, sl=sl, i=i: e.reduce_max(
            out=amc[:, i:i + 1], in_=Hh[:, sl:sl + INNER], axis=AX.X,
            apply_absolute_value=True))
        p.step("vector", lambda e, i=i: e.tensor_single_scalar(
            out=amc[:, i:i + 1], in_=amc[:, i:i + 1], scalar=1e-5,
            op=ALU.max))
        p.step("vector", lambda e, i=i: e.reciprocal(
            out=s2c[:, i:i + 1], in_=amc[:, i:i + 1]))
        p.step("vector", lambda e, i=i: e.tensor_single_scalar(
            out=s2c[:, i:i + 1], in_=s2c[:, i:i + 1], scalar=127.0,
            op=ALU.mult))
        p.step("vector", lambda e, i=i: e.tensor_single_scalar(
            out=dscc[:, i:i + 1], in_=amc[:, i:i + 1],
            scalar=1.0 / 127.0, op=ALU.mult))
        p.step("vector", lambda e, i=i: e.tensor_tensor(
            out=dscc[:, i:i + 1], in0=dscc[:, i:i + 1],
            in1=wmcol[:, 2:3], op=ALU.mult))
        for hqc in range(4):
            p.step("scalar", lambda e, sl=sl, i=i, hqc=hqc: e.activation(
                out=hqs[:], in_=Hh[:, sl + hqc * D:sl + (hqc + 1) * D],
                func=AF.Identity, scale=s2c[:, i:i + 1], bias=mgc[:]),
                waits=([vser] if hqc == 0 else []))
            p.step("scalar", lambda e, sl=sl, i=i, hqc=hqc: e.activation(
                out=Hh[:, sl + hqc * D:sl + (hqc + 1) * D], in_=hqs[:],
                func=AF.Identity, bias=nmgc[:]))
        aq2_done_v[i] = vser.n
        p.dma("gpsimd", lambda e, sl=sl, i=i: e.dma_start(
            out=hq_d[i * 128:(i + 1) * 128, :],
            in_=Hh[:, sl:sl + INNER]),
            waits=[aser], inc=dhq)
    p5a_bar = barrier("p5a_bar", [(dhq, 16 * NTT), (vser, vser.n),
                                  (aser, aser.n), (t_gu, t_gu.n)])
    free("psU2", "psU1", "psU0", "psG2", "psG1", "psG0")
    free("hqs", "gbuf", "Hh", "X2qT", "Wuq_sb", "Wgq_sb")

    dwdl = S("dwdl")
    dhql = S("dhql")
    dr2b = S("dr2b")
    dyo = S("dyo")
    t_dn = S("t_dn")
    Wdq_sb = sb("Wdq_sb", [128, NIC * D], BF16)
    HqT = sb("HqT", [128, 2 * NIC * 512], BF16)
    r2b = sb("r2b", [128, 2 * D], F32)
    yst = sb("yst", [128, 2 * D], F32)
    psD = [ps("psD0", [128, D]), ps("psD1", [128, D])]
    for t in range(NB):
        p.dma("sync", lambda e, t=t: e.dma_start(
            out=Wdq_sb[:, t * 4 * D:(t + 1) * 4 * D].rearrange(
                "p (a f) -> p a f", a=4),
            in_=wdq_d[t * 512:(t + 1) * 512, :].rearrange(
                "(a p) f -> p a f", p=128)),
            waits=[(p5a_bar, 1)], inc=dwdl)
    y_evac_v = [0] * NTT
    dtx2_hqt = {}

    def emit_hqt(q):
        qs = (q % 2) * NIC * 512
        for b32 in range(NIC):
            p.dma("sync", lambda e, q=q, b32=b32, qs=qs: e.dma_start(
                out=HqT[:, qs + b32 * 512:qs + (b32 + 1) * 512],
                in_=hq_d[q * 512:(q + 1) * 512, b32 * 128:(b32 + 1) * 128],
                transpose=True),
                waits=[(p5a_bar, 1)] + ([(t_dn, 4 * (q - 1))] if q >= 2
                                        else []),
                inc=dtx2)
        dtx2_hqt[q] = dtx2.n

    emit_hqt(0)
    for q in range(4):
        qs = (q % 2) * NIC * 512
        if q + 1 < 4:
            emit_hqt(q + 1)
        for t4 in range(4):
            i = q * 4 + t4
            pb = i % 2
            p.dma("gpsimd", lambda e, i=i, pb=pb: e.dma_start(
                out=r2b[:, pb * D:(pb + 1) * D],
                in_=r2_d[i * 128:(i + 1) * 128, :]),
                waits=[(p5a_bar, 1)] + ([(vser, y_evac_v[i - 1])]
                                        if i >= 1 else []), inc=dr2b)
            for b32 in range(NIC):
                for hf in range(2):
                    p.step("tensor", lambda e, q=q, t4=t4, b32=b32, hf=hf,
                           pb=pb, qs=qs: e.matmul(
                        out=psD[pb][:, hf * 512:(hf + 1) * 512],
                        lhsT=HqT[:, qs + b32 * 512 + t4 * 128:
                                 qs + b32 * 512 + (t4 + 1) * 128],
                        rhs=Wdq_sb[:, b32 * D + hf * 512:
                                   b32 * D + (hf + 1) * 512],
                        start=(b32 == 0), stop=(b32 == NIC - 1)),
                        waits=(([(dtx2, dtx2_hqt[q]), (dwdl, 16 * NB)]
                                if t4 == 0 and b32 == 0 and hf == 0 else [])
                               + ([(vser, y_evac_v[i - 2])]
                                  if (i >= 2 and b32 == 0 and hf == 0)
                                  else [])),
                        inc=(t_dn if (b32 == NIC - 1 and hf == 1) else None))
            p.step("vector", lambda e, i=i, pb=pb: e.scalar_tensor_tensor(
                out=yst[:, pb * D:(pb + 1) * D], in0=psD[pb][:],
                scalar=dscc[:, i:i + 1], in1=r2b[:, pb * D:(pb + 1) * D],
                op0=ALU.mult, op1=ALU.add),
                waits=[(t_dn, i + 1), (dr2b, 16 * (i + 1)), dyo])
            y_evac_v[i] = vser.n
            p.dma("gpsimd", lambda e, i=i, pb=pb: e.dma_start(
                out=y[i * 128:(i + 1) * 128, :],
                in_=yst[:, pb * D:(pb + 1) * D]),
                waits=[vser], inc=dyo)

    p.step("sync", None, waits=[dbg_done, dxh, dwq, dtrx, dxo, dxp,
                                dsg, dcv, (dyo, 16 * NTT), dwl, dqw, dal,
                                dwo, dxr, dr2, dwgl, dwul, dx2q,
                                dtx2, dhq, dwdl, dr2b])

    block = es.enter_context(nc.Block())
    p.emit(block)
    es.close()
    free_all()
    return nc


def make_inputs(inputs):
    x = np.ascontiguousarray(np.asarray(inputs["L0_attn_x"], dtype=np.float32)
                             .reshape(4 * 4096, D))
    wqkv_a = np.ascontiguousarray(np.asarray(inputs["L0_attn_W_qkv"], np.float32))
    wout_a = np.ascontiguousarray(np.asarray(inputs["L0_attn_W_out"], np.float32))
    wgT = np.ascontiguousarray(np.asarray(inputs["L0_ffn_w_gate"], np.float32).T)
    wuT = np.ascontiguousarray(np.asarray(inputs["L0_ffn_w_up"], np.float32).T)
    wdT = np.ascontiguousarray(np.asarray(inputs["L0_ffn_w_down"], np.float32).T)
    cst, cbf, cmask = host_consts()
    crow = np.ones((1, 128), dtype=np.float32)
    zeros = np.zeros((TOK, D), dtype=np.float32)
    in_maps = []
    for c in range(8):
        xo = x[c * TOK:(c + 1) * TOK]
        xp = zeros if c % 2 == 0 else x[(c - 1) * TOK:c * TOK]
        in_maps.append(dict(x_own=np.ascontiguousarray(xo),
                            x_prev=np.ascontiguousarray(xp),
                            wqkv=wqkv_a, wout=wout_a, wgT=wgT, wuT=wuT,
                            wdT=wdT, cst=cst, cbf=cbf, cmask=cmask,
                            crow=crow))
    return in_maps


def kernel(**inputs):
    in_maps = make_inputs(inputs)
    nc = build_program()
    res = run_bass_kernel_spmd(nc, in_maps, core_ids=list(range(8)))
    out = np.concatenate([r["y"] for r in res.results], axis=0)
    return out.reshape(4, 4096, D)
